# version 04
# baseline (speedup 1.0000x reference)
import math
from dataclasses import dataclass

import numpy as np

import concourse.bass as bass
import concourse.mybir as mybir
from concourse.bass_utils import run_bass_kernel_spmd

F32 = mybir.dt.float32
U32 = mybir.dt.uint32
ALU = mybir.AluOpType
ACT = mybir.ActivationFunctionType
AX = mybir.AxisListType

P = 128
DEBUG_FORCE_SEL0 = False
DEBUG_RACES = False
DEBUG_DRAIN_ALL = False


class _DrainEng:

    def __init__(self, e):
        self._e = e

    def __getattr__(self, n):
        a = getattr(self._e, n)
        if not callable(a) or n in ("drain", "wait_ge"):
            return a

        def w(*args, **kw):
            r = a(*args, **kw)
            self._e.drain()
            return r
        return w


@dataclass(frozen=True)
class Cfg:
    D: int = 1024
    TOK: int = 2048
    TT: int = 512
    H: int = 16
    DK: int = 64
    E: int = 8
    NAV: int = 128
    QBLK: int = 32

    @property
    def DC(self):
        return self.D // P

    @property
    def T(self):
        return 2 * self.TOK

    @property
    def FI(self):
        return 3 * self.D

    @property
    def FIC(self):
        return self.FI // P

    @property
    def SW(self):
        return 2 * self.D

    @property
    def SWC(self):
        return self.SW // P

    @property
    def F(self):
        return 2 * self.D

    @property
    def FC(self):
        return self.F // P

    @property
    def NTT(self):
        return self.TOK // self.TT

    @property
    def HPC(self):
        return self.H // 2

    @property
    def U(self):
        return min(self.T, max(1, int(math.log(self.T))))

    @property
    def KN(self):
        return min(512, self.T)

    @property
    def NKT(self):
        return self.T // self.KN

    @property
    def LC(self):
        return self.T // P


CFG = Cfg()


class Sem:
    def __init__(self, h, name=""):
        self.h = h
        self.name = name
        self.n = 0
        self.is_dma = False
        self.guarded = {}

    def bump(self, k):
        if k == 16:
            self.is_dma = True
        self.n += k
        return self.n


class Prog:
    ENGINES = ("sync", "tensor", "vector", "scalar", "gpsimd")

    def __init__(self, nc, waited=None):
        self.nc = nc
        self.ops = {e: [] for e in self.ENGINES}
        self._sem_cms = []
        self.sems = {}
        self.rec_waited = {}
        self.waited = waited or {}

    def sem(self, name) -> Sem:
        if name not in self.sems:
            cm = self.nc.semaphore(name)
            h = cm.__enter__()
            self._sem_cms.append(cm)
            self.sems[name] = Sem(h, name)
        return self.sems[name]

    def on(self, engine, fn):
        self.ops[engine].append(fn)

    def wait(self, engine, sem: Sem, val=None):
        v = sem.n if val is None else val
        if v > 0:
            if sem.is_dma:
                self.rec_waited.setdefault(sem.name, set()).add(v)
            self.on(engine, lambda e: e.wait_ge(sem.h, v))

    def _dma_guard(self, engine, sem: Sem):
        w = sem.n
        if (w in self.waited.get(sem.name, ()) and
                w > sem.guarded.get(engine, 0)):
            self.on(engine, lambda e: e.wait_ge(sem.h, w))
            sem.guarded[engine] = w

    def dma(self, engine, out, in_, sem: Sem, slow=False):
        self._dma_guard(engine, sem)
        v = sem.bump(16)
        if slow:
            self.on(engine, lambda e: e.dma_start(
                out=out, in_=in_, allow_slow_non_contiguous=True).then_inc(sem.h, 16))
        else:
            self.on(engine, lambda e: e.dma_start(out=out, in_=in_).then_inc(sem.h, 16))
        return v

    def dma_lazy(self, engine, out, in_fn, sem: Sem):
        self._dma_guard(engine, sem)
        v = sem.bump(16)
        self.on(engine, lambda e: e.dma_start(out=out, in_=in_fn()).then_inc(
            sem.h, 16))
        return v

    def run(self, engine, fn, sem: Sem = None, k=1):
        if DEBUG_DRAIN_ALL and engine in ("vector", "scalar"):
            inner = fn
            fn = lambda e, _f=inner: _f(_DrainEng(e))
        if sem is None:
            self.on(engine, fn)
            return None
        if k == 16:
            self._dma_guard(engine, sem)
        v = sem.bump(k)
        self.on(engine, lambda e: fn(e).then_inc(sem.h, k))
        return v

    def emit(self):
        with self.nc.Block() as blk:
            for name in self.ENGINES:
                ops = self.ops[name]

                def make(ops):
                    def body(eng):
                        for f in ops:
                            f(eng)
                    return body

                getattr(blk, name)(make(ops))


class MK:
    def __init__(self, cfg: Cfg, debug_stop=None, waited=None):
        self.cfg = cfg
        self.debug_stop = debug_stop
        self.nc = bass.Bass("TRN2", target_bir_lowering=False,
                            detect_race_conditions=DEBUG_RACES)
        self.p = Prog(self.nc, waited=waited)
        self.dram = {}
        self.sb = {}
        self.ps = {}
        self._declare_io()
        self._alloc_onchip()

    def dr(self, name, shape, dtype=F32, kind="Internal"):
        t = self.nc.dram_tensor(name, list(shape), dtype, kind=kind)
        self.dram[name] = t
        return t

    def da(self, name):
        return self.dram[name].ap()

    def sbuf(self, name, shape, dtype=F32):
        t = self.nc.alloc_sbuf_tensor(name, list(shape), dtype)
        self.sb[name] = t
        return t

    def _declare_io(self):
        c = self.cfg
        D, TOK, FI = c.D, c.TOK, c.FI
        inp = lambda n, s, dt=F32: self.dr(n, s, dt, kind="ExternalInput")
        inp("x_fm", [D, TOK])
        self.dr("out_fm", [D, TOK], kind="ExternalOutput")
        inp("wqkvT", [D, 3 * D])
        inp("wswT", [D, c.SW])
        inp("woutT", [D, D])
        inp("wupT", [D, FI])
        inp("wgateT", [D, FI])
        inp("wdownT", [FI, D])
        inp("wup_n", [FI, D])
        inp("wgate_n", [FI, D])
        inp("wdown_n", [D, FI])
        inp("woT", [D, D])
        inp("bo_t", [P, c.DC])
        MH = c.HPC * c.DK
        inp("wqT_my", [D, MH])
        inp("wkT_my", [D, MH])
        inp("wvT_my", [D, MH])
        inp("bq_my_t", [P, MH // P])
        inp("bk_my_t", [P, MH // P])
        inp("bv_my_row", [1, MH])
        inp("bq_my_row", [1, MH])
        inp("pick_even", [8, 1])
        inp("pick_odd", [8, 1])
        inp("Wp1", [D, c.NAV])
        inp("bp1", [c.NAV, 1])
        inp("Wp2", [c.NAV, c.E])
        inp("bp2", [c.E, 1])
        inp("W1", [c.E, D, c.F])
        inp("b1_t", [c.E, P, c.FC])
        inp("W2", [c.E, c.F, D])
        inp("b2_t", [c.E, P, c.DC])
        inp("Wo1", [D, D])
        inp("bo1_t", [P, c.DC])
        inp("ones", [P, 1])
        inp("onesrow", [1, P])
        inp("eye", [P, P])
        inp("pick_carry", [8, 1])
        inp("pick_sel", [8, 1])
        inp("role", [1, 8])
        inp("iotaT", [P, c.LC])
        inp("iotarow", [1, TOK])

        self.dr("xn_sp", [D, TOK])
        self.dr("xh_sp", [D, TOK])
        self.dr("xq_sp", [D, TOK])
        self.dr("wupdq_sp", [D, FI])
        self.dr("wgatedq_sp", [D, FI])
        self.dr("v_sp", [2 * TOK, c.HPC * c.DK])
        self.dr("w1_sp", [D, c.F])
        self.dr("w2_sp", [c.F, D])
        self.dr("pp0_sp", [1, self._ctx_pack_len()])
        self.dr("pp1_sp", [1, self._ctx_pack_len()])
        self.dr("qkv_sp", [3 * D, TOK])
        self.dr("gate_sp", [D, TOK])
        self.dr("w_sp", [D, TOK])
        self.dr("kv_sp", [D, TOK])
        self.dr("upraw_sp", [FI, TOK])
        self.dr("gateraw_sp", [FI, TOK])
        self.dr("oraw_sp", [FI, TOK])
        self.dr("wdowndq_sp", [FI, D])
        self.dr("xn1_sp", [D, TOK])
        self.dr("qtm_sp", [(c.HPC // min(2, c.HPC)) * 2 * TOK, min(2, c.HPC) * c.DK])
        self.dr("dbg_y", [D, TOK])
        self.dr("dbg_z", [c.SW, TOK])
        self.dr("dbg_u", [D, TOK])
        self.dr("dbg_s", [D, TOK])
        for nm in ("max1", "max23", "max4", "max5"):
            self.dr(f"cc_{nm}_in", [1, 8])
            self.dr(f"cc_{nm}_out", [1, 8])
        self.dr("cc_carry_in", [1, 2 * D])
        self.dr("cc_carry_out", [8, 2 * D])
        self.dr("cc_xn1_in", [D, TOK])
        self.dr("cc_xn1_out", [2 * D, TOK])
        nctx = self._ctx_pack_len()
        self.dr("cc_ctx_in", [1, nctx])
        self.dr("cc_ctx_out", [8, nctx])

    def _ctx_pack_len(self):
        c = self.cfg
        n = c.HPC * 8 * c.DK + c.HPC * 8 + 8
        return ((n + 511) // 512) * 512

    def _alloc_onchip(self):
        c = self.cfg
        for i in range(c.DC):
            self.sbuf(f"acc{i}", [P, c.TOK])
        for i in range(2):
            self.sbuf(f"bigA{i}", [P, 2 * c.TOK])
        for i in range(2):
            self.sbuf(f"C{i}", [P, 2 * c.TOK])
        for i in range(2):
            self.sbuf(f"WB{i}", [P, min(2048, c.FI)])
        for i in range(10):
            self.sbuf(f"T{i}", [P, max(512, c.TT)])
        RW = max(6656, 2 * c.TOK + 512, self._ctx_pack_len() + 2048)
        self.sbuf("rows", [P, RW])
        self.RW = RW
        self.sbuf("sm", [P, 64])
        self.sbuf("selv", [c.HPC, 8])
        self.sbuf("seli", [c.HPC, 8], U32)
        self.sbuf("consts", [P, P])
        self.sbuf("cones", [P, 1])
        self.sbuf("conesrow", [1, P])
        self.sbuf("cpick", [8, 4])
        self.sbuf("crole", [1, 8])
        self.sbuf("ciotaT", [P, c.LC])
        self.sbuf("qg", [P, 512])
        self.sbuf("gidxc", [P, 1], mybir.dt.int32)
        self.sbuf("qsel", [P, 8])
        self.sbuf("expb", [P, c.LC * 8])
        self.sbuf("ctxb", [c.DK, c.HPC * 8])
        self.sbuf("selrow", [1, 4 * c.HPC * 8], mybir.dt.int32)
        self.sbuf("ctxs", [c.DK, 2 * c.HPC * 8])
        self.sbuf("bffn", [P, c.FC])
        self.sbuf("bffn2", [P, c.DC])
        self.sbuf("qint", [P, 512], mybir.dt.int32)
        self.sbuf("chalf", [P, 1])
        self.sbuf("idxr0", [1, 64])
        self.sbuf("idxr1", [1, 64])
        self.sbuf("selr1", [1, 4])
        self.sbuf("s32g0", [c.HPC, 8])
        self.sbuf("csum", [P, 2 * c.DC])
        self.sbuf("csum2", [P, 4 * c.DC])
        self.ps["A"] = self.nc.alloc_psum_tensor("psA", [P, 2048], F32)
        self.ps["B"] = self.nc.alloc_psum_tensor("psB", [P, 2048], F32)

    def row(self, idx, lo, hi):
        return self.sb["rows"].ap()[idx:idx + 1, lo:hi]

    def rows2(self, r0, r1, lo, hi):
        return self.sb["rows"].ap()[r0:r1, lo:hi]

    def bigA(self, chunk, lo=0, hi=None):
        c = self.cfg
        t = self.sb[f"bigA{chunk // 2}"]
        off = (chunk % 2) * c.TOK
        hi = c.TOK if hi is None else hi
        return t[:, off + lo: off + hi]

    def tslot(self, j, w=None):
        w = (self.cfg.TT if w is None else w)
        return self.sb[f"T{2 + j}"].ap()[:, :w]

    def aslot(self, j, w=None):
        c = self.cfg
        w = c.TT if w is None else w
        spt = (2 * c.TOK) // w
        return self.sb[f"bigA{j // spt}"].ap()[:, (j % spt) * w:(j % spt + 1) * w]

    def cslot(self, which, j, w=None):
        c = self.cfg
        w = c.TT if w is None else w
        return self.sb[f"C{which}"].ap()[:, j * w:(j + 1) * w]

    def oslot(self, j, w=None):
        c = self.cfg
        w = c.TT if w is None else w
        spt = (2 * c.TOK) // w
        ar = [self.sb["C0"], self.sb["C1"], self.sb["bigA0"], self.sb["bigA1"]]
        return ar[j // spt].ap()[:, (j % spt) * w:(j % spt + 1) * w]

    def pslot(self, ch):
        c = self.cfg
        ar = [self.sb["C0"], self.sb["C1"], self.sb["bigA0"], self.sb["bigA1"]]
        return ar[ch // 2].ap()[:, (ch % 2) * c.TOK:(ch % 2 + 1) * c.TOK]

    def psA(self, lo, hi, rows=P):
        return self.ps["A"].ap()[:rows, lo:hi]

    def psB(self, lo, hi, rows=P):
        return self.ps["B"].ap()[:rows, lo:hi]

    def build(self):
        ds = self.debug_stop
        self._dump = None
        if ds and "+dump:" in ds:
            ds, self._dump = ds.split("+dump:")
            self.debug_stop = ds or None
        self.ph_consts()
        self.ph_p0()
        if self.debug_stop == "p0":
            return self.fin_or_dump()
        self.ph_l0_attn_A()
        self.ph_l0_attn_A2()
        if self.debug_stop == "a2":
            return self.fin_or_dump()
        self.ph_l0_ffn_F1()
        self.ph_l0_attn_B()
        if self.debug_stop == "b":
            return self.fin_or_dump()
        self.ph_l0_ffn_F23()
        self.ph_l0_ffn_F4()
        if self.debug_stop == "l0":
            return self.fin_or_dump()
        self.ph_l1_norm()
        self.ph_l1_attn()
        if self.debug_stop == "dumppack":
            return self.finish_dump("cc_ctx_out")
        if self.debug_stop is not None and self.debug_stop.startswith("l1a"):
            return self.fin_or_dump()
        self.ph_l1_ffn()
        return self.fin_or_dump()

    def ph_consts(self):
        p, d = self.p, self.dram
        ld = p.sem("ld")
        p.dma("sync", self.sb["consts"].ap()[:, :], self.da("eye")[:, :], ld)
        p.dma("sync", self.sb["cones"].ap()[:, :], self.da("ones")[:, :], ld)
        p.dma("sync", self.sb["conesrow"].ap()[:, :], self.da("onesrow")[:, :], ld)
        cp = self.sb["rows"].ap()[32:40, self.RW - 8:self.RW - 4]
        p.dma("sync", cp[:, 0:1], self.da("pick_carry")[:, :], ld)
        p.dma("sync", cp[:, 1:2], self.da("pick_sel")[:, :], ld)
        p.dma("sync", cp[:, 2:3], self.da("pick_even")[:, :], ld)
        p.dma("sync", cp[:, 3:4], self.da("pick_odd")[:, :], ld)
        p.dma("sync", self.sb["crole"].ap()[:, :], self.da("role")[:, :], ld)
        p.dma("sync", self.sb["ciotaT"].ap()[:, :], self.da("iotaT")[:, :], ld)
        ch = self.sb["chalf"].ap()
        qi1 = self.sb["qint"].ap()[:, 0:1]

        def mkch(e, ch=ch, qi1=qi1):
            e.memset(ch, 0.75)
            e.drain()
            e.tensor_copy(qi1, ch)
            e.drain()
            e.tensor_copy(ch, qi1)
            e.drain()
            return e.tensor_scalar(ch, ch, -0.5, 0.5, ALU.mult, ALU.add)
        p.run("vector", mkch, p.sem("ve"))

    def ph_p0(self):
        c, p = self.cfg, self.p
        ld, pe, ve, se, st, cc = (p.sem(n) for n in
                                  ("ld", "pe", "ve", "se", "st", "cc"))
        ones = self.sb["cones"].ap()
        orow = self.sb["conesrow"].ap()
        C0 = self.sb["C0"].ap()
        mx = self.sb["sm"].ap()[:, 0:1]
        p.wait("vector", ld)
        p.run("vector", lambda e: e.memset(mx, 0.0))
        for tt in range(c.NTT):
            lo, hi = tt * c.TT, (tt + 1) * c.TT
            pss0 = self.ps["A"].ap()[0:1, 0:c.TT]
            pss1 = self.ps["A"].ap()[0:1, 512:512 + c.TT]
            p.wait("sync", pe)
            p.wait("sync", se)
            p.wait("sync", ve)
            xall = []
            for i in range(c.DC):
                xall.append(p.dma("sync", self.tslot(i),
                                  self.da("x_fm")[i * P:(i + 1) * P, lo:hi], ld))
            p.wait("scalar", ld, xall[-1])
            p.wait("scalar", pe)
            sev = None
            for i in range(c.DC):
                sq = C0[:, i * c.TT:(i + 1) * c.TT]
                xv = self.tslot(i)
                sev = p.run("scalar", lambda e, sq=sq, xv=xv: e.activation(
                    sq, xv, ACT.Square), se)
            p.wait("tensor", ld, xall[-1])
            p.wait("tensor", ve)
            for i in range(c.DC):
                xv = self.tslot(i)
                p.run("tensor", lambda e, i=i, pss0=pss0, xv=xv: e.matmul(
                    pss0, ones, xv, start=(i == 0), stop=(i == c.DC - 1)))
            p.wait("tensor", se, sev)
            pev = None
            for i in range(c.DC):
                sq = C0[:, i * c.TT:(i + 1) * c.TT]
                pev = p.run("tensor", lambda e, i=i, sq=sq, pss1=pss1: e.matmul(
                    pss1, ones, sq, start=(i == 0), stop=(i == c.DC - 1)), pe)
            p.wait("vector", pe, pev)
            s_m = self.row(0, lo, hi)
            s_r = self.sb["rows"].ap()[0:1, 4096 + lo:4096 + hi]
            Df = float(c.D)

            def stat2(e, pss0=pss0, pss1=pss1, s_m=s_m, s_r=s_r, Df=Df):
                e.tensor_scalar(s_m, pss0, 1.0 / Df, None, ALU.mult)
                e.tensor_scalar(s_r, pss1, 1.0 / Df, None, ALU.mult)
                e.tensor_mul(pss0, s_m, s_m)
                e.tensor_sub(s_r, s_r, pss0)
                return e.tensor_scalar(s_r, s_r, 1e-5, None, ALU.add)
            vev = p.run("vector", stat2, ve)
            p.wait("scalar", ve, vev)
            sev = p.run("scalar", lambda e, s_r=s_r: e.activation(s_r, s_r, ACT.Sqrt), se)
            p.wait("vector", se, sev)
            vev = p.run("vector", lambda e, s_r=s_r: e.reciprocal(s_r, s_r), ve)
            bcm = self.psA(1024, 1024 + c.TT)
            bcr = self.psB(0, c.TT)
            p.wait("tensor", ve, vev)
            p.run("tensor", lambda e, s_m=s_m, bcm=bcm: e.matmul(
                bcm, orow, s_m, start=True, stop=True))
            pev = p.run("tensor", lambda e, s_r=s_r, bcr=bcr: e.matmul(
                bcr, orow, s_r, start=True, stop=True), pe)
            p.wait("vector", pe, pev)
            for i in range(c.DC):
                tn = self.sb[f"T{i % 2}"].ap()[:, :c.TT]
                xt = self.tslot(i)
                at = self.sb[f"acc{i}"].ap()[:, lo:hi]
                p.wait("vector", st)

                def xn_ops(e, xt=xt, at=at, tn=tn, bcm=bcm, bcr=bcr):
                    e.tensor_sub(tn, xt, bcm)
                    e.tensor_mul(tn, tn, bcr)
                    e.tensor_add(at, xt, tn)
                    e.tensor_reduce(self.sb["sm"].ap()[:, 1:2], tn, AX.X,
                                    ALU.max, apply_absolute_value=True)
                    e.drain()
                    return e.tensor_max(mx, mx, self.sb["sm"].ap()[:, 1:2])
                vev = p.run("vector", xn_ops, ve)
                p.wait("gpsimd", ve, vev)
                p.dma("gpsimd", self.da("xn_sp")[i * P:(i + 1) * P, lo:hi], tn, st)

        row = self.sb["rows"].ap()[0:1, 6144:][:, :P]
        p.wait("gpsimd", ve)
        p.dma("gpsimd", row, mx, st)
        p.wait("vector", st)
        g = self.sb["sm"].ap()[0:1, 2:3]
        p.run("vector", lambda e: e.tensor_reduce(g, row, AX.X, ALU.max))
        b8 = self.sb["rows"].ap()[0:1, 6144:][:, 128:136]

        def mk8(e, b8=b8, g=g):
            e.memset(b8, 0.0)
            e.drain()
            return e.tensor_copy(b8[:, 0:1], g)
        vev = p.run("vector", mk8, ve)
        p.wait("gpsimd", ve, vev)
        p.dma("gpsimd", self.da("cc_max1_in")[:, :], b8, st)
        p.wait("gpsimd", st)
        p.run("gpsimd", lambda e: e.collective_compute(
            "AllReduce", ALU.max, replica_groups=[list(range(8))],
            ins=[self.dram["cc_max1_in"].ap().opt()],
            outs=[self.dram["cc_max1_out"].ap().opt()]), cc)

    def ph_l0_attn_A(self):
        c, p = self.cfg, self.p
        ld, pe, ve, se, st = (p.sem(n) for n in ("ld", "pe", "ve", "se", "st"))
        ones = self.sb["cones"].ap()
        orow = self.sb["conesrow"].ap()
        C0 = self.sb["C0"].ap()
        NOC = 3 * c.DC
        for tt in range(c.NTT):
            lo, hi = tt * c.TT, (tt + 1) * c.TT
            pss1 = self.ps["A"].ap()[0:1, 512:512 + c.TT]
            p.wait("sync", p.sems["st"])
            p.wait("sync", pe)
            p.wait("sync", ve)
            p.wait("sync", se)
            xnv = []
            for i in range(c.DC):
                xnv.append(p.dma("sync", self.tslot(i),
                                 self.da("xn_sp")[i * P:(i + 1) * P, lo:hi], ld))
            p.wait("scalar", ld, xnv[-1])
            p.wait("scalar", pe)
            sev = None
            for i in range(c.DC):
                sq = C0[:, i * c.TT:(i + 1) * c.TT]
                xv = self.tslot(i)
                sev = p.run("scalar", lambda e, sq=sq, xv=xv: e.activation(
                    sq, xv, ACT.Square), se)
            p.wait("tensor", se, sev)
            p.wait("tensor", ve)
            pev = None
            for i in range(c.DC):
                sq = C0[:, i * c.TT:(i + 1) * c.TT]
                pev = p.run("tensor", lambda e, i=i, sq=sq, pss1=pss1: e.matmul(
                    pss1, ones, sq, start=(i == 0), stop=(i == c.DC - 1)), pe)
            p.wait("vector", pe, pev)
            s_r = self.sb["rows"].ap()[0:1, 4096 + lo:4096 + hi]
            Df = float(c.D)
            vev = p.run("vector", lambda e, pss1=pss1, s_r=s_r, Df=Df:
                        e.tensor_scalar(s_r, pss1, 1.0 / Df, 1.1920929e-07,
                                        ALU.mult, ALU.add), ve)
            p.wait("scalar", ve, vev)
            sev = p.run("scalar", lambda e, s_r=s_r: e.activation(s_r, s_r, ACT.Sqrt), se)
            p.wait("vector", se, sev)
            vev = p.run("vector", lambda e, s_r=s_r: e.reciprocal(s_r, s_r), ve)
            bcr = self.psB(0, c.TT)
            p.wait("tensor", ve, vev)
            pev = p.run("tensor", lambda e, s_r=s_r, bcr=bcr: e.matmul(
                bcr, orow, s_r, start=True, stop=True), pe)
            p.wait("vector", pe, pev)
            for i in range(c.DC):
                xv = self.tslot(i)
                vev = p.run("vector", lambda e, xv=xv, bcr=bcr: e.tensor_mul(
                    xv, xv, bcr), ve)
                p.wait("gpsimd", ve, vev)
                p.dma("gpsimd", self.da("xh_sp")[i * P:(i + 1) * P, lo:hi], xv, st)
        wq = self.da("wqkvT")
        wbuf = [self.sb["WB0"].ap(), self.sb["WB1"].ap()]
        for ts in range(c.NTT):
            lo, hi = ts * c.TT, (ts + 1) * c.TT
            p.wait("sync", pe)
            p.wait("sync", ve)
            p.wait("sync", st)
            xl = None
            for i in range(c.DC):
                xl = p.dma("sync", self.tslot(i),
                           self.da("xh_sp")[i * P:(i + 1) * P, lo:hi], ld)
            wld = [None, None]

            def wslice(b, oc):
                v = None
                for kc in range(c.DC):
                    v = p.dma("sync", wbuf[b][:, kc * P:(kc + 1) * P],
                              wq[kc * P:(kc + 1) * P, oc * P:(oc + 1) * P],
                              p.sems["ld"])
                return v
            wld[0] = wslice(0, 0)
            if NOC > 1:
                wld[1] = wslice(1, 1)
            for oc in range(NOC):
                b = oc % 2
                bank = oc % 2
                ps = self.psA(0, c.TT) if bank == 0 else self.psA(512, 512 + c.TT)
                p.wait("tensor", ld, wld[b])
                p.wait("tensor", ld, xl)
                p.wait("tensor", se)
                pev = None
                for kc in range(c.DC):
                    lhs = wbuf[b][:, kc * P:(kc + 1) * P]
                    rhs = self.tslot(kc)
                    pev = p.run("tensor", lambda e, lhs=lhs, rhs=rhs, ps=ps,
                                kc=kc: e.matmul(ps, lhs, rhs, start=(kc == 0),
                                                stop=(kc == c.DC - 1)), pe)
                if oc + 2 < NOC:
                    p.wait("sync", pe, pev)
                    wld[b] = wslice(b, oc + 2)
                sg = oc % 2
                stg = self.sb[f"T{sg}"].ap()[:, :c.TT]
                p.wait("scalar", pe, pev)
                p.wait("scalar", st)
                sev = p.run("scalar", lambda e, ps=ps, stg=stg: e.copy(stg, ps), se)
                p.wait("gpsimd", se, sev)
                p.dma("gpsimd", self.da("qkv_sp")[oc * P:(oc + 1) * P, lo:hi],
                      stg, st)

    def ph_l0_attn_A2(self):
        c, p = self.cfg, self.p
        ld, pe, ve, se, st, cc = (p.sem(n) for n in
                                  ("ld", "pe", "ve", "se", "st", "cc"))
        ones = self.sb["cones"].ap()
        orow = self.sb["conesrow"].ap()
        C0 = self.sb["C0"].ap()
        C1 = self.sb["C1"].ap()
        csum = self.sb["csum"].ap()
        p.wait("sync", p.sems["st"])
        p.wait("vector", p.sems["st"])
        vev = p.run("vector", lambda e: e.memset(csum, 0.0), ve)
        qkv = self.da("qkv_sp")
        for tt in range(c.NTT):
            lo, hi = tt * c.TT, (tt + 1) * c.TT
            p.wait("sync", ve)
            p.wait("sync", pe)
            qv = kv_ = vv = None
            for i in range(c.DC):
                qv = p.dma("sync", C0[:, i * c.TT:(i + 1) * c.TT],
                           qkv[i * P:(i + 1) * P, lo:hi], ld)
            for i in range(c.DC):
                kv_ = p.dma("sync", C1[:, i * c.TT:(i + 1) * c.TT],
                            qkv[(c.DC + i) * P:(c.DC + i + 1) * P, lo:hi], ld)
            for i in range(c.DC):
                vv = p.dma("sync", self.aslot(i),
                           qkv[(2 * c.DC + i) * P:(2 * c.DC + i + 1) * P, lo:hi], ld)
            pss0 = self.ps["A"].ap()[0:1, 0:c.TT]
            pss1 = self.ps["A"].ap()[0:1, 512:512 + c.TT]
            sq = self.sb["T3"].ap()[:, :c.TT]
            p.wait("scalar", ld, qv)
            p.wait("tensor", ld, kv_)
            pev = sev = None
            for which, (buf, pss) in enumerate(((C0, pss0), (C1, pss1))):
                for i in range(c.DC):
                    xv = buf[:, i * c.TT:(i + 1) * c.TT]
                    p.wait("scalar", pe, pev) if (i or which) else None
                    sev = p.run("scalar", lambda e, sq=sq, xv=xv: e.activation(
                        sq, xv, ACT.Square), se)
                    p.wait("tensor", se, sev)
                    pev = p.run("tensor", lambda e, sq=sq, pss=pss, i=i: e.matmul(
                        pss, ones, sq, start=(i == 0), stop=(i == c.DC - 1)), pe)
            s_q = self.sb["rows"].ap()[0:1, 4096 + lo:4096 + hi]
            s_k = self.row(0, lo, hi)
            Df = float(c.D)
            p.wait("vector", pe, pev)

            def rr(e, pss0=pss0, pss1=pss1, s_q=s_q, s_k=s_k, Df=Df):
                e.tensor_scalar(s_q, pss0, 1.0 / Df, 1.1920929e-07, ALU.mult, ALU.add)
                return e.tensor_scalar(s_k, pss1, 1.0 / Df, 1.1920929e-07,
                                       ALU.mult, ALU.add)
            vev = p.run("vector", rr, ve)
            p.wait("scalar", ve, vev)

            def sq2(e, s_q=s_q, s_k=s_k):
                e.activation(s_q, s_q, ACT.Sqrt)
                return e.activation(s_k, s_k, ACT.Sqrt)
            sev = p.run("scalar", sq2, se)
            p.wait("vector", se, sev)

            def rc2(e, s_q=s_q, s_k=s_k):
                e.reciprocal(s_q, s_q)
                return e.reciprocal(s_k, s_k)
            vev = p.run("vector", rc2, ve)
            bcq = self.psB(0, c.TT)
            bck = self.psB(512, 512 + c.TT)
            p.wait("tensor", ve, vev)
            p.run("tensor", lambda e, s_q=s_q, bcq=bcq: e.matmul(
                bcq, orow, s_q, start=True, stop=True))
            pev = p.run("tensor", lambda e, s_k=s_k, bck=bck: e.matmul(
                bck, orow, s_k, start=True, stop=True), pe)
            p.wait("vector", pe, pev)
            p.wait("vector", ld, vv)
            for i in range(c.DC):
                qv_ = C0[:, i * c.TT:(i + 1) * c.TT]
                kvv = C1[:, i * c.TT:(i + 1) * c.TT]
                vv_ = self.aslot(i)
                tg = self.sb["T4"].ap()[:, :c.TT]
                tw = self.sb["T5"].ap()[:, :c.TT]
                tkv = self.sb["T6"].ap()[:, :c.TT]
                p.wait("vector", st)
                vev = p.run("vector", lambda e, tg=tg, qv_=qv_, bcq=bcq:
                            e.tensor_mul(tg, qv_, bcq), ve)
                p.wait("scalar", ve, vev)
                sev = p.run("scalar", lambda e, tg=tg: e.activation(
                    tg, tg, ACT.Sigmoid), se)
                p.wait("vector", st)
                vev = p.run("vector", lambda e, tw=tw, kvv=kvv, bck=bck:
                            e.tensor_mul(tw, kvv, bck), ve)
                p.wait("scalar", ve, vev)
                sev = p.run("scalar", lambda e, tw=tw: e.activation(
                    tw, tw, ACT.Exp), se)
                p.wait("vector", se, sev)

                def kvops(e, tkv=tkv, tw=tw, vv_=vv_, csum=csum, i=i):
                    e.tensor_mul(tkv, tw, vv_)
                    e.tensor_reduce(self.sb["sm"].ap()[:, 3:4], tw, AX.X, ALU.add)
                    e.drain()
                    e.tensor_add(csum[:, i:i + 1], csum[:, i:i + 1],
                                 self.sb["sm"].ap()[:, 3:4])
                    e.tensor_reduce(self.sb["sm"].ap()[:, 3:4], tkv, AX.X, ALU.add)
                    e.drain()
                    return e.tensor_add(csum[:, c.DC + i:c.DC + i + 1],
                                        csum[:, c.DC + i:c.DC + i + 1],
                                        self.sb["sm"].ap()[:, 3:4])
                vev = p.run("vector", kvops, ve)
                p.wait("gpsimd", se, sev)
                p.wait("gpsimd", ve, vev)
                p.dma("gpsimd", self.da("gate_sp")[i * P:(i + 1) * P, lo:hi], tg, st)
                p.dma("gpsimd", self.da("w_sp")[i * P:(i + 1) * P, lo:hi], tw, st)
                p.dma("gpsimd", self.da("kv_sp")[i * P:(i + 1) * P, lo:hi], tkv, st)
        src = csum
        dst = self.da("cc_carry_in").rearrange("o (p c) -> (o p) c", c=2 * c.DC)
        p.wait("gpsimd", ve)
        v = p.dma("gpsimd", dst, src, st)
        p.wait("gpsimd", st, v)
        self._cc5 = p.run("gpsimd", lambda e: e.collective_compute(
            "AllGather", ALU.bypass, replica_groups=[list(range(8))],
            ins=[self.dram["cc_carry_in"].ap().opt()],
            outs=[self.dram["cc_carry_out"].ap().opt()]), cc)

    def colslice_load(self, wname, oc, buf, nchunk, ld):
        w = self.da(wname)
        v = None
        for kc in range(nchunk):
            v = self.p.dma("sync", buf[:, kc * P:(kc + 1) * P],
                           w[kc * P:(kc + 1) * P, oc * P:(oc + 1) * P],
                           self.p.sems["ld"])
        return v

    def bc_scalar(self, src11, dst_ps):
        orow = self.sb["conesrow"].ap()
        return lambda e: e.matmul(dst_ps, orow, src11, start=True, stop=True)

    def qdq_chain(self, e, x_ap, out_ap, rbc, sbc, tmp):
        w = tmp.shape[-1]
        qi = self.sb["qint"].ap()[:, :w]
        yi = out_ap.bitcast(U32)
        ti = tmp.bitcast(U32)
        e.tensor_scalar(out_ap, x_ap, rbc, None, ALU.mult)
        e.tensor_single_scalar(ti, yi, 0x7FFFFFFF, ALU.bitwise_and)
        e.tensor_scalar(tmp, tmp, self.sb["chalf"].ap(), None, ALU.add)
        e.tensor_copy(qi, tmp)
        e.tensor_copy(tmp, qi)
        e.tensor_single_scalar(yi, yi, 0x80000000, ALU.bitwise_and)
        e.tensor_tensor(yi, yi, ti, ALU.bitwise_or)
        return e.tensor_scalar(out_ap, out_ap, sbc, None, ALU.mult)

    def scale_from_cc(self, ccname, slot, colbase):
        c, p = self.cfg, self.p
        ld, pe, ve, se, cc = (p.sem(n) for n in ("ld", "pe", "ve", "se", "cc"))
        orow = self.sb["conesrow"].ap()
        row = self.sb["rows"].ap()[0:1, 140 + 16 * slot:148 + 16 * slot]
        p.wait("sync", cc)
        lv = p.dma("sync", row, self.da(ccname)[:, :], ld)
        g = row[:, slot:slot + 1] if False else row[:, 0:1]
        p.wait("vector", ld, lv)
        s11 = self.sb["rows"].ap()[0:1, 300 + 8 * slot:301 + 8 * slot]
        r11 = self.sb["rows"].ap()[0:1, 364 + 8 * slot:365 + 8 * slot]

        def mk(e, row=row, s11=s11, r11=r11, slot=slot):
            e.tensor_scalar(s11, row[:, slot:slot + 1], 1.0 / 32767.0, 1e-12,
                            ALU.mult, ALU.max)
            e.drain()
            return e.reciprocal(r11, s11)
        vev = p.run("vector", mk, ve)
        psr = self.psB(1536, 1537)
        pss = self.psB(1537, 1538)
        p.wait("tensor", ve, vev)
        p.run("tensor", lambda e, r11=r11, psr=psr: e.matmul(
            psr, orow, r11, start=True, stop=True))
        pev = p.run("tensor", lambda e, s11=s11, pss=pss: e.matmul(
            pss, orow, s11, start=True, stop=True), pe)
        sbc = self.sb["sm"].ap()[:, 8 + 2 * colbase:9 + 2 * colbase]
        rbc = self.sb["sm"].ap()[:, 9 + 2 * colbase:10 + 2 * colbase]
        p.wait("vector", pe, pev)

        def cp(e, sbc=sbc, rbc=rbc, psr=psr, pss=pss):
            e.tensor_copy(sbc, pss)
            return e.tensor_copy(rbc, psr)
        vev = p.run("vector", cp, ve)
        return sbc, rbc, vev

    def dequant_matrix(self, wT_name, wN_name, n_in_chunks, n_out,
                       natbuf_fn, rsT, nsT):
        c, p = self.cfg, self.p
        ld, pe, ve, se, st = (p.sem(n) for n in ("ld", "pe", "ve", "se", "st"))
        eye = self.sb["consts"].ap()
        n_row_tiles = n_out // P
        d_in = n_in_chunks * P
        nblk = d_in // c.QBLK
        wn = self.da(wN_name)
        for rt in range(n_row_tiles):
            bslot = natbuf_fn(rt)
            p.wait("sync", pe)
            p.wait("sync", ve)
            lv = p.dma("sync", bslot, wn[rt * P:(rt + 1) * P, :], ld)
            stile = self.sb["T9"].ap()[:, :nblk]
            p.wait("vector", ld, lv)
            bv = bslot.rearrange("p (nb b) -> p nb b", b=c.QBLK)
            vev = p.run("vector", lambda e, stile=stile, bv=bv: e.tensor_reduce(
                stile, bv, AX.X, ALU.max, apply_absolute_value=True), ve)
            pst = self.ps["B"].ap()[0:nblk, 1024:1024 + P]
            p.wait("tensor", ve, vev)
            pev = p.run("tensor", lambda e, pst=pst, stile=stile, eye=eye:
                        e.transpose(pst, stile, eye), pe)
            p.wait("vector", pe, pev)

            def ev(e, pst=pst, rt=rt, rsT=rsT, nblk=nblk):
                v = None
                for g in range(0, nblk, 32):
                    gg = min(32, nblk - g)
                    v = e.tensor_copy(rsT[g:g + gg, rt * P:(rt + 1) * P],
                                      pst[g:g + gg, :])
                return v
            vev = p.run("vector", ev, ve)

        def fin(e, rsT=rsT, nsT=nsT, nblk=nblk):
            v = None
            for g in range(0, nblk, 32):
                gg = min(32, nblk - g)
                rs = rsT[g:g + gg, :]
                ns = nsT[g:g + gg, :]
                e.tensor_scalar(rs, rs, 1.0 / 7.0, 1e-12, ALU.mult, ALU.max)
                e.tensor_scalar(ns, rs, -1.0, None, ALU.mult)
                v = e.reciprocal(rs, rs)
            return v
        vev = self.p.run("vector", fin, ve)
        return vev

    def dequant_group(self, wT_name, rsT, nsT, kc, og, og_cols, dst):
        c, p = self.cfg, self.p
        ld, pe, ve, se, st = (p.sem(n) for n in ("ld", "pe", "ve", "se", "st"))
        nrep = P // c.QBLK
        lo, hi = og * og_cols, (og + 1) * og_cols
        lv = p.dma("sync", dst, self.da(wT_name)[kc * P:(kc + 1) * P, lo:hi],
                   self.p.sems["ld"])
        rrep = self.sb["T9"].ap()[:, :og_cols]
        srep = self.sb["T3"].ap()[:, :og_cols]
        src_r = rsT[nrep * kc:nrep * kc + nrep, lo:hi].unsqueeze(1) \
            .to_broadcast([nrep, c.QBLK, og_cols])
        src_s = nsT[nrep * kc:nrep * kc + nrep, lo:hi].unsqueeze(1) \
            .to_broadcast([nrep, c.QBLK, og_cols])
        p.wait("gpsimd", ve)
        p.wait("gpsimd", pe)
        r1 = p.dma("gpsimd", rrep, src_r, st)
        r2 = p.dma("gpsimd", srep, src_s, st)
        tmp = self.sb["T4"].ap()[:, :og_cols]
        p.wait("vector", ld, lv)
        p.wait("vector", st, r2)

        def dq(e, dst=dst, rrep=rrep, srep=srep, tmp=tmp):
            w = tmp.shape[-1]
            qi = self.sb["qint"].ap()[:, :w]
            di = dst.bitcast(U32)
            ti = tmp.bitcast(U32)
            e.tensor_mul(dst, dst, rrep)
            e.tensor_single_scalar(ti, di, 0x7FFFFFFF, ALU.bitwise_and)
            e.tensor_scalar(tmp, tmp, self.sb["chalf"].ap(), None, ALU.add)
            e.tensor_copy(qi, tmp)
            e.tensor_copy(tmp, qi)
            e.tensor_single_scalar(di, di, 0x80000000, ALU.bitwise_and)
            e.tensor_single_scalar(di, di, 0x80000000, ALU.bitwise_xor)
            e.tensor_tensor(di, di, ti, ALU.bitwise_or)
            return e.tensor_mul(dst, dst, srep)
        return p.run("vector", dq, ve)

    def dequant_spill(self, wT, wN, n_in_chunks, n_out, natbuf_fn, spname,
                      srows):
        c, p = self.cfg, self.p
        ld, pe, ve, se, st = (p.sem(n) for n in ("ld", "pe", "ve", "se", "st"))
        rsT = self.sb["rows"].ap()[32:32 + srows, 0:n_out]
        nsT = self.sb["rows"].ap()[32:32 + srows, 3328:3328 + n_out]
        vev = self.dequant_matrix(wT, wN, n_in_chunks, n_out, natbuf_fn,
                                  rsT, nsT)
        OG = 512 if n_out % 512 == 0 else 256
        nog = n_out // OG
        for og in range(nog):
            for kc in range(n_in_chunks):
                dst = self.sb["T6"].ap()[:, :OG]
                p.wait("sync", st)
                p.wait("sync", ve)
                dvev = self.dequant_group(wT, rsT, nsT, kc, og, OG, dst)
                p.wait("gpsimd", ve, dvev)
                p.dma("gpsimd",
                      self.da(spname)[kc * P:(kc + 1) * P,
                                      og * OG:(og + 1) * OG], dst, st)

    def ph_l0_ffn_F1(self):
        c, p = self.cfg, self.p
        ld, pe, ve, se, st, cc = (p.sem(n) for n in
                                  ("ld", "pe", "ve", "se", "st", "cc"))
        sbc, rbc, vev = self.scale_from_cc("cc_max1_out", 0, 0)
        p.wait("sync", ve, vev)
        for tt in range(c.NTT):
            lo, hi = tt * c.TT, (tt + 1) * c.TT
            p.wait("sync", st)
            p.wait("sync", ve)
            for i in range(c.DC):
                lv = p.dma("sync", self.tslot(i),
                           self.da("xn_sp")[i * P:(i + 1) * P, lo:hi], ld)
                xv = self.tslot(i)
                tmp = self.sb["T0"].ap()[:, :c.TT]
                p.wait("vector", ld, lv)
                vev = p.run("vector", lambda e, xv=xv, tmp=tmp, rbc=rbc,
                            sbc=sbc: self.qdq_chain(e, xv, xv, rbc, sbc, tmp),
                            ve)
                p.wait("gpsimd", ve, vev)
                p.dma("gpsimd", self.da("xq_sp")[i * P:(i + 1) * P, lo:hi],
                      xv, st)
        def natbuf(rt):
            return self.sb[f"WB{rt % 2}"].ap()[:, :c.D]
        self.dequant_spill("wupT", "wup_n", c.DC, c.FI, natbuf, "wupdq_sp",
                           c.D // c.QBLK)
        self.dequant_spill("wgateT", "wgate_n", c.DC, c.FI, natbuf,
                           "wgatedq_sp", c.D // c.QBLK)
        mx2 = self.sb["sm"].ap()[:, 16:17]
        mx3 = self.sb["sm"].ap()[:, 17:18]
        p.run("vector", lambda e: e.memset(mx2, 0.0))
        p.run("vector", lambda e: e.memset(mx3, 0.0))
        wbuf = [self.sb["WB0"].ap(), self.sb["WB1"].ap()]
        for mi, (dqsp, rawsp, mxc) in enumerate((
                ("wupdq_sp", "upraw_sp", mx2),
                ("wgatedq_sp", "gateraw_sp", mx3))):
            for ts in range(c.NTT):
                lo, hi = ts * c.TT, (ts + 1) * c.TT
                p.wait("sync", pe)
                p.wait("sync", ve)
                p.wait("sync", st)
                xl = None
                for i in range(c.DC):
                    xl = p.dma("sync", self.tslot(i),
                               self.da("xq_sp")[i * P:(i + 1) * P, lo:hi], ld)
                wld = [None, None]
                wld[0] = self.colslice_load(dqsp, 0, wbuf[0], c.DC, ld)
                wld[1] = self.colslice_load(dqsp, 1, wbuf[1], c.DC, ld)
                for oc in range(c.FIC):
                    b = oc % 2
                    bank = oc % 2
                    ps = self.psA(0, c.TT) if bank == 0 else self.psA(512, 512 + c.TT)
                    p.wait("tensor", ld, wld[b])
                    p.wait("tensor", ld, xl)
                    p.wait("tensor", se)
                    pev = None
                    for kc in range(c.DC):
                        lhs = wbuf[b][:, kc * P:(kc + 1) * P]
                        rhs = self.tslot(kc)
                        pev = p.run("tensor", lambda e, lhs=lhs, rhs=rhs,
                                    ps=ps, kc=kc: e.matmul(
                                        ps, lhs, rhs, start=(kc == 0),
                                        stop=(kc == c.DC - 1)), pe)
                    if oc + 2 < c.FIC:
                        p.wait("sync", pe, pev)
                        wld[b] = self.colslice_load(dqsp, oc + 2, wbuf[b],
                                                    c.DC, ld)
                    sg = oc % 2
                    stg = self.sb[f"T{sg}"].ap()[:, :c.TT]
                    p.wait("scalar", pe, pev)
                    p.wait("scalar", st)
                    p.wait("scalar", ve)
                    sev = p.run("scalar", lambda e, ps=ps, stg=stg: e.copy(
                        stg, ps), se)
                    p.wait("vector", se, sev)

                    def mxop(e, stg=stg, mxc=mxc):
                        e.tensor_reduce(self.sb["sm"].ap()[:, 20:21], stg,
                                        AX.X, ALU.max, apply_absolute_value=True)
                        e.drain()
                        return e.tensor_max(mxc, mxc, self.sb["sm"].ap()[:, 20:21])
                    vev = p.run("vector", mxop, ve)
                    p.wait("gpsimd", se, sev)
                    p.wait("gpsimd", ve, vev)
                    p.dma("gpsimd", self.da(rawsp)[oc * P:(oc + 1) * P, lo:hi],
                          stg, st)
        row = self.sb["rows"].ap()[0:1, 6144:][:, :P]
        b8 = self.sb["rows"].ap()[0:1, 6144:][:, 128:136]
        p.wait("gpsimd", ve)
        v1 = p.dma("gpsimd", row, mx2, st)
        p.wait("vector", st, v1)
        g2 = self.sb["sm"].ap()[0:1, 21:22]
        vev = p.run("vector", lambda e, g2=g2, row=row: e.tensor_reduce(
            g2, row, AX.X, ALU.max), ve)
        p.wait("gpsimd", ve, vev)
        v2 = p.dma("gpsimd", row, mx3, st)
        p.wait("vector", st, v2)
        g3 = self.sb["sm"].ap()[0:1, 22:23]

        def pack(e, g3=g3, row=row, b8=b8, g2=g2):
            e.tensor_reduce(g3, row, AX.X, ALU.max)
            e.memset(b8, 0.0)
            e.drain()
            e.tensor_copy(b8[:, 0:1], g2)
            return e.tensor_copy(b8[:, 1:2], g3)
        vev = p.run("vector", pack, ve)
        p.wait("gpsimd", ve, vev)
        v3 = p.dma("gpsimd", self.da("cc_max23_in")[:, :], b8, st)
        p.wait("gpsimd", st, v3)
        p.run("gpsimd", lambda e: e.collective_compute(
            "AllReduce", ALU.max, replica_groups=[list(range(8))],
            ins=[self.dram["cc_max23_in"].ap().opt()],
            outs=[self.dram["cc_max23_out"].ap().opt()]), cc)

    def ph_l0_attn_B(self):
        c, p = self.cfg, self.p
        ld, pe, ve, se, st, cc = (p.sem(n) for n in
                                  ("ld", "pe", "ve", "se", "st", "cc"))
        orow = self.sb["conesrow"].ap()
        C0 = self.sb["C0"].ap()
        C1 = self.sb["C1"].ap()
        carry = self.sb["csum2"].ap()[:, :2 * self.cfg.DC]
        runc = self.sb["csum2"].ap()[:, 2 * self.cfg.DC:]
        gath = self.sb["rows"].ap()[32:40, :]
        cpick = self.sb["rows"].ap()[32:40, self.RW - 8:self.RW - 4]
        p.wait("sync", p.sems["cc"], self._cc5)
        lv = p.dma("sync", gath[:, :2 * c.D], self.da("cc_carry_out")[:, :], ld)
        p.wait("tensor", ld, lv)
        pev = None
        for j in range(0, 2 * c.D, 512):
            w = min(512, 2 * c.D - j)
            psx = self.psB(1024 + j // 4, 1024 + j // 4 + w, rows=1) if False \
                else self.psA(j % 2048, j % 2048 + w, rows=1)
            pev = p.run("tensor", lambda e, psx=psx, j=j, w=w: e.matmul(
                psx, cpick[:, 0:1], gath[:, j:j + w], start=True, stop=True), pe)
        carrow = self.sb["rows"].ap()[0:1, 6144:][0:1, :]
        carrow = self.row(0, 0, 2 * c.D)
        p.wait("vector", pe, pev)
        vev = None
        for j in range(0, 2 * c.D, 512):
            w = min(512, 2 * c.D - j)
            psx = self.psA(j % 2048, j % 2048 + w, rows=1)
            vev = p.run("vector", lambda e, psx=psx, j=j, w=w, carrow=carrow:
                        e.tensor_copy(carrow[:, j:j + w], psx), ve)
        p.wait("gpsimd", ve, vev)
        cv = p.dma("gpsimd", carry, carrow, st)
        p.wait("vector", st, cv)
        vev = p.run("vector", lambda e: e.tensor_copy(runc, carry), ve)

        wbuf = [self.sb["WB0"].ap(), self.sb["WB1"].ap()]
        for tt in range(c.NTT):
            lo, hi = tt * c.TT, (tt + 1) * c.TT
            p.wait("sync", ve)
            p.wait("sync", pe)
            lw = lk = lg = None
            for i in range(c.DC):
                lw = p.dma("sync", C0[:, i * c.TT:(i + 1) * c.TT],
                           self.da("w_sp")[i * P:(i + 1) * P, lo:hi], ld)
            for i in range(c.DC):
                lk = p.dma("sync", C1[:, i * c.TT:(i + 1) * c.TT],
                           self.da("kv_sp")[i * P:(i + 1) * P, lo:hi], ld)
            for i in range(c.DC):
                lg = p.dma("sync", self.aslot(i),
                           self.da("gate_sp")[i * P:(i + 1) * P, lo:hi], ld)
            p.wait("vector", ld, lg)
            for i in range(c.DC):
                wt = C0[:, i * c.TT:(i + 1) * c.TT]
                kvt = C1[:, i * c.TT:(i + 1) * c.TT]
                gt = self.aslot(i)
                t4 = self.sb["T4"].ap()[:, :c.TT]
                t5 = self.sb["T5"].ap()[:, :c.TT]
                zcol = self.sb["sm"].ap()[:, 24:25]

                def scans(e, wt=wt, kvt=kvt, gt=gt, t4=t4, t5=t5, i=i,
                          runc=runc, zcol=zcol):
                    e.memset(zcol, 0.0)
                    zb = zcol.to_broadcast([P, c.TT])
                    e.tensor_tensor_scan(t4, wt, zb, runc[:, i:i + 1],
                                         ALU.add, ALU.bypass)
                    e.drain()
                    e.tensor_copy(runc[:, i:i + 1], t4[:, c.TT - 1:c.TT])
                    e.tensor_tensor_scan(t5, kvt, zb,
                                         runc[:, c.DC + i:c.DC + i + 1],
                                         ALU.add, ALU.bypass)
                    e.drain()
                    e.tensor_copy(runc[:, c.DC + i:c.DC + i + 1],
                                  t5[:, c.TT - 1:c.TT])
                    e.tensor_scalar(t4, t4, 1e-6, None, ALU.add)
                    e.reciprocal(t4, t4)
                    e.tensor_mul(t5, t5, t4)
                    return e.tensor_mul(wt, t5, gt)
                vev = p.run("vector", scans, ve)
            NOC = c.SW // P
            wld = [None, None]
            p.wait("sync", ve, vev)
            p.wait("tensor", ve, vev)
            wld[0] = self.colslice_load("wswT", 0, wbuf[0], c.DC, ld)
            if NOC > 1:
                wld[1] = self.colslice_load("wswT", 1, wbuf[1], c.DC, ld)
            for oc in range(NOC):
                b = oc % 2
                bank = oc % 2
                ps = self.psA(0, c.TT) if bank == 0 else self.psA(512, 512 + c.TT)
                p.wait("tensor", ld, wld[b])
                p.wait("tensor", se)
                pev = None
                for kc in range(c.DC):
                    lhs = wbuf[b][:, kc * P:(kc + 1) * P]
                    rhs = C0[:, kc * c.TT:(kc + 1) * c.TT]
                    pev = p.run("tensor", lambda e, lhs=lhs, rhs=rhs, ps=ps,
                                kc=kc: e.matmul(ps, lhs, rhs, start=(kc == 0),
                                                stop=(kc == c.DC - 1)), pe)
                if oc + 2 < NOC:
                    p.wait("sync", pe, pev)
                    wld[b] = self.colslice_load("wswT", oc + 2, wbuf[b], c.DC, ld)
                if oc < c.DC:
                    p.wait("scalar", pe, pev)
                    p.wait("scalar", ve)
                    sev = p.run("scalar", lambda e, ps=ps, oc=oc, C1=C1: e.copy(
                        C1[:, oc * c.TT:(oc + 1) * c.TT], ps), se)
                else:
                    t6 = self.sb["T6"].ap()[:, :c.TT]
                    p.wait("scalar", pe, pev)
                    p.wait("scalar", ve)
                    sev = p.run("scalar", lambda e, ps=ps, t6=t6: e.activation(
                        t6, ps, ACT.Silu), se)
                    uslot = C1[:, (oc - c.DC) * c.TT:(oc - c.DC + 1) * c.TT]
                    p.wait("vector", se, sev)
                    vev = p.run("vector", lambda e, uslot=uslot, t6=t6:
                                e.tensor_mul(uslot, uslot, t6), ve)
            p.wait("sync", pe)
            wld[0] = self.colslice_load("woutT", 0, wbuf[0], c.DC, ld)
            if c.DC > 1:
                wld[1] = self.colslice_load("woutT", 1, wbuf[1], c.DC, ld)
            for oc in range(c.DC):
                b = oc % 2
                bank = oc % 2
                ps = self.psB(0, c.TT) if bank == 0 else self.psB(512, 512 + c.TT)
                p.wait("tensor", ld, wld[b])
                p.wait("tensor", ve)
                pev = None
                for kc in range(c.DC):
                    lhs = wbuf[b][:, kc * P:(kc + 1) * P]
                    rhs = C1[:, kc * c.TT:(kc + 1) * c.TT]
                    pev = p.run("tensor", lambda e, lhs=lhs, rhs=rhs, ps=ps,
                                kc=kc: e.matmul(ps, lhs, rhs, start=(kc == 0),
                                                stop=(kc == c.DC - 1)), pe)
                if oc + 2 < c.DC:
                    p.wait("sync", pe, pev)
                    wld[b] = self.colslice_load("woutT", oc + 2, wbuf[b], c.DC, ld)
                at = self.sb[f"acc{oc}"].ap()[:, lo:hi]
                p.wait("vector", pe, pev)
                vev = p.run("vector", lambda e, at=at, ps=ps: e.tensor_add(
                    at, at, ps), ve)

    def _gate2_tile(self, gq, tmp, tmp2, s3bc, r3bc):
        c, p = self.cfg, self.p
        ve, se = p.sem("ve"), p.sem("se")
        vev = p.run("vector", lambda e, gq=gq, tmp=tmp, r3bc=r3bc, s3bc=s3bc:
                    self.qdq_chain(e, gq, gq, r3bc, s3bc, tmp), ve)
        p.wait("scalar", ve, vev)
        sev = p.run("scalar", lambda e, tmp=tmp, gq=gq: e.activation(
            tmp, gq, ACT.Sigmoid), se)
        p.wait("vector", se, sev)

        def fx(e, gq=gq, tmp=tmp, tmp2=tmp2):
            w = tmp.shape[-1]
            qi = self.sb["qint"].ap()[:, :w]
            e.tensor_scalar(tmp, tmp, 65536.0, None, ALU.mult)
            e.tensor_scalar(tmp, tmp, self.sb["chalf"].ap(), None, ALU.add)
            e.tensor_copy(qi, tmp)
            e.tensor_copy(tmp, qi)
            e.tensor_scalar(tmp, tmp, 65535.0, 1.0 / 65536.0, ALU.min, ALU.mult)
            return e.tensor_mul(gq, gq, tmp)
        return p.run("vector", fx, ve)

    def ph_l0_ffn_F23(self):
        c, p = self.cfg, self.p
        ld, pe, ve, se, st, cc = (p.sem(n) for n in
                                  ("ld", "pe", "ve", "se", "st", "cc"))
        s3bc, r3bc, vev = self.scale_from_cc("cc_max23_out", 1, 1)
        s2bc, r2bc, vev = self.scale_from_cc("cc_max23_out", 0, 2)
        C0 = self.sb["C0"].ap()
        C1 = self.sb["C1"].ap()
        mx4 = self.sb["sm"].ap()[:, 26:27]
        mx5 = self.sb["sm"].ap()[:, 27:28]
        p.run("vector", lambda e: e.memset(mx4, 0.0))
        tmp = self.sb["T4"].ap()[:, :c.TT]
        tmp2 = self.sb["T5"].ap()[:, :c.TT]
        NTL = c.FIC * c.NTT
        for it in range(NTL):
            oc, ts = divmod(it, c.NTT)
            lo, hi = ts * c.TT, (ts + 1) * c.TT
            slot = it % 4
            gq = C0[:, slot * c.TT:(slot + 1) * c.TT]
            p.wait("sync", ve)
            lv = p.dma("sync", gq, self.da("gateraw_sp")[oc * P:(oc + 1) * P, lo:hi], ld)
            p.wait("vector", ld, lv)
            gvev = self._gate2_tile(gq, tmp, tmp2, s3bc, r3bc)

            def mxop(e, gq=gq, mx4=mx4):
                e.tensor_reduce(self.sb["sm"].ap()[:, 20:21], gq, AX.X,
                                ALU.max, apply_absolute_value=True)
                e.drain()
                return e.tensor_max(mx4, mx4, self.sb["sm"].ap()[:, 20:21])
            vev = p.run("vector", mxop, ve)
        row = self.sb["rows"].ap()[0:1, 6144:][:, :P]
        b8 = self.sb["rows"].ap()[0:1, 6144:][:, 128:136]
        p.wait("gpsimd", ve, vev)
        v1 = p.dma("gpsimd", row, mx4, st)
        p.wait("vector", st, v1)
        g4 = self.sb["sm"].ap()[0:1, 21:22]

        def pk(e, g4=g4, row=row, b8=b8):
            e.tensor_reduce(g4, row, AX.X, ALU.max)
            e.memset(b8, 0.0)
            e.drain()
            return e.tensor_copy(b8[:, 0:1], g4)
        vev = p.run("vector", pk, ve)
        p.wait("gpsimd", ve, vev)
        v2 = p.dma("gpsimd", self.da("cc_max4_in")[:, :], b8, st)
        p.wait("gpsimd", st, v2)
        p.run("gpsimd", lambda e: e.collective_compute(
            "AllReduce", ALU.max, replica_groups=[list(range(8))],
            ins=[self.dram["cc_max4_in"].ap().opt()],
            outs=[self.dram["cc_max4_out"].ap().opt()]), cc)
        s4bc, r4bc, vev = self.scale_from_cc("cc_max4_out", 0, 3)
        p.run("vector", lambda e: e.memset(mx5, 0.0))
        for it in range(NTL):
            oc, ts = divmod(it, c.NTT)
            lo, hi = ts * c.TT, (ts + 1) * c.TT
            slot = it % 4
            gq = C0[:, slot * c.TT:(slot + 1) * c.TT]
            uq = C1[:, slot * c.TT:(slot + 1) * c.TT]
            p.wait("sync", ve)
            p.wait("sync", st)
            lv = p.dma("sync", gq, self.da("gateraw_sp")[oc * P:(oc + 1) * P, lo:hi], ld)
            lv2 = p.dma("sync", uq, self.da("upraw_sp")[oc * P:(oc + 1) * P, lo:hi], ld)
            p.wait("vector", ld, lv)
            gvev = self._gate2_tile(gq, tmp, tmp2, s3bc, r3bc)

            def oraw(e, gq=gq, uq=uq, tmp=tmp, r4bc=r4bc, s4bc=s4bc,
                     r2bc=r2bc, s2bc=s2bc, mx5=mx5):
                self.qdq_chain(e, gq, gq, r4bc, s4bc, tmp)
                self.qdq_chain(e, uq, uq, r2bc, s2bc, tmp)
                e.tensor_mul(gq, gq, uq)
                e.tensor_reduce(self.sb["sm"].ap()[:, 20:21], gq, AX.X,
                                ALU.max, apply_absolute_value=True)
                e.drain()
                return e.tensor_max(mx5, mx5, self.sb["sm"].ap()[:, 20:21])
            p.wait("vector", ld, lv2)
            vev = p.run("vector", oraw, ve)
            p.wait("gpsimd", ve, vev)
            p.dma("gpsimd", self.da("oraw_sp")[oc * P:(oc + 1) * P, lo:hi], gq, st)
        p.wait("gpsimd", ve, vev)
        v1 = p.dma("gpsimd", row, mx5, st)
        p.wait("vector", st, v1)
        g5 = self.sb["sm"].ap()[0:1, 21:22]

        def pk5(e, g5=g5, row=row, b8=b8):
            e.tensor_reduce(g5, row, AX.X, ALU.max)
            e.memset(b8, 0.0)
            e.drain()
            return e.tensor_copy(b8[:, 0:1], g5)
        vev = p.run("vector", pk5, ve)
        p.wait("gpsimd", ve, vev)
        v2 = p.dma("gpsimd", self.da("cc_max5_in")[:, :], b8, st)
        p.wait("gpsimd", st, v2)
        p.run("gpsimd", lambda e: e.collective_compute(
            "AllReduce", ALU.max, replica_groups=[list(range(8))],
            ins=[self.dram["cc_max5_in"].ap().opt()],
            outs=[self.dram["cc_max5_out"].ap().opt()]), cc)

    def ph_l0_ffn_F4(self):
        c, p = self.cfg, self.p
        ld, pe, ve, se, st, cc = (p.sem(n) for n in
                                  ("ld", "pe", "ve", "se", "st", "cc"))
        C0 = self.sb["C0"].ap()

        def natbuf(rt):
            if 2 * c.TOK >= c.FI:
                return C0[:, :c.FI]
            return self.sb[f"WB{rt % 2}"].ap()[:, :c.FI]
        self.dequant_spill("wdownT", "wdown_n", c.FIC, c.D, natbuf,
                           "wdowndq_sp", c.FI // c.QBLK)
        p.wait("sync", st)
        s5bc, r5bc, vev = self.scale_from_cc("cc_max5_out", 0, 4)
        wbuf = [self.sb["WB0"].ap(), self.sb["WB1"].ap()]
        tmp = self.sb["T0"].ap()[:, :c.TT]
        WKC = min(c.FIC, 2048 // P)
        for tt in range(c.NTT):
            lo, hi = tt * c.TT, (tt + 1) * c.TT
            p.wait("sync", ve)
            p.wait("sync", pe)
            p.wait("sync", se)
            lvs = []
            for i in range(c.FIC):
                lvs.append(p.dma("sync", self.oslot(i),
                                 self.da("oraw_sp")[i * P:(i + 1) * P, lo:hi], ld))
            for i in range(c.FIC):
                sl = self.oslot(i)
                p.wait("vector", ld, lvs[i])
                vev = p.run("vector", lambda e, sl=sl, tmp=tmp, r5bc=r5bc,
                            s5bc=s5bc: self.qdq_chain(e, sl, sl, r5bc, s5bc,
                                                      tmp), ve)
            p.wait("tensor", ve, vev)
            for oc in range(c.DC):
                b = oc % 2
                bank = oc % 2
                ps = self.psB(0, c.TT) if bank == 0 else self.psB(512, 512 + c.TT)
                nw = (c.FIC + WKC - 1) // WKC
                pev = None
                for w in range(nw):
                    k0, k1 = w * WKC, min((w + 1) * WKC, c.FIC)
                    p.wait("sync", pe)
                    wl = None
                    for kc in range(k0, k1):
                        wl = p.dma("sync", wbuf[b][:, (kc - k0) * P:(kc - k0 + 1) * P],
                                   self.da("wdowndq_sp")[kc * P:(kc + 1) * P,
                                                         oc * P:(oc + 1) * P],
                                   p.sems["ld"])
                    p.wait("tensor", ld, wl)
                    if w == 0:
                        p.wait("tensor", ve)
                    for kc in range(k0, k1):
                        lhs = wbuf[b][:, (kc - k0) * P:(kc - k0 + 1) * P]
                        pev = p.run("tensor", lambda e, lhs=lhs, kc=kc, ps=ps:
                                    e.matmul(ps, lhs, self.oslot(kc),
                                             start=(kc == 0),
                                             stop=(kc == c.FIC - 1)), pe)
                at = self.sb[f"acc{oc}"].ap()[:, lo:hi]
                p.wait("vector", pe, pev)
                vev = p.run("vector", lambda e, at=at, ps=ps: e.tensor_add(
                    at, at, ps), ve)

    def ph_l1_norm(self):
        c, p = self.cfg, self.p
        ld, pe, ve, se, st, cc = (p.sem(n) for n in
                                  ("ld", "pe", "ve", "se", "st", "cc"))
        ones = self.sb["cones"].ap()
        orow = self.sb["conesrow"].ap()
        stats = self.sb["rows"].ap()[0:1, :]
        C0 = self.sb["C0"].ap()
        for tt in range(c.NTT):
            lo, hi = tt * c.TT, (tt + 1) * c.TT
            pss0 = self.ps["A"].ap()[0:1, 0:c.TT]
            pss1 = self.ps["A"].ap()[0:1, 512:512 + c.TT]
            p.wait("scalar", pe)
            p.wait("scalar", ve)
            sev = None
            for i in range(c.DC):
                sq = C0[:, i * c.TT:(i + 1) * c.TT]
                xv = self.sb[f"acc{i}"].ap()[:, lo:hi]
                sev = p.run("scalar", lambda e, sq=sq, xv=xv: e.activation(
                    sq, xv, ACT.Square), se)
            p.wait("tensor", ve)
            for i in range(c.DC):
                xv = self.sb[f"acc{i}"].ap()[:, lo:hi]
                p.run("tensor", lambda e, i=i, pss0=pss0, xv=xv: e.matmul(
                    pss0, ones, xv, start=(i == 0), stop=(i == c.DC - 1)))
            p.wait("tensor", se, sev)
            pev = None
            for i in range(c.DC):
                sq = C0[:, i * c.TT:(i + 1) * c.TT]
                pev = p.run("tensor", lambda e, i=i, sq=sq, pss1=pss1: e.matmul(
                    pss1, ones, sq, start=(i == 0), stop=(i == c.DC - 1)), pe)
            p.wait("vector", pe, pev)
            s_m = stats[0:1, lo:hi]
            s_r = self.sb["rows"].ap()[0:1, 4096 + lo:4096 + hi]
            Df = float(c.D)

            def stat2(e, pss0=pss0, pss1=pss1, s_m=s_m, s_r=s_r, Df=Df):
                e.tensor_scalar(s_m, pss0, 1.0 / Df, None, ALU.mult)
                e.tensor_scalar(s_r, pss1, 1.0 / Df, None, ALU.mult)
                e.tensor_mul(pss0, s_m, s_m)
                e.tensor_sub(s_r, s_r, pss0)
                return e.tensor_scalar(s_r, s_r, 1e-5, None, ALU.add)
            vev = p.run("vector", stat2, ve)
            p.wait("scalar", ve, vev)
            sev = p.run("scalar", lambda e, s_r=s_r: e.activation(s_r, s_r, ACT.Sqrt), se)
            p.wait("vector", se, sev)
            vev = p.run("vector", lambda e, s_r=s_r: e.reciprocal(s_r, s_r), ve)
            bcm = self.psA(1024, 1024 + c.TT)
            bcr = self.psB(0, c.TT)
            p.wait("tensor", ve, vev)
            p.run("tensor", lambda e, s_m=s_m, bcm=bcm: e.matmul(
                bcm, orow, s_m, start=True, stop=True))
            pev = p.run("tensor", lambda e, s_r=s_r, bcr=bcr: e.matmul(
                bcr, orow, s_r, start=True, stop=True), pe)
            p.wait("vector", pe, pev)
            for i in range(c.DC):
                tn = self.sb[f"T{1 + (i % 2)}"].ap()[:, :c.TT]
                xt = self.sb[f"acc{i}"].ap()[:, lo:hi]
                p.wait("vector", st)

                def xn_ops(e, xt=xt, tn=tn, bcm=bcm, bcr=bcr):
                    e.tensor_sub(tn, xt, bcm)
                    e.tensor_mul(tn, tn, bcr)
                    return e.tensor_add(xt, xt, tn)
                vev = p.run("vector", xn_ops, ve)
                p.wait("gpsimd", ve, vev)
                p.dma("gpsimd", self.da("xn1_sp")[i * P:(i + 1) * P, lo:hi], tn, st)
                p.dma("gpsimd", self.da("cc_xn1_in")[i * P:(i + 1) * P, lo:hi], tn, st)
        p.wait("gpsimd", st)
        self._cc6 = p.run("gpsimd", lambda e: e.collective_compute(
            "AllGather", ALU.bypass,
            replica_groups=[[0, 1], [2, 3], [4, 5], [6, 7]],
            ins=[self.dram["cc_xn1_in"].ap().opt()],
            outs=[self.dram["cc_xn1_out"].ap().opt()]), cc)

    def ph_l1_attn(self):
        c, p = self.cfg, self.p
        ld, pe, ve, se, st, cc = (p.sem(n) for n in
                                  ("ld", "pe", "ve", "se", "st", "cc"))
        ones = self.sb["cones"].ap()
        orow = self.sb["conesrow"].ap()
        eye = self.sb["consts"].ap()
        C0 = self.sb["C0"].ap()
        C1 = self.sb["C1"].ap()
        GH = min(2, c.HPC)
        NG = c.HPC // GH
        NTB = c.T // 512
        p.wait("sync", p.sems["cc"], self._cc6)

        wbuf = [self.sb["WB0"].ap(), self.sb["WB1"].ap()]
        smax = self.sb["rows"].ap()[32:32 + min(2, self.cfg.HPC), :self.cfg.T]
        qsel = self.sb["qsel"].ap()
        expb = self.sb["expb"].ap()
        ctxb = self.sb["ctxb"].ap()
        for hg in range(NG):
            p.wait("sync", pe)
            p.wait("sync", ve)
            lq = lk = lv = None
            for ocl in range(GH * c.DK // P):
                oc = hg * (GH * c.DK // P) + ocl
                for kc in range(c.DC):
                    lq = p.dma("sync", wbuf[0][:, (ocl * c.DC + kc) * P:
                                                (ocl * c.DC + kc + 1) * P],
                               self.da("wqT_my")[kc * P:(kc + 1) * P,
                                                 oc * P:(oc + 1) * P], ld)
                    lk = p.dma("sync", wbuf[1][:, (ocl * c.DC + kc) * P:
                                                (ocl * c.DC + kc + 1) * P],
                               self.da("wkT_my")[kc * P:(kc + 1) * P,
                                                 oc * P:(oc + 1) * P], ld)
            VW = GH * c.DK
            NOCL = GH * c.DK // P
            C1a = self.sb["C1"].ap()

            def wvslot(kc):
                return C1a[:, kc * VW:(kc + 1) * VW]
            for kc in range(c.DC):
                lv = p.dma("sync", wvslot(kc),
                           self.da("wvT_my")[kc * P:(kc + 1) * P,
                                             hg * VW:(hg + 1) * VW], ld)
            bv = p.dma("sync", self.sb["rows"].ap()[0:1, 2048:][0:1, 0:VW],
                       self.da("bv_my_row")[0:1, hg * VW:(hg + 1) * VW], ld)
            bql = None
            for which, bias in enumerate(("bq_my_t", "bk_my_t")):
                for ocl in range(NOCL):
                    bcol = self.da(bias)[:, hg * NOCL + ocl:hg * NOCL + ocl + 1]
                    bs = self.sb["sm"].ap()[:, 40 + which * NOCL + ocl:
                                            41 + which * NOCL + ocl]
                    bql = p.dma("sync", bs, bcol, ld, slow=True)

            TBW = min(512, c.TOK)
            NTB = c.T // TBW
            TSUBS = TBW // P
            for tb in range(NTB):
                xl = []
                p.wait("sync", pe)
                half = (tb * TBW) // c.TOK
                lcol = (tb * TBW) % c.TOK
                for kc in range(c.DC):
                    xl.append(p.dma("sync", self.cslot(0, kc, TBW),
                                    self.da("cc_xn1_out")[
                                        half * c.D + kc * P:
                                        half * c.D + (kc + 1) * P,
                                        lcol:lcol + TBW],
                                    ld))
                p.wait("tensor", ld, xl[-1])
                p.wait("tensor", ld, lv)
                for which, (wb, bias, dst0) in enumerate((
                        (wbuf[0], "bq_my_t", 0), (wbuf[1], "bk_my_t", 1))):
                    for ocl in range(NOCL):
                        bank = (which * NOCL + ocl) % 2
                        ps = self.psA(0, TBW) if bank == 0 else self.psA(512, 512 + TBW)
                        p.wait("tensor", se)
                        pev = None
                        for kc in range(c.DC):
                            lhs = wb[:, (ocl * c.DC + kc) * P:(ocl * c.DC + kc + 1) * P]
                            rhs = self.cslot(0, kc, TBW)
                            pev = p.run("tensor", lambda e, lhs=lhs, rhs=rhs,
                                        ps=ps, kc=kc: e.matmul(
                                            ps, lhs, rhs, start=(kc == 0),
                                            stop=(kc == c.DC - 1)), pe)
                        dst = self.sb[f"bigA{dst0 + ocl}"].ap()[:, tb * TBW:(tb + 1) * TBW]
                        bs = self.sb["sm"].ap()[:, 40 + which * NOCL + ocl:
                                                41 + which * NOCL + ocl]
                        p.wait("scalar", ld, bql)
                        p.wait("scalar", pe, pev)
                        sev = p.run("scalar", lambda e, dst=dst, ps=ps, bs=bs:
                                    e.add(dst, ps, bs), se)
                for tsub in range(TSUBS):
                    psv = self.psB(0, VW)
                    p.wait("tensor", se)
                    pev = None
                    for kc in range(c.DC):
                        lhs = self.cslot(0, kc, TBW)[:, tsub * P:(tsub + 1) * P]
                        rhs = wvslot(kc)
                        pev = p.run("tensor", lambda e, lhs=lhs, rhs=rhs,
                                    psv=psv, kc=kc: e.matmul(
                                        psv, lhs, rhs, start=(kc == 0),
                                        stop=False), pe)
                    bvrow = self.sb["rows"].ap()[0:1, 2048:][0:1, 0:VW]
                    pev = p.run("tensor", lambda e, psv=psv, bvrow=bvrow:
                                e.matmul(psv, orow, bvrow, start=False,
                                         stop=True), pe)
                    vs = self.sb["T1"].ap()[:, :VW]
                    p.wait("scalar", pe, pev)
                    p.wait("scalar", st)
                    sev = p.run("scalar", lambda e, vs=vs, psv=psv:
                                e.copy(vs, psv), se)
                    rb = tb * TBW + tsub * P
                    p.wait("gpsimd", se, sev)
                    p.dma("gpsimd", self.da("v_sp")[rb:rb + P,
                                                    hg * VW:(hg + 1) * VW],
                          vs, st)
                wqmv = wbuf[0][:, :NOCL * c.DC * P].rearrange(
                    "p (o k n) -> p o k n", o=NOCL, k=c.DC)
                bqrow = self.sb["rows"].ap()[0:1, 2048:][0:1, 256:256 + VW]
                if tb == 0:
                    bqv = p.dma("sync", bqrow,
                                self.da("bq_my_row")[0:1, hg * VW:(hg + 1) * VW], ld)
                    p.wait("tensor", ld, bqv)
                for tsub in range(TSUBS):
                    psq = self.psB(1024, 1024 + VW)
                    p.wait("tensor", se)
                    pev = None
                    for kc in range(c.DC):
                        lhs = self.cslot(0, kc, TBW)[:, tsub * P:(tsub + 1) * P]
                        rhs = wqmv[:, :, kc, :]
                        pev = p.run("tensor", lambda e, lhs=lhs, rhs=rhs,
                                    psq=psq, kc=kc: e.matmul(
                                        psq, lhs, rhs, start=(kc == 0),
                                        stop=False), pe)
                    pev = p.run("tensor", lambda e, psq=psq, bqrow=bqrow:
                                e.matmul(psq, orow, bqrow, start=False,
                                         stop=True), pe)
                    stg = self.sb["T1"].ap()[:, :VW]
                    p.wait("scalar", pe, pev)
                    p.wait("scalar", st)
                    sev = p.run("scalar", lambda e, stg=stg, psq=psq:
                                e.copy(stg, psq), se)
                    rowbase = hg * c.T + tb * TBW + tsub * P
                    p.wait("gpsimd", se, sev)
                    p.dma("gpsimd",
                          self.da("qtm_sp")[rowbase:rowbase + P, :], stg, st)
            if self.debug_stop == "l1a1":
                return
            smx = self.sb["T0"].ap()[:, :c.LC]
            for hh in range(GH):
                qa = self.sb["bigA0"].ap()
                ka = self.sb["bigA1"].ap()
                base = (hh % 2) * c.DK
                for lt in range(c.LC):
                    nb = min(4, c.NKT)
                    pev = None
                    mxv = self.sb["sm"].ap()[:, 48:49]
                    for half in range((c.NKT + nb - 1) // nb):
                        psr = (self.ps["A"] if half == 0 else self.ps["B"]).ap()
                        p.wait("tensor", se)
                        p.wait("tensor", ve)
                        for g in range(nb):
                            kn = half * nb + g
                            if kn >= c.NKT:
                                continue
                            lhs = qa[base:base + c.DK, lt * P:(lt + 1) * P]
                            rhs = ka[base:base + c.DK, kn * c.KN:(kn + 1) * c.KN]
                            psx = psr[0:P, g * c.KN:(g + 1) * c.KN]
                            pev = p.run("tensor", lambda e, lhs=lhs, rhs=rhs,
                                        psx=psx: e.matmul(psx, lhs, rhs,
                                                          start=True, stop=True), pe)
                        wid = min(nb * c.KN, c.T - half * nb * c.KN)
                        p.wait("vector", pe, pev)

                        def red(e, psr=psr, wid=wid, half=half, mxv=mxv,
                                smx=smx, lt=lt):
                            e.tensor_reduce(self.sb["sm"].ap()[:, 49:50],
                                            psr[0:P, 0:wid], AX.X, ALU.max)
                            e.drain()
                            if half == 0:
                                return e.tensor_copy(mxv, self.sb["sm"].ap()[:, 49:50])
                            e.tensor_max(mxv, mxv, self.sb["sm"].ap()[:, 49:50])
                            e.drain()
                            return e.tensor_copy(smx[:, lt:lt + 1], mxv)
                        vev = p.run("vector", red, ve)
                        if c.NKT <= nb:
                            pass
                    if c.NKT <= nb:
                        def cpmx(e, smx=smx, lt=lt, mxv=mxv):
                            e.drain()
                            return e.tensor_copy(smx[:, lt:lt + 1], mxv)
                        vev = p.run("vector", cpmx, ve)
                pst = self.psB(1024, 1024 + P, rows=c.LC)
                p.wait("tensor", ve, vev)
                pev = p.run("tensor", lambda e, pst=pst, smx=smx, eye=eye:
                            e.transpose(pst, smx, eye), pe)
                t1 = self.sb["T1"].ap()[:c.LC, :P]
                p.wait("vector", pe, pev)
                p.wait("vector", st)
                vev = p.run("vector", lambda e, t1=t1, pst=pst:
                            e.tensor_copy(t1, pst), ve)
                p.wait("gpsimd", ve, vev)
                fv = p.dma("gpsimd", smax[hh:hh + 1, :], t1, st)
            if self.debug_stop == "l1a2":
                return
            g0 = hg * GH
            p.wait("vector", p.sems["st"])

            def sel8(e, g0=g0):
                e.max(self.sb["selv"].ap()[0:GH, :], smax[0:GH, :])
                e.drain()
                return e.max_index(self.sb["seli"].ap()[0:GH, :],
                                   self.sb["selv"].ap()[0:GH, :],
                                   smax[0:GH, :])
            vev = p.run("vector", sel8, ve)
            p.wait("gpsimd", ve, vev)
            iv = p.dma("gpsimd", self.sb["selrow"].ap()[0:1, g0 * 8:(g0 + GH) * 8],
                       self.sb["seli"].ap()[0:GH, :], st)
            fvv = p.dma("gpsimd", self.sb["rows"].ap()[0:1, 2048:][0:1, 512 + g0 * 8:512 + (g0 + GH) * 8],
                        self.sb["selv"].ap()[0:GH, :], st)
            s32g = self.sb["s32g0"].ap()
            def cps32(e, s32g=s32g):
                e.drain()
                return e.tensor_copy(s32g[:GH, :], self.sb["seli"].ap()[0:GH, :])
            vev = p.run("vector", cps32, ve)
            fparr = self.sb["rows"].ap()[0:1, 2048:][0:1, 768:768 + 128]
            vev = p.run("vector", lambda e, fparr=fparr: e.memset(fparr, 0.0), ve)
            p.wait("gpsimd", ve, vev)
            fav = None
            for hh in range(GH):
                fav = p.dma("gpsimd", fparr[0:1, hh * 32:hh * 32 + 8],
                            s32g[hh:hh + 1, :], st)
            psqi = self.psB(1544, 1545, rows=P)
            p.wait("tensor", st, fav)
            pev = p.run("tensor", lambda e, psqi=psqi, fparr=fparr:
                        e.transpose(psqi, fparr, self.sb["consts"].ap()[0:1, 0:1]), pe)
            gidxc = self.sb["gidxc"].ap()
            p.wait("vector", pe, pev)
            vev = p.run("vector", lambda e, gidxc=gidxc, psqi=psqi, hg=hg:
                        e.tensor_scalar(gidxc, psqi, float(hg * c.T), None,
                                        ALU.add), ve)
            p.wait("gpsimd", ve, vev)
            p.wait("gpsimd", st)
            from concourse.bass_core_rust import IndirectOffsetOnAxis as IOA
            qg = self.sb["qg"].ap()
            gqs = p.run("gpsimd", lambda e, qg=qg, gidxc=gidxc: e.indirect_dma_start(
                qg[:, :VW], None, self.da("qtm_sp")[:, :],
                IOA(gidxc, 0)), p.sems["st"], k=16)
            gqv = p.run("gpsimd", lambda e: e.drain(), p.sem("gp"))
            self._gqs = gqs
            if self.debug_stop == "l1a3":
                return
            for hh in range(GH):
                hr = hg * GH + hh
                qa = self.sb["bigA0"].ap()
                ka = self.sb["bigA1"].ap()
                base = (hh % 2) * c.DK
                p.wait("vector", p.sem("gp"), gqv)
                p.wait("vector", p.sems["st"], self._gqs)
                tq = self.sb["T1"].ap()[0:32, 64:64 + c.DK]

                def gq(e, hh=hh, base=base, qsel=qsel, tq=tq):
                    src = self.sb["qg"].ap()[hh * 32:hh * 32 + 32,
                                             hh * c.DK:(hh + 1) * c.DK]
                    e.transpose(tq, src)
                    e.drain()
                    e.tensor_copy(qsel[base:base + 32, :], tq[0:32, 0:8])
                    return e.tensor_copy(qsel[base + 32:base + 64, :],
                                         tq[0:32, 32:40])
                vev = p.run("vector", gq, ve)
                nm = self.sb["rows"].ap()[0:1, 2048:][0:1, 640:648]
                p.wait("vector", st, fvv)
                vev = p.run("vector", lambda e, nm=nm, hr=hr: e.tensor_scalar(
                    nm, self.sb["rows"].ap()[0:1, 2048:][0:1, 512 + hr * 8:512 + hr * 8 + 8],
                    -1.0, None, ALU.mult), ve)
                p.wait("tensor", ve, vev)
                for kt in range(c.LC):
                    bank = kt % 2
                    psx = self.psA(0, 8, rows=P) if bank == 0 \
                        else self.psA(512, 520, rows=P)
                    p.wait("tensor", se)
                    lhs = ka[base:base + c.DK, kt * P:(kt + 1) * P]
                    p.run("tensor", lambda e, lhs=lhs, psx=psx, qsel=qsel,
                          base=base: e.matmul(psx, lhs,
                                              qsel[base:base + c.DK, :],
                                              start=True, stop=False), pe)
                    pev = p.run("tensor", lambda e, psx=psx, nm=nm: e.matmul(
                        psx, orow, nm, start=False, stop=True), pe)
                    dst = expb[:, kt * 8:(kt + 1) * 8]
                    p.wait("scalar", pe, pev)
                    sev = p.run("scalar", lambda e, dst=dst, psx=psx:
                                e.activation(dst, psx, ACT.Exp, scale=0.125), se)
                pss = self.psB(1536, 1537, rows=8)
                p.wait("tensor", se, sev)
                pev = None
                for kt in range(c.LC):
                    src = expb[:, kt * 8:(kt + 1) * 8]
                    pev = p.run("tensor", lambda e, src=src, pss=pss, kt=kt:
                                e.matmul(pss, src, ones, start=(kt == 0),
                                         stop=(kt == c.LC - 1)), pe)
                s81 = self.sb["T1"].ap()[0:8, 0:1]
                p.wait("vector", pe, pev)
                vev = p.run("vector", lambda e, s81=s81, pss=pss:
                            e.tensor_copy(s81, pss), ve)
                pst8 = self.psB(1544, 1552, rows=1)
                p.wait("tensor", ve, vev)
                pev = p.run("tensor", lambda e, pst8=pst8, s81=s81, eye=eye:
                            e.transpose(pst8, s81, eye[0:8, 0:8]), pe)
                r8 = self.sb["rows"].ap()[0:1, 2048:][0:1, 656:664]
                p.wait("vector", pe, pev)

                def rc(e, r8=r8, pst8=pst8):
                    e.tensor_copy(r8, pst8)
                    e.drain()
                    return e.reciprocal(r8, r8)
                vev = p.run("vector", rc, ve)
                psb8 = self.psB(1552, 1560)
                p.wait("tensor", ve, vev)
                pev = p.run("tensor", lambda e, psb8=psb8, r8=r8: e.matmul(
                    psb8, orow, r8, start=True, stop=True), pe)
                rs8 = self.sb["T1"].ap()[:, 8:16]
                p.wait("vector", pe, pev)
                vev = p.run("vector", lambda e, rs8=rs8, psb8=psb8:
                            e.tensor_copy(rs8, psb8), ve)
                vh = self.sb["C1"].ap()[:, c.TOK:c.TOK + c.LC * c.DK]
                vsrc = self.da("v_sp").rearrange(
                    "(kt p) m -> p kt m", p=P)[:, :, hg * VW + hh * c.DK:
                                               hg * VW + (hh + 1) * c.DK]
                p.wait("sync", pe)
                p.wait("sync", st)
                vhl = p.dma("sync", vh.rearrange("p (kt d) -> p kt d", d=c.DK),
                            vsrc, ld)
                psc = self.psB(1560, 1568, rows=c.DK)
                pev = None
                p.wait("tensor", se)
                p.wait("tensor", ld, vhl)
                for kt in range(c.LC):
                    vsl = vh[:, kt * c.DK:(kt + 1) * c.DK]
                    src = expb[:, kt * 8:(kt + 1) * 8]
                    pev = p.run("tensor", lambda e, vsl=vsl, src=src, psc=psc,
                                kt=kt: e.matmul(psc, vsl, src, start=(kt == 0),
                                                stop=(kt == c.LC - 1)), pe)
                dstc = ctxb[:, hr * 8:(hr + 1) * 8]
                p.wait("vector", pe, pev)
                vev = p.run("vector", lambda e, dstc=dstc, psc=psc, rs8=rs8:
                            e.tensor_tensor(dstc, psc, rs8[0:c.DK, :], ALU.mult), ve)
        if self.debug_stop == "l1a4":
            return
        lvs = []
        p.wait("sync", ve)
        for kc in range(c.DC):
            lvs.append(p.dma("sync", self.sb[f"T{2 + kc % 8}"].ap()[:, 0:1],
                             self.da("xn1_sp")[kc * P:(kc + 1) * P,
                                               c.TOK - 1:c.TOK], ld, slow=True))
        lw = None
        for kc in range(c.DC):
            lw = p.dma("sync", wbuf[0][:, kc * c.NAV:(kc + 1) * c.NAV],
                       self.da("Wp1")[kc * P:(kc + 1) * P, :], ld)
        lb1 = p.dma("sync", self.sb["T1"].ap()[:, 16:17], self.da("bp1")[:, :], ld)
        psh = self.psB(0, 1, rows=P)
        p.wait("tensor", ld, lw)
        p.wait("tensor", ld, lvs[-1])
        p.wait("tensor", se)
        p.wait("tensor", ve)
        pev = None
        for kc in range(c.DC):
            lhs = wbuf[0][:, kc * c.NAV:(kc + 1) * c.NAV]
            rhs = self.sb[f"T{2 + kc % 8}"].ap()[:, 0:1]
            pev = p.run("tensor", lambda e, lhs=lhs, rhs=rhs, psh=psh, kc=kc:
                        e.matmul(psh, lhs, rhs, start=(kc == 0),
                                 stop=(kc == c.DC - 1)), pe)
        h1 = self.sb["T1"].ap()[:, 17:18]
        p.wait("scalar", pe, pev)
        p.wait("scalar", ld, lb1)
        sev = p.run("scalar", lambda e, h1=h1, psh=psh: e.activation(
            h1, psh, ACT.Gelu, bias=self.sb["T1"].ap()[:, 16:17]), se)
        p.wait("sync", pe)
        lw2 = p.dma("sync", self.sb["T3"].ap()[:c.NAV, 0:c.E], self.da("Wp2")[:, :], ld)
        lb2 = p.dma("sync", self.sb["T3"].ap()[:c.E, 8:9], self.da("bp2")[:, :], ld)
        psl = self.psB(512, 513, rows=c.E)
        p.wait("tensor", se, sev)
        p.wait("tensor", ld, lw2)
        pev = p.run("tensor", lambda e, psl=psl, h1=h1: e.matmul(
            psl, self.sb["T3"].ap()[:c.NAV, 0:c.E], h1, start=True,
            stop=True), pe)
        lg8 = self.sb["T3"].ap()[:c.E, 9:10]
        p.wait("scalar", pe, pev)
        p.wait("scalar", ld, lb2)
        sev = p.run("scalar", lambda e, lg8=lg8, psl=psl: e.add(
            lg8, psl, self.sb["T3"].ap()[:c.E, 8:9]), se)
        pstl = self.psB(1024, 1024 + c.E, rows=1)
        p.wait("tensor", se, sev)
        pev = p.run("tensor", lambda e, pstl=pstl, lg8=lg8, eye=eye:
                    e.transpose(pstl, lg8, eye[0:c.E, 0:c.E]), pe)
        lrow = self.sb["rows"].ap()[0:1, 2048:][0:1, 672:672 + c.E]
        vrow = self.sb["rows"].ap()[0:1, 2048:][0:1, 688:696]
        irow = self.sb["selrow"].ap()[0:1, 4 * c.HPC * 8 - 8:4 * c.HPC * 8]
        p.wait("vector", pe, pev)

        def amax(e, lrow=lrow, vrow=vrow, irow=irow, pstl=pstl):
            e.tensor_copy(lrow, pstl)
            e.drain()
            e.max(vrow, lrow)
            e.drain()
            return e.max_index(irow.bitcast(U32), vrow, lrow)
        vev = p.run("vector", amax, ve)
        selslot = self.sb["rows"].ap()[0:1, 2048:][0:1, 704:705]
        def cpsel(e, selslot=selslot, irow=irow):
            e.drain()
            return e.tensor_copy(selslot, irow[0:1, 0:1])
        vev = p.run("vector", cpsel, ve)
        if self.debug_stop == "l1a5":
            return
        nctx_ctx = c.DK * c.HPC * 8
        nctx = self._ctx_pack_len()
        zrow = self.sb["rows"].ap()[64:65, 0:512]
        vev = p.run("vector", lambda e, zrow=zrow: e.memset(zrow, 0.0), ve)
        idxf = self.sb["rows"].ap()[0:1, 2048:][0:1, 896:896 + c.HPC * 8]
        p.wait("vector", st)
        vev = p.run("vector", lambda e, idxf=idxf: e.tensor_copy(
            idxf, self.sb["selrow"].ap()[0:1, :c.HPC * 8]), ve)
        p.wait("gpsimd", ve, vev)
        p.wait("gpsimd", se)
        cin = self.da("cc_ctx_in")
        pv1 = p.dma("gpsimd", cin[0:1, 0:nctx_ctx], ctxb, st)
        pv2 = p.dma("gpsimd", cin[0:1, nctx_ctx:nctx_ctx + c.HPC * 8], idxf, st)
        pv3 = p.dma("gpsimd", cin[0:1, nctx_ctx + c.HPC * 8:
                                 nctx_ctx + c.HPC * 8 + 1], selslot, st)
        padlen = nctx - (nctx_ctx + c.HPC * 8 + 1)
        v = pv3
        off = nctx_ctx + c.HPC * 8 + 1
        while padlen > 0:
            wdt = min(512, padlen)
            v = p.dma("gpsimd", cin[0:1, off:off + wdt], zrow[0:1, :wdt], st)
            off += wdt
            padlen -= wdt
        p.wait("gpsimd", st, v)
        self._cc7 = p.run("gpsimd", lambda e: e.collective_compute(
            "AllGather", ALU.bypass, replica_groups=[list(range(8))],
            ins=[self.dram["cc_ctx_in"].ap().opt()],
            outs=[self.dram["cc_ctx_out"].ap().opt()]), cc)
        if self.debug_stop in ("l1a6", "dumppack"):
            return
        nctx = self._ctx_pack_len()
        gath = self.sb["rows"].ap()[32:40, :]
        cpick = self.sb["rows"].ap()[32:40, self.RW - 8:self.RW - 4]
        p.wait("sync", cc, self._cc7)
        lv = p.dma("sync", gath[:, :nctx], self.da("cc_ctx_out")[:, :], ld)
        p.wait("tensor", ld, lv)
        p.wait("tensor", ve)
        p.wait("tensor", se)
        packs = []
        stgrow = self.sb["rows"].ap()[64:65, 512:1024]
        for which, pickcol in ((0, 2), (1, 3)):
            spn = f"pp{which}_sp"
            pev = None
            for j in range(0, nctx, 512):
                w = min(512, nctx - j)
                psx = self.psA(j % 2048, j % 2048 + w, rows=1)
                pev = p.run("tensor", lambda e, psx=psx, j=j, w=w,
                            pickcol=pickcol: e.matmul(
                                psx, cpick[:, pickcol:pickcol + 1],
                                gath[:, j:j + w], start=True, stop=True), pe)
                p.wait("vector", pe, pev)
                p.wait("vector", st)
                vev = p.run("vector", lambda e, psx=psx, w=w, stgrow=stgrow:
                            e.tensor_copy(stgrow[0:1, :w], psx), ve)
                p.wait("gpsimd", ve, vev)
                p.dma("gpsimd", self.da(spn)[0:1, j:j + w], stgrow[0:1, :w], st)
                p.wait("tensor", ve, vev)
            packs.append(spn)
        ctxs = self.sb["ctxs"].ap()
        NIT = 2 * c.HPC * 8
        fr = self.sb["rows"].ap()[0:1, 2048:]
        ltrow_f = fr[0:1, 1024:1024 + NIT]
        nctx_ctx2 = c.DK * c.HPC * 8
        for sidx in range(2):
            spn = packs[sidx]
            p.wait("gpsimd", ve)
            p.wait("gpsimd", st)
            cvv = p.dma("gpsimd", ctxs[:, sidx * c.HPC * 8:(sidx + 1) * c.HPC * 8],
                        self.da(spn)[0:1, 0:nctx_ctx2], st)
            idxrow = self.sb[f"idxr{sidx}"].ap()[0:1, :c.HPC * 8]
            p.wait("sync", st)
            iv2 = p.dma("sync", idxrow,
                        self.da(spn)[0:1, nctx_ctx2:nctx_ctx2 + c.HPC * 8], ld)
            p.wait("vector", ld, iv2)
            ltf = ltrow_f[0:1, sidx * c.HPC * 8:(sidx + 1) * c.HPC * 8]

            def mklt(e, idxrow=idxrow, ltf=ltf):
                toff = self.sb["crole"].ap()[0:1, 1:2]
                return e.scalar_tensor_tensor(
                    ltf, idxrow, 0.0, toff.to_broadcast([1, c.HPC * 8]),
                    ALU.add, ALU.subtract)
            vev = p.run("vector", mklt, ve)
            if c.U < 8:
                def drop(e, ltf=ltf):
                    r = None
                    for h in range(c.HPC):
                        r = e.memset(ltf[0:1, h * 8 + c.U:h * 8 + 8], -1.0)
                    return r
                vev = p.run("vector", drop, ve)
        psltc = self.psB(1024, 1025, rows=NIT)
        eye = self.sb["consts"].ap()
        p.wait("tensor", ve, vev)
        pev = p.run("tensor", lambda e, psltc=psltc, ltrow_f=ltrow_f, eye=eye:
                    e.transpose(psltc, ltrow_f, eye[0:1, 0:1]), pe)
        ltcol = self.sb["sm"].ap()[:NIT, 54:55]
        p.wait("vector", pe, pev)
        p.wait("vector", st)
        vev = p.run("vector", lambda e, ltcol=ltcol, psltc=psltc:
                    e.tensor_copy(ltcol, psltc), ve)

        def pcols(ci):
            return self.sb[f"T{2 + ci}"].ap()[:, :NIT]

        def pcolsT(ci):
            return self.sb["rows"].ap()[0:NIT, 5120 + ci * P:5120 + (ci + 1) * P]
        def memz(e):
            r = None
            for ci in range(c.DC):
                r = e.memset(pcols(ci), 0.0)
            return r
        vev2 = p.run("vector", memz, ve)

        def asm(e):
            r = None
            for sidx in range(2):
                for h in range(c.HPC):
                    ghead = sidx * c.HPC + h
                    drow = ghead * c.DK
                    ci, bs = drow // P, drow % P
                    cols = slice((sidx * c.HPC + h) * 8, (sidx * c.HPC + h) * 8 + 8)
                    r = e.tensor_copy(pcols(ci)[bs:bs + c.DK, cols],
                                      ctxs[:, cols])
            return r
        vev = p.run("vector", asm, ve)
        p.wait("tensor", ve, vev)
        for ci in range(c.DC):
            pst = self.psB(0, P, rows=NIT)
            pev = p.run("tensor", lambda e, pst=pst, ci=ci, eye=eye:
                        e.transpose(pst, pcols(ci), eye), pe)
            p.wait("vector", pe, pev)
            vev = p.run("vector", lambda e, pst=pst, ci=ci:
                        e.tensor_copy(pcolsT(ci), pst), ve)
            p.wait("tensor", ve, vev)
        iota = self.da("iotarow")
        SLW = min(512, c.TOK)
        iosb = self.sb["T0"].ap()[0:1, :SLW]
        for sl in range(c.TOK // SLW):
            ls, lw = sl * SLW, SLW
            lio = p.dma("sync", iosb, iota[0:1, ls:ls + lw], ld)
            psio = self.psB(512, 512 + lw)
            p.wait("tensor", ld, lio)
            p.wait("tensor", ve)
            pev = p.run("tensor", lambda e, psio=psio, iosb=iosb: e.matmul(
                psio, self.sb["conesrow"].ap(), iosb, start=True, stop=True), pe)
            oh = self.sb["qg"].ap()[:, :lw]
            p.wait("vector", pe, pev)
            vev = p.run("vector", lambda e, oh=oh, psio=psio, ltcol=ltcol:
                        e.tensor_scalar(oh[:NIT, :], psio[:NIT, :], ltcol, None,
                                        ALU.is_equal), ve)
            p.wait("tensor", ve, vev)
            for ci in range(c.DC):
                psp = self.psA(0, lw) if ci % 2 == 0 else self.psA(512, 512 + lw)
                p.wait("tensor", se)
                pev = p.run("tensor", lambda e, psp=psp, ci=ci, oh=oh: e.matmul(
                    psp, pcolsT(ci)[:NIT, :], oh[:NIT, :], start=True,
                    stop=True), pe)
                dst = self.pslot(ci)[:, ls:ls + lw]
                p.wait("scalar", pe, pev)
                sev = p.run("scalar", lambda e, dst=dst, psp=psp:
                            e.copy(dst, psp), se)
        p.wait("tensor", se, sev)
        wbufs = [self.sb["WB0"].ap(), self.sb["WB1"].ap()]
        p.wait("sync", ve, vev)
        for tt in range(c.NTT):
            lo, hi = tt * c.TT, (tt + 1) * c.TT
            wld = [None, None]
            p.wait("sync", pe)
            wld[0] = self.colslice_load("woT", 0, wbufs[0], c.DC, ld)
            if c.DC > 1:
                wld[1] = self.colslice_load("woT", 1, wbufs[1], c.DC, ld)
            for oc in range(c.DC):
                b = oc % 2
                ps = self.psA(0, c.TT) if b == 0 else self.psA(512, 512 + c.TT)
                p.wait("tensor", ld, wld[b])
                p.wait("tensor", ve)
                pev = None
                for kc in range(c.DC):
                    lhs = wbufs[b][:, kc * P:(kc + 1) * P]
                    rhs = self.pslot(kc)[:, lo:hi]
                    pev = p.run("tensor", lambda e, lhs=lhs, rhs=rhs, ps=ps,
                                kc=kc: e.matmul(ps, lhs, rhs, start=(kc == 0),
                                                stop=(kc == c.DC - 1)), pe)
                if oc + 2 < c.DC:
                    p.wait("sync", pe, pev)
                    wld[b] = self.colslice_load("woT", oc + 2, wbufs[b], c.DC, ld)
                at = self.sb[f"acc{oc}"].ap()[:, lo:hi]
                bo = self.da("bo_t")[:, oc:oc + 1]
                bos = self.sb["sm"].ap()[:, 50:51]
                p.wait("sync", ve)
                lb = p.dma("sync", bos, bo, ld, slow=True)
                p.wait("vector", pe, pev)
                p.wait("vector", ld, lb)
                vev = p.run("vector", lambda e, at=at, ps=ps, bos=bos:
                            e.scalar_tensor_tensor(at, ps, bos, at, ALU.add,
                                                   ALU.add), ve)

    def ph_l1_ffn(self):
        c, p = self.cfg, self.p
        ld, pe, ve, se, st, cc = (p.sem(n) for n in
                                  ("ld", "pe", "ve", "se", "st", "cc"))
        C0 = self.sb["C0"].ap()
        C1 = self.sb["C1"].ap()
        nctx_ctx = c.DK * c.HPC * 8
        selr = self.sb["selr1"].ap()[0:1, 0:1]
        p.wait("sync", st)
        lv = p.dma("sync", selr,
                   self.da("pp1_sp")[0:1, nctx_ctx + c.HPC * 8:
                                     nctx_ctx + c.HPC * 8 + 1], ld)
        p.wait("tensor", ld, lv)
        if DEBUG_FORCE_SEL0:
            p.wait("vector", ld, lv)
            p.run("vector", lambda e: e.memset(selr, 0.0), p.sems["ve"])
            p.wait("tensor", p.sems["ve"])
        from concourse.bass_core_rust import IndirectOffsetOnAxis as IOA
        orow = self.sb["conesrow"].ap()
        selps = self.psB(1544, 1545)
        pev = p.run("tensor", lambda e: e.matmul(selps, orow, selr,
                                                 start=True, stop=True), pe)
        selc = self.sb["sm"].ap()[:, 60:61]
        idxf = self.sb["sm"].ap()[:, 61:62]
        gidx = self.sb["gidxc"].ap()
        iop = self.sb["ciotaT"].ap()[:, 0:1]
        p.wait("vector", pe, pev)
        vev = p.run("vector", lambda e: e.tensor_copy(selc, selps), ve)

        W1f = self.da("W1").rearrange("e d f -> (e d) f")
        W2f = self.da("W2").rearrange("e f d -> (e f) d")
        b1f = self.da("b1_t").rearrange("e p f -> (e p) f")
        b2f = self.da("b2_t").rearrange("e p d -> (e p) d")
        wb0 = self.sb["WB0"].ap()
        b1s = self.sb["bffn"].ap()[:, :c.FC]
        b2s = self.sb["bffn2"].ap()[:, :]
        p.wait("gpsimd", pe)
        p.wait("gpsimd", se)
        p.wait("gpsimd", ld)

        def gather_rows(srcf, nr, kc, dst_sb):
            def mkidx(e, kc=kc, nr=nr):
                e.drain()
                e.tensor_scalar(idxf, selc, float(nr), float(kc * P),
                                ALU.mult, ALU.add)
                e.drain()
                e.tensor_tensor(idxf, idxf, iop, ALU.add)
                e.drain()
                return e.tensor_copy(gidx, idxf)
            mv = p.run("vector", mkidx, ve)
            p.wait("gpsimd", ve, mv)
            gsv = p.run("gpsimd", lambda e, dst_sb=dst_sb, srcf=srcf:
                        e.indirect_dma_start(dst_sb, None, srcf,
                                             IOA(gidx, 0)), st, k=16)
            gv = p.run("gpsimd", lambda e: e.drain(), p.sem("gp"))
            p.wait("vector", p.sem("gp"), gv)
            p.wait("vector", st, gsv)
            return (gv, gsv)

        vw1 = vw2 = None
        for (srcf, nrch, width, spname) in ((W1f, c.DC, c.F, "w1_sp"),
                                            (W2f, c.FC, c.D, "w2_sp")):
            for kc in range(nrch):
                gv, gsv = gather_rows(srcf, nrch * P, kc, wb0[:, :width])
                p.wait("sync", p.sem("gp"), gv)
                p.wait("sync", st, gsv)
                lv = p.dma("sync", self.da(spname)[kc * P:(kc + 1) * P, :],
                           wb0[:, :width], ld)
                p.wait("gpsimd", ld, lv)
            if spname == "w1_sp":
                vw1 = lv
            else:
                vw2 = lv
        v1, v1s = gather_rows(b1f, P, 0, b1s)
        v2, v2s = gather_rows(b2f, P, 0, b2s)
        if self.debug_stop == "l1f1":
            return
        wbufs = [self.sb["WB0"].ap(), self.sb["WB1"].ap()]
        arenas = [self.sb["C1"].ap(), self.sb["bigA0"].ap(),
                  self.sb["bigA1"].ap()]
        per = (2 * c.TOK) // c.TT

        def hslot(i):
            a, sidx = divmod(i, per)
            return arenas[a][:, sidx * c.TT:(sidx + 1) * c.TT]

        for tt in range(c.NTT):
            if self.debug_stop == f"l1f2_{tt}":
                return
            lo, hi = tt * c.TT, (tt + 1) * c.TT
            p.wait("sync", pe)
            p.wait("sync", ve)
            xl = None
            for i in range(c.DC):
                xl = p.dma("sync", C0[:, i * c.TT:(i + 1) * c.TT],
                           self.da("xn1_sp")[i * P:(i + 1) * P, lo:hi], ld)
            for oc in range(c.FC):
                b = oc % 2
                p.wait("sync", pe)
                if tt == 0 and oc == 0:
                    p.wait("sync", ld, vw1)
                wl = self.colslice_load("w1_sp", oc, wbufs[b], c.DC, ld)
                ps = self.psA(0, c.TT) if b == 0 else self.psA(512, 512 + c.TT)
                p.wait("tensor", ld, wl)
                p.wait("tensor", ld, xl)
                p.wait("tensor", se)
                p.wait("tensor", ve)
                pev = None
                for kc in range(c.DC):
                    lhs = wbufs[b][:, kc * P:(kc + 1) * P]
                    rhs = C0[:, kc * c.TT:(kc + 1) * c.TT]
                    pev = p.run("tensor", lambda e, lhs=lhs, rhs=rhs, ps=ps,
                                kc=kc: e.matmul(ps, lhs, rhs, start=(kc == 0),
                                                stop=(kc == c.DC - 1)), pe)
                dst = hslot(oc)
                p.wait("scalar", pe, pev)
                p.wait("scalar", p.sem("gp"), v1)
                if tt == 0 and oc == 0:
                    p.wait("scalar", st, v1s)
                sev = p.run("scalar", lambda e, dst=dst, ps=ps, oc=oc, b1s=b1s:
                            e.activation(dst, ps, ACT.Gelu,
                                         bias=b1s[:, oc:oc + 1]), se)
            rbase = c.FC

            def rslot(i):
                return hslot(rbase + i)
            for oc in range(c.DC):
                b = oc % 2
                p.wait("sync", pe)
                if tt == 0 and oc == 0:
                    p.wait("sync", ld, vw2)
                wl = self.colslice_load("w2_sp", oc, wbufs[b], c.FC, ld)
                ps = self.psB(0, c.TT) if b == 0 else self.psB(512, 512 + c.TT)
                p.wait("tensor", ld, wl)
                p.wait("tensor", se)
                pev = None
                for kc in range(c.FC):
                    lhs = wbufs[b][:, kc * P:(kc + 1) * P]
                    pev = p.run("tensor", lambda e, lhs=lhs, kc=kc, ps=ps:
                                e.matmul(ps, lhs, hslot(kc), start=(kc == 0),
                                         stop=(kc == c.FC - 1)), pe)
                    dst = rslot(oc)
                p.wait("scalar", pe, pev)
                p.wait("scalar", p.sem("gp"), v2)
                if tt == 0 and oc == 0:
                    p.wait("scalar", st, v2s)
                sev = p.run("scalar", lambda e, dst=dst, ps=ps, oc=oc, b2s=b2s:
                            e.add(dst, ps, b2s[:, oc:oc + 1]), se)
            wld = [None, None]
            p.wait("sync", pe)
            p.wait("sync", se)
            wld[0] = self.colslice_load("Wo1", 0, wbufs[0], c.DC, ld)
            if c.DC > 1:
                wld[1] = self.colslice_load("Wo1", 1, wbufs[1], c.DC, ld)
            for oc in range(c.DC):
                b = oc % 2
                ps = self.psA(0, c.TT) if b == 0 else self.psA(512, 512 + c.TT)
                p.wait("tensor", ld, wld[b])
                p.wait("tensor", ve)
                p.wait("tensor", se)
                pev = None
                for kc in range(c.DC):
                    lhs = wbufs[b][:, kc * P:(kc + 1) * P]
                    pev = p.run("tensor", lambda e, lhs=lhs, kc=kc, ps=ps:
                                e.matmul(ps, lhs, rslot(kc), start=(kc == 0),
                                         stop=(kc == c.DC - 1)), pe)
                if oc + 2 < c.DC:
                    p.wait("sync", pe, pev)
                    wld[b] = self.colslice_load("Wo1", oc + 2, wbufs[b], c.DC, ld)
                at = self.sb[f"acc{oc}"].ap()[:, lo:hi]
                p.wait("vector", pe, pev)
                vev = p.run("vector", lambda e, at=at, ps=ps:
                            e.scalar_tensor_tensor(at, ps, 0.3, at, ALU.mult,
                                                   ALU.add), ve)
        for i in range(c.DC):
            bos = self.sb["sm"].ap()[:, 51:52]
            p.wait("sync", ve)
            lb = p.dma("sync", bos, self.da("bo1_t")[:, i:i + 1], ld, slow=True)
            at = self.sb[f"acc{i}"].ap()
            p.wait("vector", ld, lb)
            p.wait("vector", pe)

            def bo1add(e, at=at, bos=bos):
                e.tensor_scalar(self.sb["sm"].ap()[:, 52:53], bos, 0.3, None,
                                ALU.mult)
                e.drain()
                return e.tensor_scalar(at[:, :], at[:, :],
                                       self.sb["sm"].ap()[:, 52:53], None,
                                       ALU.add)
            vev = p.run("vector", bo1add, ve)

    def fin_or_dump(self):
        if getattr(self, "_dump", None):
            return self.finish_dump(self._dump)
        return self.finish()

    def finish_dump(self, spec):
        c, p = self.cfg, self.p
        st = p.sem("st")
        p.wait("gpsimd", p.sem("cc"))
        for eng, sname in (("gpsimd", "ve"), ("gpsimd", "se"), ("gpsimd", "pe")):
            p.wait(eng, p.sem(sname))
        outrow = 0
        v = None
        for part in spec.split(","):
            if "@" in part:
                name, rr = part.split("@")
                lo, hi = (int(t) for t in rr.split(":"))
            else:
                name, lo, hi = part, 0, None
            src = self.da(name)
            nrow, ncol = src.shape[0], src.shape[1]
            if hi is None:
                hi = nrow
            per = (ncol + c.TOK - 1) // c.TOK
            for r in range(lo, hi):
                for j in range(per):
                    w = min(c.TOK, ncol - j * c.TOK)
                    v = p.dma("gpsimd",
                              self.da("out_fm")[outrow:outrow + 1, :w],
                              src[r:r + 1, j * c.TOK:j * c.TOK + w], st)
                    outrow += 1
        assert outrow <= c.D, f"dump too big: {outrow}"
        p.wait("gpsimd", st, v)
        self.p.emit()
        return self.nc

    def finish(self):
        c, p = self.cfg, self.p
        st = p.sem("st")
        for eng, s in (("gpsimd", "ve"), ("gpsimd", "se"), ("gpsimd", "pe")):
            p.wait(eng, p.sem(s))
        v = None
        for i in range(c.DC):
            v = p.dma("gpsimd", self.da("out_fm")[i * P:(i + 1) * P, :],
                      self.sb[f"acc{i}"].ap()[:, :], st)
        p.wait("gpsimd", st, v)
        self.p.emit()
        return self.nc



def host_inmaps(cfg: Cfg, inputs: dict) -> list:
    c = cfg
    f32 = lambda a: np.ascontiguousarray(a, dtype=np.float32)
    tr = lambda a: f32(np.asarray(a).T)
    nat = lambda a: f32(np.asarray(a))

    shared = {
        "wqkvT": tr(inputs["L0_attn_w_qkv"]),
        "wswT": tr(inputs["L0_attn_w_swiglu"]),
        "woutT": tr(inputs["L0_attn_w_out"]),
        "wupT": tr(inputs["L0_ffn_w_up"]),
        "wgateT": tr(inputs["L0_ffn_w_gate"]),
        "wdownT": tr(inputs["L0_ffn_w_down"]),
        "wup_n": nat(inputs["L0_ffn_w_up"]),
        "wgate_n": nat(inputs["L0_ffn_w_gate"]),
        "wdown_n": nat(inputs["L0_ffn_w_down"]),
        "woT": tr(inputs["L1_attn_wo"]),
        "bo_t": f32(np.asarray(inputs["L1_attn_bo"]).reshape(c.DC, P).T),
        "Wp1": nat(inputs["L1_ffn_Wp1"]),
        "bp1": f32(np.asarray(inputs["L1_ffn_bp1"]).reshape(c.NAV, 1)),
        "Wp2": nat(inputs["L1_ffn_Wp2"]),
        "bp2": f32(np.asarray(inputs["L1_ffn_bp2"]).reshape(c.E, 1)),
        "W1": nat(inputs["L1_ffn_W1"]),
        "b1_t": f32(np.asarray(inputs["L1_ffn_b1"]).reshape(c.E, c.FC, P)
                    .transpose(0, 2, 1)),
        "W2": nat(inputs["L1_ffn_W2"]),
        "b2_t": f32(np.asarray(inputs["L1_ffn_b2"]).reshape(c.E, c.DC, P)
                    .transpose(0, 2, 1)),
        "Wo1": nat(inputs["L1_ffn_Wo"]),
        "bo1_t": f32(np.asarray(inputs["L1_ffn_bo"]).reshape(c.DC, P).T),
        "ones": np.ones((P, 1), np.float32),
        "onesrow": np.ones((1, P), np.float32),
        "eye": np.eye(P, dtype=np.float32),
        "iotaT": f32(np.arange(c.T, dtype=np.float32).reshape(c.LC, P).T),
        "iotarow": f32(np.arange(c.TOK, dtype=np.float32).reshape(1, c.TOK)),
    }

    x = np.asarray(inputs["L0_attn_x"])
    wqT = tr(inputs["L1_attn_wq"])
    wkT = tr(inputs["L1_attn_wk"])
    wvT = tr(inputs["L1_attn_wv"])
    bq = np.asarray(inputs["L1_attn_bq"], dtype=np.float32)
    bk = np.asarray(inputs["L1_attn_bk"], dtype=np.float32)
    bv = np.asarray(inputs["L1_attn_bv"], dtype=np.float32)
    MH = c.HPC * c.DK
    maps = []
    for core in range(8):
        b, h = core // 2, core % 2
        m = dict(shared)
        m["x_fm"] = f32(x[b, h * c.TOK:(h + 1) * c.TOK, :].T)
        hsl = slice(h * MH, (h + 1) * MH)
        m["wqT_my"] = f32(wqT[:, hsl])
        m["wkT_my"] = f32(wkT[:, hsl])
        m["wvT_my"] = f32(wvT[:, hsl])
        m["bq_my_t"] = f32(bq[hsl].reshape(MH // P, P).T)
        m["bk_my_t"] = f32(bk[hsl].reshape(MH // P, P).T)
        m["bv_my_row"] = f32(bv[hsl].reshape(1, MH))
        m["bq_my_row"] = f32(bq[hsl].reshape(1, MH))
        pc = np.zeros((8, 1), np.float32)
        if h == 1:
            pc[2 * b, 0] = 1.0
        psel = np.zeros((8, 1), np.float32)
        psel[2 * b + 1, 0] = 1.0
        pe_ = np.zeros((8, 1), np.float32)
        pe_[2 * b, 0] = 1.0
        po_ = np.zeros((8, 1), np.float32)
        po_[2 * b + 1, 0] = 1.0
        m["pick_carry"] = pc
        m["pick_sel"] = psel
        m["pick_even"] = pe_
        m["pick_odd"] = po_
        m["role"] = np.array([[float(h), float(h * c.TOK), float(core),
                               float(core ^ 1), 0, 0, 0, 0]], np.float32)
        maps.append(m)
    return maps


def host_gather(cfg: Cfg, results) -> np.ndarray:
    c = cfg
    out = np.empty((4, c.T, c.D), np.float32)
    for core in range(8):
        b, h = core // 2, core % 2
        out[b, h * c.TOK:(h + 1) * c.TOK, :] = results[core]["out_fm"].T
    return out


def build_nc(cfg: Cfg = CFG, debug_stop=None):
    m1 = MK(cfg, debug_stop=debug_stop)
    m1.build()
    m2 = MK(cfg, debug_stop=debug_stop, waited=m1.p.rec_waited)
    return m2.build()


def kernel(**inputs):
    cfg = CFG
    nc = build_nc(cfg)
    in_maps = host_inmaps(cfg, inputs)
    res = run_bass_kernel_spmd(nc, in_maps, core_ids=list(range(8)))
    return host_gather(cfg, res.results)


# version 05
# speedup vs baseline: 1.0395x; 1.0395x over previous
import math
from dataclasses import dataclass

import numpy as np

import concourse.bass as bass
import concourse.mybir as mybir
from concourse.bass_utils import run_bass_kernel_spmd

F32 = mybir.dt.float32
U32 = mybir.dt.uint32
ALU = mybir.AluOpType
ACT = mybir.ActivationFunctionType
AX = mybir.AxisListType

P = 128
DEBUG_FORCE_SEL0 = False
DEBUG_RACES = False
DEBUG_DRAIN_ALL = False


class _DrainEng:

    def __init__(self, e):
        self._e = e

    def __getattr__(self, n):
        a = getattr(self._e, n)
        if not callable(a) or n in ("drain", "wait_ge"):
            return a

        def w(*args, **kw):
            r = a(*args, **kw)
            self._e.drain()
            return r
        return w


@dataclass(frozen=True)
class Cfg:
    D: int = 1024
    TOK: int = 2048
    TT: int = 512
    H: int = 16
    DK: int = 64
    E: int = 8
    NAV: int = 128
    QBLK: int = 32

    @property
    def DC(self):
        return self.D // P

    @property
    def T(self):
        return 2 * self.TOK

    @property
    def FI(self):
        return 3 * self.D

    @property
    def FIC(self):
        return self.FI // P

    @property
    def SW(self):
        return 2 * self.D

    @property
    def SWC(self):
        return self.SW // P

    @property
    def F(self):
        return 2 * self.D

    @property
    def FC(self):
        return self.F // P

    @property
    def NTT(self):
        return self.TOK // self.TT

    @property
    def HPC(self):
        return self.H // 2

    @property
    def U(self):
        return min(self.T, max(1, int(math.log(self.T))))

    @property
    def KN(self):
        return min(512, self.T)

    @property
    def NKT(self):
        return self.T // self.KN

    @property
    def LC(self):
        return self.T // P


CFG = Cfg()


class Sem:
    def __init__(self, h, name=""):
        self.h = h
        self.name = name
        self.n = 0
        self.is_dma = False
        self.guarded = {}

    def bump(self, k):
        if k == 16:
            self.is_dma = True
        self.n += k
        return self.n


class Prog:
    ENGINES = ("sync", "tensor", "vector", "scalar", "gpsimd")

    def __init__(self, nc, waited=None):
        self.nc = nc
        self.ops = {e: [] for e in self.ENGINES}
        self._sem_cms = []
        self.sems = {}
        self.rec_waited = {}
        self.waited = waited or {}

    def sem(self, name) -> Sem:
        if name not in self.sems:
            cm = self.nc.semaphore(name)
            h = cm.__enter__()
            self._sem_cms.append(cm)
            self.sems[name] = Sem(h, name)
        return self.sems[name]

    def on(self, engine, fn):
        self.ops[engine].append(fn)

    def wait(self, engine, sem: Sem, val=None):
        v = sem.n if val is None else val
        if v > 0:
            if sem.is_dma:
                self.rec_waited.setdefault(sem.name, set()).add(v)
            self.on(engine, lambda e: e.wait_ge(sem.h, v))

    def _dma_guard(self, engine, sem: Sem):
        w = sem.n
        if (w in self.waited.get(sem.name, ()) and
                w > sem.guarded.get(engine, 0)):
            self.on(engine, lambda e: e.wait_ge(sem.h, w))
            sem.guarded[engine] = w

    def dma(self, engine, out, in_, sem: Sem, slow=False):
        self._dma_guard(engine, sem)
        v = sem.bump(16)
        if slow:
            self.on(engine, lambda e: e.dma_start(
                out=out, in_=in_, allow_slow_non_contiguous=True).then_inc(sem.h, 16))
        else:
            self.on(engine, lambda e: e.dma_start(out=out, in_=in_).then_inc(sem.h, 16))
        return v

    def dma_lazy(self, engine, out, in_fn, sem: Sem):
        self._dma_guard(engine, sem)
        v = sem.bump(16)
        self.on(engine, lambda e: e.dma_start(out=out, in_=in_fn()).then_inc(
            sem.h, 16))
        return v

    def run(self, engine, fn, sem: Sem = None, k=1):
        if DEBUG_DRAIN_ALL and engine in ("vector", "scalar"):
            inner = fn
            fn = lambda e, _f=inner: _f(_DrainEng(e))
        if sem is None:
            self.on(engine, fn)
            return None
        if k == 16:
            self._dma_guard(engine, sem)
        v = sem.bump(k)
        self.on(engine, lambda e: fn(e).then_inc(sem.h, k))
        return v

    def emit(self):
        with self.nc.Block() as blk:
            for name in self.ENGINES:
                ops = self.ops[name]

                def make(ops):
                    def body(eng):
                        for f in ops:
                            f(eng)
                    return body

                getattr(blk, name)(make(ops))


class MK:
    def __init__(self, cfg: Cfg, debug_stop=None, waited=None):
        self.cfg = cfg
        self.debug_stop = debug_stop
        self.nc = bass.Bass("TRN2", target_bir_lowering=False,
                            detect_race_conditions=DEBUG_RACES)
        self.p = Prog(self.nc, waited=waited)
        self.dram = {}
        self.sb = {}
        self.ps = {}
        self._declare_io()
        self._alloc_onchip()

    def dr(self, name, shape, dtype=F32, kind="Internal"):
        t = self.nc.dram_tensor(name, list(shape), dtype, kind=kind)
        self.dram[name] = t
        return t

    def da(self, name):
        return self.dram[name].ap()

    def sbuf(self, name, shape, dtype=F32):
        t = self.nc.alloc_sbuf_tensor(name, list(shape), dtype)
        self.sb[name] = t
        return t

    def _declare_io(self):
        c = self.cfg
        D, TOK, FI = c.D, c.TOK, c.FI
        inp = lambda n, s, dt=F32: self.dr(n, s, dt, kind="ExternalInput")
        inp("x_fm", [D, TOK])
        self.dr("out_fm", [D, TOK], kind="ExternalOutput")
        inp("wqkvT", [D, 3 * D])
        inp("wswT", [D, c.SW])
        inp("woutT", [D, D])
        inp("wupT", [D, FI])
        inp("wgateT", [D, FI])
        inp("wdownT", [FI, D])
        inp("wup_n", [FI, D])
        inp("wgate_n", [FI, D])
        inp("wdown_n", [D, FI])
        inp("woT", [D, D])
        inp("bo_t", [P, c.DC])
        MH = c.HPC * c.DK
        inp("wqT_my", [D, MH])
        inp("wkT_my", [D, MH])
        inp("wvT_my", [D, MH])
        inp("bq_my_t", [P, MH // P])
        inp("bk_my_t", [P, MH // P])
        inp("bv_my_row", [1, MH])
        inp("bq_my_row", [1, MH])
        inp("pick_even", [8, 1])
        inp("pick_odd", [8, 1])
        inp("Wp1", [D, c.NAV])
        inp("bp1", [c.NAV, 1])
        inp("Wp2", [c.NAV, c.E])
        inp("bp2", [c.E, 1])
        inp("W1", [c.E, D, c.F])
        inp("b1_t", [c.E, P, c.FC])
        inp("W2", [c.E, c.F, D])
        inp("b2_t", [c.E, P, c.DC])
        inp("Wo1", [D, D])
        inp("bo1_t", [P, c.DC])
        inp("ones", [P, 1])
        inp("onesrow", [1, P])
        inp("eye", [P, P])
        inp("pick_carry", [8, 1])
        inp("pick_sel", [8, 1])
        inp("role", [1, 8])
        inp("iotaT", [P, c.LC])
        inp("iotarow", [1, TOK])

        self.dr("xn_sp", [D, TOK])
        self.dr("xh_sp", [D, TOK])
        self.dr("xq_sp", [D, TOK])
        self.dr("wupdq_sp", [D, FI])
        self.dr("wgatedq_sp", [D, FI])
        self.dr("v_sp", [2 * TOK, c.HPC * c.DK])
        self.dr("w1_sp", [D, c.F])
        self.dr("w2_sp", [c.F, D])
        self.dr("pp0_sp", [1, self._ctx_pack_len()])
        self.dr("pp1_sp", [1, self._ctx_pack_len()])
        self.dr("qkv_sp", [3 * D, TOK])
        self.dr("gate_sp", [D, TOK])
        self.dr("w_sp", [D, TOK])
        self.dr("kv_sp", [D, TOK])
        self.dr("upraw_sp", [FI, TOK])
        self.dr("gateraw_sp", [FI, TOK])
        self.dr("oraw_sp", [FI, TOK])
        self.dr("wdowndq_sp", [FI, D])
        self.dr("xn1_sp", [D, TOK])
        self.dr("qtm_sp", [(c.HPC // min(2, c.HPC)) * 2 * TOK, min(2, c.HPC) * c.DK])
        self.dr("dbg_y", [D, TOK])
        self.dr("dbg_z", [c.SW, TOK])
        self.dr("dbg_u", [D, TOK])
        self.dr("dbg_s", [D, TOK])
        for nm in ("max1", "max23", "max4", "max5"):
            self.dr(f"cc_{nm}_in", [1, 8])
            self.dr(f"cc_{nm}_out", [1, 8])
        self.dr("cc_carry_in", [1, 2 * D])
        self.dr("cc_carry_out", [8, 2 * D])
        self.dr("cc_xn1_in", [D, TOK])
        self.dr("cc_xn1_out", [2 * D, TOK])
        nctx = self._ctx_pack_len()
        self.dr("cc_ctx_in", [1, nctx])
        self.dr("cc_ctx_out", [8, nctx])

    def _ctx_pack_len(self):
        c = self.cfg
        n = c.HPC * 8 * c.DK + c.HPC * 8 + 8
        return ((n + 511) // 512) * 512

    def _alloc_onchip(self):
        c = self.cfg
        for i in range(c.DC):
            self.sbuf(f"acc{i}", [P, c.TOK])
        for i in range(2):
            self.sbuf(f"bigA{i}", [P, 2 * c.TOK])
        for i in range(2):
            self.sbuf(f"C{i}", [P, 2 * c.TOK])
        for i in range(2):
            self.sbuf(f"WB{i}", [P, min(2048, c.FI)])
        for i in range(10):
            self.sbuf(f"T{i}", [P, max(512, c.TT)])
        RW = max(6656, 2 * c.TOK + 512, self._ctx_pack_len() + 2048)
        self.sbuf("rows", [P, RW])
        self.RW = RW
        self.sbuf("sm", [P, 64])
        self.sbuf("selv", [c.HPC, 8])
        self.sbuf("seli", [c.HPC, 8], U32)
        self.sbuf("consts", [P, P])
        self.sbuf("cones", [P, 1])
        self.sbuf("conesrow", [1, P])
        self.sbuf("cpick", [8, 4])
        self.sbuf("crole", [1, 8])
        self.sbuf("ciotaT", [P, c.LC])
        self.sbuf("qg", [P, 512])
        self.sbuf("gidxc", [P, 1], mybir.dt.int32)
        self.sbuf("qsel", [P, 8])
        self.sbuf("expb", [P, c.LC * 8])
        self.sbuf("ctxb", [c.DK, c.HPC * 8])
        self.sbuf("selrow", [1, 4 * c.HPC * 8], mybir.dt.int32)
        self.sbuf("ctxs", [c.DK, 2 * c.HPC * 8])
        self.sbuf("bffn", [P, c.FC])
        self.sbuf("bffn2", [P, c.DC])
        self.sbuf("qint", [P, 512], mybir.dt.int32)
        self.sbuf("chalf", [P, 1])
        self.sbuf("idxr0", [1, 64])
        self.sbuf("idxr1", [1, 64])
        self.sbuf("selr1", [1, 4])
        self.sbuf("s32g0", [c.HPC, 8])
        self.sbuf("csum", [P, 2 * c.DC])
        self.sbuf("csum2", [P, 4 * c.DC])
        self.ps["A"] = self.nc.alloc_psum_tensor("psA", [P, 2048], F32)
        self.ps["B"] = self.nc.alloc_psum_tensor("psB", [P, 2048], F32)

    def row(self, idx, lo, hi):
        return self.sb["rows"].ap()[idx:idx + 1, lo:hi]

    def rows2(self, r0, r1, lo, hi):
        return self.sb["rows"].ap()[r0:r1, lo:hi]

    def bigA(self, chunk, lo=0, hi=None):
        c = self.cfg
        t = self.sb[f"bigA{chunk // 2}"]
        off = (chunk % 2) * c.TOK
        hi = c.TOK if hi is None else hi
        return t[:, off + lo: off + hi]

    def tslot(self, j, w=None):
        w = (self.cfg.TT if w is None else w)
        return self.sb[f"T{2 + j}"].ap()[:, :w]

    def aslot(self, j, w=None):
        c = self.cfg
        w = c.TT if w is None else w
        spt = (2 * c.TOK) // w
        return self.sb[f"bigA{j // spt}"].ap()[:, (j % spt) * w:(j % spt + 1) * w]

    def cslot(self, which, j, w=None):
        c = self.cfg
        w = c.TT if w is None else w
        return self.sb[f"C{which}"].ap()[:, j * w:(j + 1) * w]

    def oslot(self, j, w=None):
        c = self.cfg
        w = c.TT if w is None else w
        spt = (2 * c.TOK) // w
        ar = [self.sb["C0"], self.sb["C1"], self.sb["bigA0"], self.sb["bigA1"]]
        return ar[j // spt].ap()[:, (j % spt) * w:(j % spt + 1) * w]

    def pslot(self, ch):
        c = self.cfg
        ar = [self.sb["C0"], self.sb["C1"], self.sb["bigA0"], self.sb["bigA1"]]
        return ar[ch // 2].ap()[:, (ch % 2) * c.TOK:(ch % 2 + 1) * c.TOK]

    def psA(self, lo, hi, rows=P):
        return self.ps["A"].ap()[:rows, lo:hi]

    def psB(self, lo, hi, rows=P):
        return self.ps["B"].ap()[:rows, lo:hi]

    def build(self):
        ds = self.debug_stop
        self._dump = None
        if ds and "+dump:" in ds:
            ds, self._dump = ds.split("+dump:")
            self.debug_stop = ds or None
        self.ph_consts()
        self.ph_p0()
        if self.debug_stop == "p0":
            return self.fin_or_dump()
        self.ph_l0_attn_A()
        self.ph_l0_attn_A2()
        if self.debug_stop == "a2":
            return self.fin_or_dump()
        self.ph_l0_ffn_F1()
        self.ph_l0_attn_B()
        if self.debug_stop == "b":
            return self.fin_or_dump()
        self.ph_l0_ffn_F23()
        self.ph_l0_ffn_F4()
        if self.debug_stop == "l0":
            return self.fin_or_dump()
        self.ph_l1_norm()
        self.ph_l1_attn()
        if self.debug_stop == "dumppack":
            return self.finish_dump("cc_ctx_out")
        if self.debug_stop is not None and self.debug_stop.startswith("l1a"):
            return self.fin_or_dump()
        self.ph_l1_ffn()
        return self.fin_or_dump()

    def ph_consts(self):
        p, d = self.p, self.dram
        ld = p.sem("ld")
        p.dma("sync", self.sb["consts"].ap()[:, :], self.da("eye")[:, :], ld)
        p.dma("sync", self.sb["cones"].ap()[:, :], self.da("ones")[:, :], ld)
        p.dma("sync", self.sb["conesrow"].ap()[:, :], self.da("onesrow")[:, :], ld)
        cp = self.sb["rows"].ap()[32:40, self.RW - 8:self.RW - 4]
        p.dma("sync", cp[:, 0:1], self.da("pick_carry")[:, :], ld)
        p.dma("sync", cp[:, 1:2], self.da("pick_sel")[:, :], ld)
        p.dma("sync", cp[:, 2:3], self.da("pick_even")[:, :], ld)
        p.dma("sync", cp[:, 3:4], self.da("pick_odd")[:, :], ld)
        p.dma("sync", self.sb["crole"].ap()[:, :], self.da("role")[:, :], ld)
        p.dma("sync", self.sb["ciotaT"].ap()[:, :], self.da("iotaT")[:, :], ld)
        ch = self.sb["chalf"].ap()
        qi1 = self.sb["qint"].ap()[:, 0:1]

        def mkch(e, ch=ch, qi1=qi1):
            e.memset(ch, 0.75)
            e.drain()
            e.tensor_copy(qi1, ch)
            e.drain()
            e.tensor_copy(ch, qi1)
            e.drain()
            return e.tensor_scalar(ch, ch, -0.5, 0.5, ALU.mult, ALU.add)
        p.run("vector", mkch, p.sem("ve"))

    def ph_p0(self):
        c, p = self.cfg, self.p
        ld, pe, ve, se, st, cc = (p.sem(n) for n in
                                  ("ld", "pe", "ve", "se", "st", "cc"))
        ones = self.sb["cones"].ap()
        orow = self.sb["conesrow"].ap()
        C0 = self.sb["C0"].ap()
        mx = self.sb["sm"].ap()[:, 0:1]
        p.wait("vector", ld)
        p.run("vector", lambda e: e.memset(mx, 0.0))
        for tt in range(c.NTT):
            lo, hi = tt * c.TT, (tt + 1) * c.TT
            pss0 = self.ps["A"].ap()[0:1, 0:c.TT]
            pss1 = self.ps["A"].ap()[0:1, 512:512 + c.TT]
            p.wait("sync", pe)
            p.wait("sync", se)
            p.wait("sync", ve)
            xall = []
            for i in range(c.DC):
                xall.append(p.dma("sync", self.tslot(i),
                                  self.da("x_fm")[i * P:(i + 1) * P, lo:hi], ld))
            p.wait("scalar", ld, xall[-1])
            p.wait("scalar", pe)
            sev = None
            for i in range(c.DC):
                sq = C0[:, i * c.TT:(i + 1) * c.TT]
                xv = self.tslot(i)
                sev = p.run("scalar", lambda e, sq=sq, xv=xv: e.activation(
                    sq, xv, ACT.Square), se)
            p.wait("tensor", ld, xall[-1])
            p.wait("tensor", ve)
            for i in range(c.DC):
                xv = self.tslot(i)
                p.run("tensor", lambda e, i=i, pss0=pss0, xv=xv: e.matmul(
                    pss0, ones, xv, start=(i == 0), stop=(i == c.DC - 1)))
            p.wait("tensor", se, sev)
            pev = None
            for i in range(c.DC):
                sq = C0[:, i * c.TT:(i + 1) * c.TT]
                pev = p.run("tensor", lambda e, i=i, sq=sq, pss1=pss1: e.matmul(
                    pss1, ones, sq, start=(i == 0), stop=(i == c.DC - 1)), pe)
            p.wait("vector", pe, pev)
            s_m = self.row(0, lo, hi)
            s_r = self.sb["rows"].ap()[0:1, 4096 + lo:4096 + hi]
            Df = float(c.D)

            def stat2(e, pss0=pss0, pss1=pss1, s_m=s_m, s_r=s_r, Df=Df):
                e.tensor_scalar(s_m, pss0, 1.0 / Df, None, ALU.mult)
                e.tensor_scalar(s_r, pss1, 1.0 / Df, None, ALU.mult)
                e.tensor_mul(pss0, s_m, s_m)
                e.tensor_sub(s_r, s_r, pss0)
                return e.tensor_scalar(s_r, s_r, 1e-5, None, ALU.add)
            vev = p.run("vector", stat2, ve)
            p.wait("scalar", ve, vev)
            sev = p.run("scalar", lambda e, s_r=s_r: e.activation(s_r, s_r, ACT.Sqrt), se)
            p.wait("vector", se, sev)
            vev = p.run("vector", lambda e, s_r=s_r: e.reciprocal(s_r, s_r), ve)
            bcm = self.psA(1024, 1024 + c.TT)
            bcr = self.psB(0, c.TT)
            p.wait("tensor", ve, vev)
            p.run("tensor", lambda e, s_m=s_m, bcm=bcm: e.matmul(
                bcm, orow, s_m, start=True, stop=True))
            pev = p.run("tensor", lambda e, s_r=s_r, bcr=bcr: e.matmul(
                bcr, orow, s_r, start=True, stop=True), pe)
            p.wait("vector", pe, pev)
            for i in range(c.DC):
                tn = self.sb[f"T{i % 2}"].ap()[:, :c.TT]
                xt = self.tslot(i)
                at = self.sb[f"acc{i}"].ap()[:, lo:hi]
                p.wait("vector", st)

                def xn_ops(e, xt=xt, at=at, tn=tn, bcm=bcm, bcr=bcr):
                    e.tensor_sub(tn, xt, bcm)
                    e.tensor_mul(tn, tn, bcr)
                    e.tensor_add(at, xt, tn)
                    e.tensor_reduce(self.sb["sm"].ap()[:, 1:2], tn, AX.X,
                                    ALU.max, apply_absolute_value=True)
                    e.drain()
                    return e.tensor_max(mx, mx, self.sb["sm"].ap()[:, 1:2])
                vev = p.run("vector", xn_ops, ve)
                p.wait("gpsimd", ve, vev)
                p.dma("gpsimd", self.da("xn_sp")[i * P:(i + 1) * P, lo:hi], tn, st)

        row = self.sb["rows"].ap()[0:1, 6144:][:, :P]
        p.wait("gpsimd", ve)
        p.dma("gpsimd", row, mx, st)
        p.wait("vector", st)
        g = self.sb["sm"].ap()[0:1, 2:3]
        p.run("vector", lambda e: e.tensor_reduce(g, row, AX.X, ALU.max))
        b8 = self.sb["rows"].ap()[0:1, 6144:][:, 128:136]

        def mk8(e, b8=b8, g=g):
            e.memset(b8, 0.0)
            e.drain()
            return e.tensor_copy(b8[:, 0:1], g)
        vev = p.run("vector", mk8, ve)
        p.wait("gpsimd", ve, vev)
        p.dma("gpsimd", self.da("cc_max1_in")[:, :], b8, st)
        p.wait("gpsimd", st)
        p.run("gpsimd", lambda e: e.collective_compute(
            "AllReduce", ALU.max, replica_groups=[list(range(8))],
            ins=[self.dram["cc_max1_in"].ap().opt()],
            outs=[self.dram["cc_max1_out"].ap().opt()]), cc)

    def ph_l0_attn_A(self):
        c, p = self.cfg, self.p
        ld, pe, ve, se, st = (p.sem(n) for n in ("ld", "pe", "ve", "se", "st"))
        ones = self.sb["cones"].ap()
        orow = self.sb["conesrow"].ap()
        C0 = self.sb["C0"].ap()
        NOC = 3 * c.DC
        for tt in range(c.NTT):
            lo, hi = tt * c.TT, (tt + 1) * c.TT
            pss1 = self.ps["A"].ap()[0:1, 512:512 + c.TT]
            p.wait("sync", p.sems["st"])
            p.wait("sync", pe)
            p.wait("sync", ve)
            p.wait("sync", se)
            xnv = []
            for i in range(c.DC):
                xnv.append(p.dma("sync", self.tslot(i),
                                 self.da("xn_sp")[i * P:(i + 1) * P, lo:hi], ld))
            p.wait("scalar", ld, xnv[-1])
            p.wait("scalar", pe)
            sev = None
            for i in range(c.DC):
                sq = C0[:, i * c.TT:(i + 1) * c.TT]
                xv = self.tslot(i)
                sev = p.run("scalar", lambda e, sq=sq, xv=xv: e.activation(
                    sq, xv, ACT.Square), se)
            p.wait("tensor", se, sev)
            p.wait("tensor", ve)
            pev = None
            for i in range(c.DC):
                sq = C0[:, i * c.TT:(i + 1) * c.TT]
                pev = p.run("tensor", lambda e, i=i, sq=sq, pss1=pss1: e.matmul(
                    pss1, ones, sq, start=(i == 0), stop=(i == c.DC - 1)), pe)
            p.wait("vector", pe, pev)
            s_r = self.sb["rows"].ap()[0:1, 4096 + lo:4096 + hi]
            Df = float(c.D)
            vev = p.run("vector", lambda e, pss1=pss1, s_r=s_r, Df=Df:
                        e.tensor_scalar(s_r, pss1, 1.0 / Df, 1.1920929e-07,
                                        ALU.mult, ALU.add), ve)
            p.wait("scalar", ve, vev)
            sev = p.run("scalar", lambda e, s_r=s_r: e.activation(s_r, s_r, ACT.Sqrt), se)
            p.wait("vector", se, sev)
            vev = p.run("vector", lambda e, s_r=s_r: e.reciprocal(s_r, s_r), ve)
            bcr = self.psB(0, c.TT)
            p.wait("tensor", ve, vev)
            pev = p.run("tensor", lambda e, s_r=s_r, bcr=bcr: e.matmul(
                bcr, orow, s_r, start=True, stop=True), pe)
            p.wait("vector", pe, pev)
            for i in range(c.DC):
                xv = self.tslot(i)
                vev = p.run("vector", lambda e, xv=xv, bcr=bcr: e.tensor_mul(
                    xv, xv, bcr), ve)
                p.wait("gpsimd", ve, vev)
                p.dma("gpsimd", self.da("xh_sp")[i * P:(i + 1) * P, lo:hi], xv, st)
        wq = self.da("wqkvT")
        wbuf = [self.sb["WB0"].ap(), self.sb["WB1"].ap()]
        for ts in range(c.NTT):
            lo, hi = ts * c.TT, (ts + 1) * c.TT
            p.wait("sync", pe)
            p.wait("sync", ve)
            p.wait("sync", st)
            xl = None
            for i in range(c.DC):
                xl = p.dma("sync", self.tslot(i),
                           self.da("xh_sp")[i * P:(i + 1) * P, lo:hi], ld)
            wld = [None, None]

            def wslice(b, oc):
                v = None
                for kc in range(c.DC):
                    v = p.dma("sync", wbuf[b][:, kc * P:(kc + 1) * P],
                              wq[kc * P:(kc + 1) * P, oc * P:(oc + 1) * P],
                              p.sems["ld"])
                return v
            wld[0] = wslice(0, 0)
            if NOC > 1:
                wld[1] = wslice(1, 1)
            for oc in range(NOC):
                b = oc % 2
                bank = oc % 2
                ps = self.psA(0, c.TT) if bank == 0 else self.psA(512, 512 + c.TT)
                p.wait("tensor", ld, wld[b])
                p.wait("tensor", ld, xl)
                p.wait("tensor", se)
                pev = None
                for kc in range(c.DC):
                    lhs = wbuf[b][:, kc * P:(kc + 1) * P]
                    rhs = self.tslot(kc)
                    pev = p.run("tensor", lambda e, lhs=lhs, rhs=rhs, ps=ps,
                                kc=kc: e.matmul(ps, lhs, rhs, start=(kc == 0),
                                                stop=(kc == c.DC - 1)), pe)
                if oc + 2 < NOC:
                    p.wait("sync", pe, pev)
                    wld[b] = wslice(b, oc + 2)
                sg = oc % 2
                stg = self.sb[f"T{sg}"].ap()[:, :c.TT]
                p.wait("scalar", pe, pev)
                p.wait("scalar", st)
                sev = p.run("scalar", lambda e, ps=ps, stg=stg: e.copy(stg, ps), se)
                p.wait("gpsimd", se, sev)
                p.dma("gpsimd", self.da("qkv_sp")[oc * P:(oc + 1) * P, lo:hi],
                      stg, st)

    def ph_l0_attn_A2(self):
        c, p = self.cfg, self.p
        ld, pe, ve, se, st, cc = (p.sem(n) for n in
                                  ("ld", "pe", "ve", "se", "st", "cc"))
        ones = self.sb["cones"].ap()
        orow = self.sb["conesrow"].ap()
        C0 = self.sb["C0"].ap()
        C1 = self.sb["C1"].ap()
        csum = self.sb["csum"].ap()
        p.wait("sync", p.sems["st"])
        p.wait("vector", p.sems["st"])
        vev = p.run("vector", lambda e: e.memset(csum, 0.0), ve)
        qkv = self.da("qkv_sp")
        for tt in range(c.NTT):
            lo, hi = tt * c.TT, (tt + 1) * c.TT
            p.wait("sync", ve)
            p.wait("sync", pe)
            qv = kv_ = vv = None
            for i in range(c.DC):
                qv = p.dma("sync", C0[:, i * c.TT:(i + 1) * c.TT],
                           qkv[i * P:(i + 1) * P, lo:hi], ld)
            for i in range(c.DC):
                kv_ = p.dma("sync", C1[:, i * c.TT:(i + 1) * c.TT],
                            qkv[(c.DC + i) * P:(c.DC + i + 1) * P, lo:hi], ld)
            for i in range(c.DC):
                vv = p.dma("sync", self.aslot(i),
                           qkv[(2 * c.DC + i) * P:(2 * c.DC + i + 1) * P, lo:hi], ld)
            pss0 = self.ps["A"].ap()[0:1, 0:c.TT]
            pss1 = self.ps["A"].ap()[0:1, 512:512 + c.TT]
            sq = self.sb["T3"].ap()[:, :c.TT]
            p.wait("scalar", ld, qv)
            p.wait("tensor", ld, kv_)
            pev = sev = None
            for which, (buf, pss) in enumerate(((C0, pss0), (C1, pss1))):
                for i in range(c.DC):
                    xv = buf[:, i * c.TT:(i + 1) * c.TT]
                    p.wait("scalar", pe, pev) if (i or which) else None
                    sev = p.run("scalar", lambda e, sq=sq, xv=xv: e.activation(
                        sq, xv, ACT.Square), se)
                    p.wait("tensor", se, sev)
                    pev = p.run("tensor", lambda e, sq=sq, pss=pss, i=i: e.matmul(
                        pss, ones, sq, start=(i == 0), stop=(i == c.DC - 1)), pe)
            s_q = self.sb["rows"].ap()[0:1, 4096 + lo:4096 + hi]
            s_k = self.row(0, lo, hi)
            Df = float(c.D)
            p.wait("vector", pe, pev)

            def rr(e, pss0=pss0, pss1=pss1, s_q=s_q, s_k=s_k, Df=Df):
                e.tensor_scalar(s_q, pss0, 1.0 / Df, 1.1920929e-07, ALU.mult, ALU.add)
                return e.tensor_scalar(s_k, pss1, 1.0 / Df, 1.1920929e-07,
                                       ALU.mult, ALU.add)
            vev = p.run("vector", rr, ve)
            p.wait("scalar", ve, vev)

            def sq2(e, s_q=s_q, s_k=s_k):
                e.activation(s_q, s_q, ACT.Sqrt)
                return e.activation(s_k, s_k, ACT.Sqrt)
            sev = p.run("scalar", sq2, se)
            p.wait("vector", se, sev)

            def rc2(e, s_q=s_q, s_k=s_k):
                e.reciprocal(s_q, s_q)
                return e.reciprocal(s_k, s_k)
            vev = p.run("vector", rc2, ve)
            bcq = self.psB(0, c.TT)
            bck = self.psB(512, 512 + c.TT)
            p.wait("tensor", ve, vev)
            p.run("tensor", lambda e, s_q=s_q, bcq=bcq: e.matmul(
                bcq, orow, s_q, start=True, stop=True))
            pev = p.run("tensor", lambda e, s_k=s_k, bck=bck: e.matmul(
                bck, orow, s_k, start=True, stop=True), pe)
            p.wait("vector", pe, pev)
            p.wait("vector", ld, vv)
            for i in range(c.DC):
                qv_ = C0[:, i * c.TT:(i + 1) * c.TT]
                kvv = C1[:, i * c.TT:(i + 1) * c.TT]
                vv_ = self.aslot(i)
                tg = self.sb["T4"].ap()[:, :c.TT]
                tw = self.sb["T5"].ap()[:, :c.TT]
                tkv = self.sb["T6"].ap()[:, :c.TT]
                p.wait("vector", st)
                vev = p.run("vector", lambda e, tg=tg, qv_=qv_, bcq=bcq:
                            e.tensor_mul(tg, qv_, bcq), ve)
                p.wait("scalar", ve, vev)
                sev = p.run("scalar", lambda e, tg=tg: e.activation(
                    tg, tg, ACT.Sigmoid), se)
                p.wait("vector", st)
                vev = p.run("vector", lambda e, tw=tw, kvv=kvv, bck=bck:
                            e.tensor_mul(tw, kvv, bck), ve)
                p.wait("scalar", ve, vev)
                sev = p.run("scalar", lambda e, tw=tw: e.activation(
                    tw, tw, ACT.Exp), se)
                p.wait("vector", se, sev)

                def kvops(e, tkv=tkv, tw=tw, vv_=vv_, csum=csum, i=i):
                    e.tensor_mul(tkv, tw, vv_)
                    e.tensor_reduce(self.sb["sm"].ap()[:, 3:4], tw, AX.X, ALU.add)
                    e.drain()
                    e.tensor_add(csum[:, i:i + 1], csum[:, i:i + 1],
                                 self.sb["sm"].ap()[:, 3:4])
                    e.tensor_reduce(self.sb["sm"].ap()[:, 3:4], tkv, AX.X, ALU.add)
                    e.drain()
                    return e.tensor_add(csum[:, c.DC + i:c.DC + i + 1],
                                        csum[:, c.DC + i:c.DC + i + 1],
                                        self.sb["sm"].ap()[:, 3:4])
                vev = p.run("vector", kvops, ve)
                p.wait("gpsimd", se, sev)
                p.wait("gpsimd", ve, vev)
                p.dma("gpsimd", self.da("gate_sp")[i * P:(i + 1) * P, lo:hi], tg, st)
                p.dma("gpsimd", self.da("w_sp")[i * P:(i + 1) * P, lo:hi], tw, st)
                p.dma("gpsimd", self.da("kv_sp")[i * P:(i + 1) * P, lo:hi], tkv, st)
        src = csum
        dst = self.da("cc_carry_in").rearrange("o (p c) -> (o p) c", c=2 * c.DC)
        p.wait("gpsimd", ve)
        v = p.dma("gpsimd", dst, src, st)
        p.wait("gpsimd", st, v)
        self._cc5 = p.run("gpsimd", lambda e: e.collective_compute(
            "AllGather", ALU.bypass, replica_groups=[list(range(8))],
            ins=[self.dram["cc_carry_in"].ap().opt()],
            outs=[self.dram["cc_carry_out"].ap().opt()]), cc)

    def colslice_load(self, wname, oc, buf, nchunk, ld):
        w = self.da(wname)
        v = None
        for kc in range(nchunk):
            v = self.p.dma("sync", buf[:, kc * P:(kc + 1) * P],
                           w[kc * P:(kc + 1) * P, oc * P:(oc + 1) * P],
                           self.p.sems["ld"])
        return v

    def bc_scalar(self, src11, dst_ps):
        orow = self.sb["conesrow"].ap()
        return lambda e: e.matmul(dst_ps, orow, src11, start=True, stop=True)

    def qdq_chain(self, e, x_ap, out_ap, rbc, sbc, tmp):
        w = tmp.shape[-1]
        qi = self.sb["qint"].ap()[:, :w]
        yi = out_ap.bitcast(U32)
        ti = tmp.bitcast(U32)
        e.tensor_scalar(out_ap, x_ap, rbc, None, ALU.mult)
        e.tensor_single_scalar(ti, yi, 0x7FFFFFFF, ALU.bitwise_and)
        e.tensor_scalar(tmp, tmp, self.sb["chalf"].ap(), None, ALU.add)
        e.tensor_copy(qi, tmp)
        e.tensor_copy(tmp, qi)
        e.tensor_single_scalar(yi, yi, 0x80000000, ALU.bitwise_and)
        e.tensor_tensor(yi, yi, ti, ALU.bitwise_or)
        return e.tensor_scalar(out_ap, out_ap, sbc, None, ALU.mult)

    def scale_from_cc(self, ccname, slot, colbase):
        c, p = self.cfg, self.p
        ld, pe, ve, se, cc = (p.sem(n) for n in ("ld", "pe", "ve", "se", "cc"))
        orow = self.sb["conesrow"].ap()
        row = self.sb["rows"].ap()[0:1, 140 + 16 * slot:148 + 16 * slot]
        p.wait("sync", cc)
        lv = p.dma("sync", row, self.da(ccname)[:, :], ld)
        g = row[:, slot:slot + 1] if False else row[:, 0:1]
        p.wait("vector", ld, lv)
        s11 = self.sb["rows"].ap()[0:1, 300 + 8 * slot:301 + 8 * slot]
        r11 = self.sb["rows"].ap()[0:1, 364 + 8 * slot:365 + 8 * slot]

        def mk(e, row=row, s11=s11, r11=r11, slot=slot):
            e.tensor_scalar(s11, row[:, slot:slot + 1], 1.0 / 32767.0, 1e-12,
                            ALU.mult, ALU.max)
            e.drain()
            return e.reciprocal(r11, s11)
        vev = p.run("vector", mk, ve)
        psr = self.psB(1536, 1537)
        pss = self.psB(1537, 1538)
        p.wait("tensor", ve, vev)
        p.run("tensor", lambda e, r11=r11, psr=psr: e.matmul(
            psr, orow, r11, start=True, stop=True))
        pev = p.run("tensor", lambda e, s11=s11, pss=pss: e.matmul(
            pss, orow, s11, start=True, stop=True), pe)
        sbc = self.sb["sm"].ap()[:, 8 + 2 * colbase:9 + 2 * colbase]
        rbc = self.sb["sm"].ap()[:, 9 + 2 * colbase:10 + 2 * colbase]
        p.wait("vector", pe, pev)

        def cp(e, sbc=sbc, rbc=rbc, psr=psr, pss=pss):
            e.tensor_copy(sbc, pss)
            return e.tensor_copy(rbc, psr)
        vev = p.run("vector", cp, ve)
        return sbc, rbc, vev

    def dequant_matrix(self, wT_name, wN_name, n_in_chunks, n_out,
                       natbuf_fn, rsT, nsT):
        c, p = self.cfg, self.p
        ld, pe, ve, se, st = (p.sem(n) for n in ("ld", "pe", "ve", "se", "st"))
        eye = self.sb["consts"].ap()
        n_row_tiles = n_out // P
        d_in = n_in_chunks * P
        nblk = d_in // c.QBLK
        wn = self.da(wN_name)
        for rt in range(n_row_tiles):
            bslot = natbuf_fn(rt)
            p.wait("sync", pe)
            p.wait("sync", ve)
            lv = p.dma("sync", bslot, wn[rt * P:(rt + 1) * P, :], ld)
            stile = self.sb["T9"].ap()[:, :nblk]
            p.wait("vector", ld, lv)
            bv = bslot.rearrange("p (nb b) -> p nb b", b=c.QBLK)
            vev = p.run("vector", lambda e, stile=stile, bv=bv: e.tensor_reduce(
                stile, bv, AX.X, ALU.max, apply_absolute_value=True), ve)
            pst = self.ps["B"].ap()[0:nblk, 1024:1024 + P]
            p.wait("tensor", ve, vev)
            pev = p.run("tensor", lambda e, pst=pst, stile=stile, eye=eye:
                        e.transpose(pst, stile, eye), pe)
            p.wait("vector", pe, pev)

            def ev(e, pst=pst, rt=rt, rsT=rsT, nblk=nblk):
                v = None
                for g in range(0, nblk, 32):
                    gg = min(32, nblk - g)
                    v = e.tensor_copy(rsT[g:g + gg, rt * P:(rt + 1) * P],
                                      pst[g:g + gg, :])
                return v
            vev = p.run("vector", ev, ve)

        def fin(e, rsT=rsT, nsT=nsT, nblk=nblk):
            v = None
            for g in range(0, nblk, 32):
                gg = min(32, nblk - g)
                rs = rsT[g:g + gg, :]
                ns = nsT[g:g + gg, :]
                e.tensor_scalar(rs, rs, 1.0 / 7.0, 1e-12, ALU.mult, ALU.max)
                e.tensor_scalar(ns, rs, -1.0, None, ALU.mult)
                v = e.reciprocal(rs, rs)
            return v
        vev = self.p.run("vector", fin, ve)
        return vev

    def dequant_group(self, wT_name, rsT, nsT, kc, og, og_cols, dst):
        c, p = self.cfg, self.p
        ld, pe, ve, se, st = (p.sem(n) for n in ("ld", "pe", "ve", "se", "st"))
        nrep = P // c.QBLK
        lo, hi = og * og_cols, (og + 1) * og_cols
        lv = p.dma("sync", dst, self.da(wT_name)[kc * P:(kc + 1) * P, lo:hi],
                   self.p.sems["ld"])
        rrep = self.sb["T9"].ap()[:, :og_cols]
        srep = self.sb["T3"].ap()[:, :og_cols]
        src_r = rsT[nrep * kc:nrep * kc + nrep, lo:hi].unsqueeze(1) \
            .to_broadcast([nrep, c.QBLK, og_cols])
        src_s = nsT[nrep * kc:nrep * kc + nrep, lo:hi].unsqueeze(1) \
            .to_broadcast([nrep, c.QBLK, og_cols])
        p.wait("gpsimd", ve)
        p.wait("gpsimd", pe)
        r1 = p.dma("gpsimd", rrep, src_r, st)
        r2 = p.dma("gpsimd", srep, src_s, st)
        tmp = self.sb["T4"].ap()[:, :og_cols]
        p.wait("vector", ld, lv)
        p.wait("vector", st, r2)

        def dq(e, dst=dst, rrep=rrep, srep=srep, tmp=tmp):
            w = tmp.shape[-1]
            qi = self.sb["qint"].ap()[:, :w]
            di = dst.bitcast(U32)
            ti = tmp.bitcast(U32)
            e.tensor_mul(dst, dst, rrep)
            e.tensor_single_scalar(ti, di, 0x7FFFFFFF, ALU.bitwise_and)
            e.tensor_scalar(tmp, tmp, self.sb["chalf"].ap(), None, ALU.add)
            e.tensor_copy(qi, tmp)
            e.tensor_copy(tmp, qi)
            e.tensor_single_scalar(di, di, 0x80000000, ALU.bitwise_and)
            e.tensor_single_scalar(di, di, 0x80000000, ALU.bitwise_xor)
            e.tensor_tensor(di, di, ti, ALU.bitwise_or)
            return e.tensor_mul(dst, dst, srep)
        return p.run("vector", dq, ve)

    def dequant_spill(self, wT, wN, n_in_chunks, n_out, natbuf_fn, spname,
                      srows):
        c, p = self.cfg, self.p
        ld, pe, ve, se, st = (p.sem(n) for n in ("ld", "pe", "ve", "se", "st"))
        rsT = self.sb["rows"].ap()[32:32 + srows, 0:n_out]
        nsT = self.sb["rows"].ap()[32:32 + srows, 3328:3328 + n_out]
        vev = self.dequant_matrix(wT, wN, n_in_chunks, n_out, natbuf_fn,
                                  rsT, nsT)
        OG = 512 if n_out % 512 == 0 else 256
        nog = n_out // OG
        for og in range(nog):
            for kc in range(n_in_chunks):
                dst = self.sb["T6"].ap()[:, :OG]
                p.wait("sync", st)
                p.wait("sync", ve)
                dvev = self.dequant_group(wT, rsT, nsT, kc, og, OG, dst)
                p.wait("gpsimd", ve, dvev)
                p.dma("gpsimd",
                      self.da(spname)[kc * P:(kc + 1) * P,
                                      og * OG:(og + 1) * OG], dst, st)

    def ph_l0_ffn_F1(self):
        c, p = self.cfg, self.p
        ld, pe, ve, se, st, cc = (p.sem(n) for n in
                                  ("ld", "pe", "ve", "se", "st", "cc"))
        sbc, rbc, vev = self.scale_from_cc("cc_max1_out", 0, 0)
        p.wait("sync", ve, vev)
        for tt in range(c.NTT):
            lo, hi = tt * c.TT, (tt + 1) * c.TT
            p.wait("sync", st)
            p.wait("sync", ve)
            for i in range(c.DC):
                lv = p.dma("sync", self.tslot(i),
                           self.da("xn_sp")[i * P:(i + 1) * P, lo:hi], ld)
                xv = self.tslot(i)
                tmp = self.sb["T0"].ap()[:, :c.TT]
                p.wait("vector", ld, lv)
                vev = p.run("vector", lambda e, xv=xv, tmp=tmp, rbc=rbc,
                            sbc=sbc: self.qdq_chain(e, xv, xv, rbc, sbc, tmp),
                            ve)
                p.wait("gpsimd", ve, vev)
                p.dma("gpsimd", self.da("xq_sp")[i * P:(i + 1) * P, lo:hi],
                      xv, st)
        def natbuf(rt):
            return self.sb[f"WB{rt % 2}"].ap()[:, :c.D]
        self.dequant_spill("wupT", "wup_n", c.DC, c.FI, natbuf, "wupdq_sp",
                           c.D // c.QBLK)
        self.dequant_spill("wgateT", "wgate_n", c.DC, c.FI, natbuf,
                           "wgatedq_sp", c.D // c.QBLK)
        mx2 = self.sb["sm"].ap()[:, 16:17]
        mx3 = self.sb["sm"].ap()[:, 17:18]
        p.run("vector", lambda e: e.memset(mx2, 0.0))
        p.run("vector", lambda e: e.memset(mx3, 0.0))
        wbuf = [self.sb["WB0"].ap(), self.sb["WB1"].ap()]
        for mi, (dqsp, rawsp, mxc) in enumerate((
                ("wupdq_sp", "upraw_sp", mx2),
                ("wgatedq_sp", "gateraw_sp", mx3))):
            for ts in range(c.NTT):
                lo, hi = ts * c.TT, (ts + 1) * c.TT
                p.wait("sync", pe)
                p.wait("sync", ve)
                p.wait("sync", st)
                xl = None
                for i in range(c.DC):
                    xl = p.dma("sync", self.tslot(i),
                               self.da("xq_sp")[i * P:(i + 1) * P, lo:hi], ld)
                wld = [None, None]
                wld[0] = self.colslice_load(dqsp, 0, wbuf[0], c.DC, ld)
                wld[1] = self.colslice_load(dqsp, 1, wbuf[1], c.DC, ld)
                for oc in range(c.FIC):
                    b = oc % 2
                    bank = oc % 2
                    ps = self.psA(0, c.TT) if bank == 0 else self.psA(512, 512 + c.TT)
                    p.wait("tensor", ld, wld[b])
                    p.wait("tensor", ld, xl)
                    p.wait("tensor", se)
                    pev = None
                    for kc in range(c.DC):
                        lhs = wbuf[b][:, kc * P:(kc + 1) * P]
                        rhs = self.tslot(kc)
                        pev = p.run("tensor", lambda e, lhs=lhs, rhs=rhs,
                                    ps=ps, kc=kc: e.matmul(
                                        ps, lhs, rhs, start=(kc == 0),
                                        stop=(kc == c.DC - 1)), pe)
                    if oc + 2 < c.FIC:
                        p.wait("sync", pe, pev)
                        wld[b] = self.colslice_load(dqsp, oc + 2, wbuf[b],
                                                    c.DC, ld)
                    sg = oc % 2
                    stg = self.sb[f"T{sg}"].ap()[:, :c.TT]
                    p.wait("scalar", pe, pev)
                    p.wait("scalar", st)
                    p.wait("scalar", ve)
                    sev = p.run("scalar", lambda e, ps=ps, stg=stg: e.copy(
                        stg, ps), se)
                    p.wait("vector", se, sev)

                    def mxop(e, stg=stg, mxc=mxc):
                        e.tensor_reduce(self.sb["sm"].ap()[:, 20:21], stg,
                                        AX.X, ALU.max, apply_absolute_value=True)
                        e.drain()
                        return e.tensor_max(mxc, mxc, self.sb["sm"].ap()[:, 20:21])
                    vev = p.run("vector", mxop, ve)
                    p.wait("gpsimd", se, sev)
                    p.wait("gpsimd", ve, vev)
                    p.dma("gpsimd", self.da(rawsp)[oc * P:(oc + 1) * P, lo:hi],
                          stg, st)
        row = self.sb["rows"].ap()[0:1, 6144:][:, :P]
        b8 = self.sb["rows"].ap()[0:1, 6144:][:, 128:136]
        p.wait("gpsimd", ve)
        v1 = p.dma("gpsimd", row, mx2, st)
        p.wait("vector", st, v1)
        g2 = self.sb["sm"].ap()[0:1, 21:22]
        vev = p.run("vector", lambda e, g2=g2, row=row: e.tensor_reduce(
            g2, row, AX.X, ALU.max), ve)
        p.wait("gpsimd", ve, vev)
        v2 = p.dma("gpsimd", row, mx3, st)
        p.wait("vector", st, v2)
        g3 = self.sb["sm"].ap()[0:1, 22:23]

        def pack(e, g3=g3, row=row, b8=b8, g2=g2):
            e.tensor_reduce(g3, row, AX.X, ALU.max)
            e.memset(b8, 0.0)
            e.drain()
            e.tensor_copy(b8[:, 0:1], g2)
            return e.tensor_copy(b8[:, 1:2], g3)
        vev = p.run("vector", pack, ve)
        p.wait("gpsimd", ve, vev)
        v3 = p.dma("gpsimd", self.da("cc_max23_in")[:, :], b8, st)
        p.wait("gpsimd", st, v3)
        p.run("gpsimd", lambda e: e.collective_compute(
            "AllReduce", ALU.max, replica_groups=[list(range(8))],
            ins=[self.dram["cc_max23_in"].ap().opt()],
            outs=[self.dram["cc_max23_out"].ap().opt()]), cc)

    def ph_l0_attn_B(self):
        c, p = self.cfg, self.p
        ld, pe, ve, se, st, cc = (p.sem(n) for n in
                                  ("ld", "pe", "ve", "se", "st", "cc"))
        orow = self.sb["conesrow"].ap()
        C0 = self.sb["C0"].ap()
        C1 = self.sb["C1"].ap()
        carry = self.sb["csum2"].ap()[:, :2 * self.cfg.DC]
        runc = self.sb["csum2"].ap()[:, 2 * self.cfg.DC:]
        gath = self.sb["rows"].ap()[32:40, :]
        cpick = self.sb["rows"].ap()[32:40, self.RW - 8:self.RW - 4]
        p.wait("sync", p.sems["cc"], self._cc5)
        lv = p.dma("sync", gath[:, :2 * c.D], self.da("cc_carry_out")[:, :], ld)
        p.wait("tensor", ld, lv)
        pev = None
        for j in range(0, 2 * c.D, 512):
            w = min(512, 2 * c.D - j)
            psx = self.psB(1024 + j // 4, 1024 + j // 4 + w, rows=1) if False \
                else self.psA(j % 2048, j % 2048 + w, rows=1)
            pev = p.run("tensor", lambda e, psx=psx, j=j, w=w: e.matmul(
                psx, cpick[:, 0:1], gath[:, j:j + w], start=True, stop=True), pe)
        carrow = self.sb["rows"].ap()[0:1, 6144:][0:1, :]
        carrow = self.row(0, 0, 2 * c.D)
        p.wait("vector", pe, pev)
        vev = None
        for j in range(0, 2 * c.D, 512):
            w = min(512, 2 * c.D - j)
            psx = self.psA(j % 2048, j % 2048 + w, rows=1)
            vev = p.run("vector", lambda e, psx=psx, j=j, w=w, carrow=carrow:
                        e.tensor_copy(carrow[:, j:j + w], psx), ve)
        p.wait("gpsimd", ve, vev)
        cv = p.dma("gpsimd", carry, carrow, st)
        p.wait("vector", st, cv)
        vev = p.run("vector", lambda e: e.tensor_copy(runc, carry), ve)

        wbuf = [self.sb["WB0"].ap(), self.sb["WB1"].ap()]
        for tt in range(c.NTT):
            lo, hi = tt * c.TT, (tt + 1) * c.TT
            p.wait("sync", ve)
            p.wait("sync", pe)
            lw = lk = lg = None
            for i in range(c.DC):
                lw = p.dma("sync", C0[:, i * c.TT:(i + 1) * c.TT],
                           self.da("w_sp")[i * P:(i + 1) * P, lo:hi], ld)
            for i in range(c.DC):
                lk = p.dma("sync", C1[:, i * c.TT:(i + 1) * c.TT],
                           self.da("kv_sp")[i * P:(i + 1) * P, lo:hi], ld)
            for i in range(c.DC):
                lg = p.dma("sync", self.aslot(i),
                           self.da("gate_sp")[i * P:(i + 1) * P, lo:hi], ld)
            p.wait("vector", ld, lg)
            for i in range(c.DC):
                wt = C0[:, i * c.TT:(i + 1) * c.TT]
                kvt = C1[:, i * c.TT:(i + 1) * c.TT]
                gt = self.aslot(i)
                t4 = self.sb["T4"].ap()[:, :c.TT]
                t5 = self.sb["T5"].ap()[:, :c.TT]
                zcol = self.sb["sm"].ap()[:, 24:25]

                def scans(e, wt=wt, kvt=kvt, gt=gt, t4=t4, t5=t5, i=i,
                          runc=runc, zcol=zcol):
                    e.memset(zcol, 0.0)
                    zb = zcol.to_broadcast([P, c.TT])
                    e.tensor_tensor_scan(t4, wt, zb, runc[:, i:i + 1],
                                         ALU.add, ALU.bypass)
                    e.drain()
                    e.tensor_copy(runc[:, i:i + 1], t4[:, c.TT - 1:c.TT])
                    e.tensor_tensor_scan(t5, kvt, zb,
                                         runc[:, c.DC + i:c.DC + i + 1],
                                         ALU.add, ALU.bypass)
                    e.drain()
                    e.tensor_copy(runc[:, c.DC + i:c.DC + i + 1],
                                  t5[:, c.TT - 1:c.TT])
                    e.tensor_scalar(t4, t4, 1e-6, None, ALU.add)
                    e.reciprocal(t4, t4)
                    e.tensor_mul(t5, t5, t4)
                    return e.tensor_mul(wt, t5, gt)
                vev = p.run("vector", scans, ve)
            NOC = c.SW // P
            wld = [None, None]
            p.wait("sync", ve, vev)
            p.wait("tensor", ve, vev)
            wld[0] = self.colslice_load("wswT", 0, wbuf[0], c.DC, ld)
            if NOC > 1:
                wld[1] = self.colslice_load("wswT", 1, wbuf[1], c.DC, ld)
            for oc in range(NOC):
                b = oc % 2
                bank = oc % 2
                ps = self.psA(0, c.TT) if bank == 0 else self.psA(512, 512 + c.TT)
                p.wait("tensor", ld, wld[b])
                p.wait("tensor", se)
                pev = None
                for kc in range(c.DC):
                    lhs = wbuf[b][:, kc * P:(kc + 1) * P]
                    rhs = C0[:, kc * c.TT:(kc + 1) * c.TT]
                    pev = p.run("tensor", lambda e, lhs=lhs, rhs=rhs, ps=ps,
                                kc=kc: e.matmul(ps, lhs, rhs, start=(kc == 0),
                                                stop=(kc == c.DC - 1)), pe)
                if oc + 2 < NOC:
                    p.wait("sync", pe, pev)
                    wld[b] = self.colslice_load("wswT", oc + 2, wbuf[b], c.DC, ld)
                if oc < c.DC:
                    p.wait("scalar", pe, pev)
                    p.wait("scalar", ve)
                    sev = p.run("scalar", lambda e, ps=ps, oc=oc, C1=C1: e.copy(
                        C1[:, oc * c.TT:(oc + 1) * c.TT], ps), se)
                else:
                    t6 = self.sb["T6"].ap()[:, :c.TT]
                    p.wait("scalar", pe, pev)
                    p.wait("scalar", ve)
                    sev = p.run("scalar", lambda e, ps=ps, t6=t6: e.activation(
                        t6, ps, ACT.Silu), se)
                    uslot = C1[:, (oc - c.DC) * c.TT:(oc - c.DC + 1) * c.TT]
                    p.wait("vector", se, sev)
                    vev = p.run("vector", lambda e, uslot=uslot, t6=t6:
                                e.tensor_mul(uslot, uslot, t6), ve)
            p.wait("sync", pe)
            wld[0] = self.colslice_load("woutT", 0, wbuf[0], c.DC, ld)
            if c.DC > 1:
                wld[1] = self.colslice_load("woutT", 1, wbuf[1], c.DC, ld)
            for oc in range(c.DC):
                b = oc % 2
                bank = oc % 2
                ps = self.psB(0, c.TT) if bank == 0 else self.psB(512, 512 + c.TT)
                p.wait("tensor", ld, wld[b])
                p.wait("tensor", ve)
                pev = None
                for kc in range(c.DC):
                    lhs = wbuf[b][:, kc * P:(kc + 1) * P]
                    rhs = C1[:, kc * c.TT:(kc + 1) * c.TT]
                    pev = p.run("tensor", lambda e, lhs=lhs, rhs=rhs, ps=ps,
                                kc=kc: e.matmul(ps, lhs, rhs, start=(kc == 0),
                                                stop=(kc == c.DC - 1)), pe)
                if oc + 2 < c.DC:
                    p.wait("sync", pe, pev)
                    wld[b] = self.colslice_load("woutT", oc + 2, wbuf[b], c.DC, ld)
                at = self.sb[f"acc{oc}"].ap()[:, lo:hi]
                p.wait("vector", pe, pev)
                vev = p.run("vector", lambda e, at=at, ps=ps: e.tensor_add(
                    at, at, ps), ve)

    def _gate2_tile(self, gq, tmp, tmp2, s3bc, r3bc):
        c, p = self.cfg, self.p
        ve, se = p.sem("ve"), p.sem("se")
        vev = p.run("vector", lambda e, gq=gq, tmp=tmp, r3bc=r3bc, s3bc=s3bc:
                    self.qdq_chain(e, gq, gq, r3bc, s3bc, tmp), ve)
        p.wait("scalar", ve, vev)
        sev = p.run("scalar", lambda e, tmp=tmp, gq=gq: e.activation(
            tmp, gq, ACT.Sigmoid), se)
        p.wait("vector", se, sev)

        def fx(e, gq=gq, tmp=tmp, tmp2=tmp2):
            w = tmp.shape[-1]
            qi = self.sb["qint"].ap()[:, :w]
            e.tensor_scalar(tmp, tmp, 65536.0, None, ALU.mult)
            e.tensor_scalar(tmp, tmp, self.sb["chalf"].ap(), None, ALU.add)
            e.tensor_copy(qi, tmp)
            e.tensor_copy(tmp, qi)
            e.tensor_scalar(tmp, tmp, 65535.0, 1.0 / 65536.0, ALU.min, ALU.mult)
            return e.tensor_mul(gq, gq, tmp)
        return p.run("vector", fx, ve)

    def ph_l0_ffn_F23(self):
        c, p = self.cfg, self.p
        ld, pe, ve, se, st, cc = (p.sem(n) for n in
                                  ("ld", "pe", "ve", "se", "st", "cc"))
        s3bc, r3bc, vev = self.scale_from_cc("cc_max23_out", 1, 1)
        s2bc, r2bc, vev = self.scale_from_cc("cc_max23_out", 0, 2)
        C0 = self.sb["C0"].ap()
        C1 = self.sb["C1"].ap()
        mx4 = self.sb["sm"].ap()[:, 26:27]
        mx5 = self.sb["sm"].ap()[:, 27:28]
        p.run("vector", lambda e: e.memset(mx4, 0.0))
        tmp = self.sb["T4"].ap()[:, :c.TT]
        tmp2 = self.sb["T5"].ap()[:, :c.TT]
        NTL = c.FIC * c.NTT
        vhist = []
        for it in range(NTL):
            oc, ts = divmod(it, c.NTT)
            lo, hi = ts * c.TT, (ts + 1) * c.TT
            slot = it % 4
            gq = C0[:, slot * c.TT:(slot + 1) * c.TT]
            if it >= 4:
                p.wait("sync", ve, vhist[it - 4])
            lv = p.dma("sync", gq, self.da("gateraw_sp")[oc * P:(oc + 1) * P, lo:hi], ld)
            p.wait("vector", ld, lv)
            gvev = self._gate2_tile(gq, tmp, tmp2, s3bc, r3bc)

            def mxop(e, gq=gq, mx4=mx4):
                e.tensor_reduce(self.sb["sm"].ap()[:, 20:21], gq, AX.X,
                                ALU.max, apply_absolute_value=True)
                e.drain()
                return e.tensor_max(mx4, mx4, self.sb["sm"].ap()[:, 20:21])
            vev = p.run("vector", mxop, ve)
            vhist.append(vev)
        row = self.sb["rows"].ap()[0:1, 6144:][:, :P]
        b8 = self.sb["rows"].ap()[0:1, 6144:][:, 128:136]
        p.wait("gpsimd", ve, vev)
        v1 = p.dma("gpsimd", row, mx4, st)
        p.wait("vector", st, v1)
        g4 = self.sb["sm"].ap()[0:1, 21:22]

        def pk(e, g4=g4, row=row, b8=b8):
            e.tensor_reduce(g4, row, AX.X, ALU.max)
            e.memset(b8, 0.0)
            e.drain()
            return e.tensor_copy(b8[:, 0:1], g4)
        vev = p.run("vector", pk, ve)
        p.wait("gpsimd", ve, vev)
        v2 = p.dma("gpsimd", self.da("cc_max4_in")[:, :], b8, st)
        p.wait("gpsimd", st, v2)
        p.run("gpsimd", lambda e: e.collective_compute(
            "AllReduce", ALU.max, replica_groups=[list(range(8))],
            ins=[self.dram["cc_max4_in"].ap().opt()],
            outs=[self.dram["cc_max4_out"].ap().opt()]), cc)
        s4bc, r4bc, vev = self.scale_from_cc("cc_max4_out", 0, 3)
        p.run("vector", lambda e: e.memset(mx5, 0.0))
        vhist3 = []
        shist3 = []
        for it in range(NTL):
            oc, ts = divmod(it, c.NTT)
            lo, hi = ts * c.TT, (ts + 1) * c.TT
            slot = it % 4
            gq = C0[:, slot * c.TT:(slot + 1) * c.TT]
            uq = C1[:, slot * c.TT:(slot + 1) * c.TT]
            if it >= 4:
                p.wait("sync", ve, vhist3[it - 4])
                p.wait("sync", st, shist3[it - 4])
            lv = p.dma("sync", gq, self.da("gateraw_sp")[oc * P:(oc + 1) * P, lo:hi], ld)
            lv2 = p.dma("sync", uq, self.da("upraw_sp")[oc * P:(oc + 1) * P, lo:hi], ld)
            p.wait("vector", ld, lv)
            gvev = self._gate2_tile(gq, tmp, tmp2, s3bc, r3bc)

            def oraw(e, gq=gq, uq=uq, tmp=tmp, r4bc=r4bc, s4bc=s4bc,
                     r2bc=r2bc, s2bc=s2bc, mx5=mx5):
                self.qdq_chain(e, gq, gq, r4bc, s4bc, tmp)
                self.qdq_chain(e, uq, uq, r2bc, s2bc, tmp)
                e.tensor_mul(gq, gq, uq)
                e.tensor_reduce(self.sb["sm"].ap()[:, 20:21], gq, AX.X,
                                ALU.max, apply_absolute_value=True)
                e.drain()
                return e.tensor_max(mx5, mx5, self.sb["sm"].ap()[:, 20:21])
            p.wait("vector", ld, lv2)
            vev = p.run("vector", oraw, ve)
            vhist3.append(vev)
            p.wait("gpsimd", ve, vev)
            shist3.append(p.dma("gpsimd", self.da("oraw_sp")[oc * P:(oc + 1) * P,
                                                             lo:hi], gq, st))
        p.wait("gpsimd", ve, vev)
        v1 = p.dma("gpsimd", row, mx5, st)
        p.wait("vector", st, v1)
        g5 = self.sb["sm"].ap()[0:1, 21:22]

        def pk5(e, g5=g5, row=row, b8=b8):
            e.tensor_reduce(g5, row, AX.X, ALU.max)
            e.memset(b8, 0.0)
            e.drain()
            return e.tensor_copy(b8[:, 0:1], g5)
        vev = p.run("vector", pk5, ve)
        p.wait("gpsimd", ve, vev)
        v2 = p.dma("gpsimd", self.da("cc_max5_in")[:, :], b8, st)
        p.wait("gpsimd", st, v2)
        p.run("gpsimd", lambda e: e.collective_compute(
            "AllReduce", ALU.max, replica_groups=[list(range(8))],
            ins=[self.dram["cc_max5_in"].ap().opt()],
            outs=[self.dram["cc_max5_out"].ap().opt()]), cc)

    def ph_l0_ffn_F4(self):
        c, p = self.cfg, self.p
        ld, pe, ve, se, st, cc = (p.sem(n) for n in
                                  ("ld", "pe", "ve", "se", "st", "cc"))
        C0 = self.sb["C0"].ap()

        def natbuf(rt):
            if 2 * c.TOK >= c.FI:
                return C0[:, :c.FI]
            return self.sb[f"WB{rt % 2}"].ap()[:, :c.FI]
        self.dequant_spill("wdownT", "wdown_n", c.FIC, c.D, natbuf,
                           "wdowndq_sp", c.FI // c.QBLK)
        p.wait("sync", st)
        s5bc, r5bc, vev = self.scale_from_cc("cc_max5_out", 0, 4)
        wbuf = [self.sb["WB0"].ap(), self.sb["WB1"].ap()]
        tmp = self.sb["T0"].ap()[:, :c.TT]
        WKC = min(c.FIC, 2048 // P)
        for tt in range(c.NTT):
            lo, hi = tt * c.TT, (tt + 1) * c.TT
            p.wait("sync", ve)
            p.wait("sync", pe)
            p.wait("sync", se)
            lvs = []
            for i in range(c.FIC):
                lvs.append(p.dma("sync", self.oslot(i),
                                 self.da("oraw_sp")[i * P:(i + 1) * P, lo:hi], ld))
            for i in range(c.FIC):
                sl = self.oslot(i)
                p.wait("vector", ld, lvs[i])
                vev = p.run("vector", lambda e, sl=sl, tmp=tmp, r5bc=r5bc,
                            s5bc=s5bc: self.qdq_chain(e, sl, sl, r5bc, s5bc,
                                                      tmp), ve)
            p.wait("tensor", ve, vev)
            for oc in range(c.DC):
                b = oc % 2
                bank = oc % 2
                ps = self.psB(0, c.TT) if bank == 0 else self.psB(512, 512 + c.TT)
                nw = (c.FIC + WKC - 1) // WKC
                pev = None
                for w in range(nw):
                    k0, k1 = w * WKC, min((w + 1) * WKC, c.FIC)
                    p.wait("sync", pe)
                    wl = None
                    for kc in range(k0, k1):
                        wl = p.dma("sync", wbuf[b][:, (kc - k0) * P:(kc - k0 + 1) * P],
                                   self.da("wdowndq_sp")[kc * P:(kc + 1) * P,
                                                         oc * P:(oc + 1) * P],
                                   p.sems["ld"])
                    p.wait("tensor", ld, wl)
                    if w == 0:
                        p.wait("tensor", ve)
                    for kc in range(k0, k1):
                        lhs = wbuf[b][:, (kc - k0) * P:(kc - k0 + 1) * P]
                        pev = p.run("tensor", lambda e, lhs=lhs, kc=kc, ps=ps:
                                    e.matmul(ps, lhs, self.oslot(kc),
                                             start=(kc == 0),
                                             stop=(kc == c.FIC - 1)), pe)
                at = self.sb[f"acc{oc}"].ap()[:, lo:hi]
                p.wait("vector", pe, pev)
                vev = p.run("vector", lambda e, at=at, ps=ps: e.tensor_add(
                    at, at, ps), ve)

    def ph_l1_norm(self):
        c, p = self.cfg, self.p
        ld, pe, ve, se, st, cc = (p.sem(n) for n in
                                  ("ld", "pe", "ve", "se", "st", "cc"))
        ones = self.sb["cones"].ap()
        orow = self.sb["conesrow"].ap()
        stats = self.sb["rows"].ap()[0:1, :]
        C0 = self.sb["C0"].ap()
        for tt in range(c.NTT):
            lo, hi = tt * c.TT, (tt + 1) * c.TT
            pss0 = self.ps["A"].ap()[0:1, 0:c.TT]
            pss1 = self.ps["A"].ap()[0:1, 512:512 + c.TT]
            p.wait("scalar", pe)
            p.wait("scalar", ve)
            sev = None
            for i in range(c.DC):
                sq = C0[:, i * c.TT:(i + 1) * c.TT]
                xv = self.sb[f"acc{i}"].ap()[:, lo:hi]
                sev = p.run("scalar", lambda e, sq=sq, xv=xv: e.activation(
                    sq, xv, ACT.Square), se)
            p.wait("tensor", ve)
            for i in range(c.DC):
                xv = self.sb[f"acc{i}"].ap()[:, lo:hi]
                p.run("tensor", lambda e, i=i, pss0=pss0, xv=xv: e.matmul(
                    pss0, ones, xv, start=(i == 0), stop=(i == c.DC - 1)))
            p.wait("tensor", se, sev)
            pev = None
            for i in range(c.DC):
                sq = C0[:, i * c.TT:(i + 1) * c.TT]
                pev = p.run("tensor", lambda e, i=i, sq=sq, pss1=pss1: e.matmul(
                    pss1, ones, sq, start=(i == 0), stop=(i == c.DC - 1)), pe)
            p.wait("vector", pe, pev)
            s_m = stats[0:1, lo:hi]
            s_r = self.sb["rows"].ap()[0:1, 4096 + lo:4096 + hi]
            Df = float(c.D)

            def stat2(e, pss0=pss0, pss1=pss1, s_m=s_m, s_r=s_r, Df=Df):
                e.tensor_scalar(s_m, pss0, 1.0 / Df, None, ALU.mult)
                e.tensor_scalar(s_r, pss1, 1.0 / Df, None, ALU.mult)
                e.tensor_mul(pss0, s_m, s_m)
                e.tensor_sub(s_r, s_r, pss0)
                return e.tensor_scalar(s_r, s_r, 1e-5, None, ALU.add)
            vev = p.run("vector", stat2, ve)
            p.wait("scalar", ve, vev)
            sev = p.run("scalar", lambda e, s_r=s_r: e.activation(s_r, s_r, ACT.Sqrt), se)
            p.wait("vector", se, sev)
            vev = p.run("vector", lambda e, s_r=s_r: e.reciprocal(s_r, s_r), ve)
            bcm = self.psA(1024, 1024 + c.TT)
            bcr = self.psB(0, c.TT)
            p.wait("tensor", ve, vev)
            p.run("tensor", lambda e, s_m=s_m, bcm=bcm: e.matmul(
                bcm, orow, s_m, start=True, stop=True))
            pev = p.run("tensor", lambda e, s_r=s_r, bcr=bcr: e.matmul(
                bcr, orow, s_r, start=True, stop=True), pe)
            p.wait("vector", pe, pev)
            for i in range(c.DC):
                tn = self.sb[f"T{1 + (i % 2)}"].ap()[:, :c.TT]
                xt = self.sb[f"acc{i}"].ap()[:, lo:hi]
                p.wait("vector", st)

                def xn_ops(e, xt=xt, tn=tn, bcm=bcm, bcr=bcr):
                    e.tensor_sub(tn, xt, bcm)
                    e.tensor_mul(tn, tn, bcr)
                    return e.tensor_add(xt, xt, tn)
                vev = p.run("vector", xn_ops, ve)
                p.wait("gpsimd", ve, vev)
                p.dma("gpsimd", self.da("xn1_sp")[i * P:(i + 1) * P, lo:hi], tn, st)
                p.dma("gpsimd", self.da("cc_xn1_in")[i * P:(i + 1) * P, lo:hi], tn, st)
        p.wait("gpsimd", st)
        self._cc6 = p.run("gpsimd", lambda e: e.collective_compute(
            "AllGather", ALU.bypass,
            replica_groups=[[0, 1], [2, 3], [4, 5], [6, 7]],
            ins=[self.dram["cc_xn1_in"].ap().opt()],
            outs=[self.dram["cc_xn1_out"].ap().opt()]), cc)

    def ph_l1_attn(self):
        c, p = self.cfg, self.p
        ld, pe, ve, se, st, cc = (p.sem(n) for n in
                                  ("ld", "pe", "ve", "se", "st", "cc"))
        ones = self.sb["cones"].ap()
        orow = self.sb["conesrow"].ap()
        eye = self.sb["consts"].ap()
        C0 = self.sb["C0"].ap()
        C1 = self.sb["C1"].ap()
        GH = min(2, c.HPC)
        NG = c.HPC // GH
        NTB = c.T // 512
        p.wait("sync", p.sems["cc"], self._cc6)

        wbuf = [self.sb["WB0"].ap(), self.sb["WB1"].ap()]
        smax = self.sb["rows"].ap()[32:32 + min(2, self.cfg.HPC), :self.cfg.T]
        qsel = self.sb["qsel"].ap()
        expb = self.sb["expb"].ap()
        ctxb = self.sb["ctxb"].ap()
        for hg in range(NG):
            p.wait("sync", pe)
            p.wait("sync", ve)
            lq = lk = lv = None
            for ocl in range(GH * c.DK // P):
                oc = hg * (GH * c.DK // P) + ocl
                for kc in range(c.DC):
                    lq = p.dma("sync", wbuf[0][:, (ocl * c.DC + kc) * P:
                                                (ocl * c.DC + kc + 1) * P],
                               self.da("wqT_my")[kc * P:(kc + 1) * P,
                                                 oc * P:(oc + 1) * P], ld)
                    lk = p.dma("sync", wbuf[1][:, (ocl * c.DC + kc) * P:
                                                (ocl * c.DC + kc + 1) * P],
                               self.da("wkT_my")[kc * P:(kc + 1) * P,
                                                 oc * P:(oc + 1) * P], ld)
            VW = GH * c.DK
            NOCL = GH * c.DK // P
            C1a = self.sb["C1"].ap()

            def wvslot(kc):
                return C1a[:, kc * VW:(kc + 1) * VW]
            for kc in range(c.DC):
                lv = p.dma("sync", wvslot(kc),
                           self.da("wvT_my")[kc * P:(kc + 1) * P,
                                             hg * VW:(hg + 1) * VW], ld)
            bv = p.dma("sync", self.sb["rows"].ap()[0:1, 2048:][0:1, 0:VW],
                       self.da("bv_my_row")[0:1, hg * VW:(hg + 1) * VW], ld)
            bql = None
            for which, bias in enumerate(("bq_my_t", "bk_my_t")):
                for ocl in range(NOCL):
                    bcol = self.da(bias)[:, hg * NOCL + ocl:hg * NOCL + ocl + 1]
                    bs = self.sb["sm"].ap()[:, 40 + which * NOCL + ocl:
                                            41 + which * NOCL + ocl]
                    bql = p.dma("sync", bs, bcol, ld, slow=True)

            TBW = min(512, c.TOK)
            NTB = c.T // TBW
            TSUBS = TBW // P
            for tb in range(NTB):
                xl = []
                p.wait("sync", pe)
                half = (tb * TBW) // c.TOK
                lcol = (tb * TBW) % c.TOK
                for kc in range(c.DC):
                    xl.append(p.dma("sync", self.cslot(0, kc, TBW),
                                    self.da("cc_xn1_out")[
                                        half * c.D + kc * P:
                                        half * c.D + (kc + 1) * P,
                                        lcol:lcol + TBW],
                                    ld))
                p.wait("tensor", ld, xl[-1])
                p.wait("tensor", ld, lv)
                for which, (wb, bias, dst0) in enumerate((
                        (wbuf[0], "bq_my_t", 0), (wbuf[1], "bk_my_t", 1))):
                    for ocl in range(NOCL):
                        bank = (which * NOCL + ocl) % 2
                        ps = self.psA(0, TBW) if bank == 0 else self.psA(512, 512 + TBW)
                        p.wait("tensor", se)
                        pev = None
                        for kc in range(c.DC):
                            lhs = wb[:, (ocl * c.DC + kc) * P:(ocl * c.DC + kc + 1) * P]
                            rhs = self.cslot(0, kc, TBW)
                            pev = p.run("tensor", lambda e, lhs=lhs, rhs=rhs,
                                        ps=ps, kc=kc: e.matmul(
                                            ps, lhs, rhs, start=(kc == 0),
                                            stop=(kc == c.DC - 1)), pe)
                        dst = self.sb[f"bigA{dst0 + ocl}"].ap()[:, tb * TBW:(tb + 1) * TBW]
                        bs = self.sb["sm"].ap()[:, 40 + which * NOCL + ocl:
                                                41 + which * NOCL + ocl]
                        p.wait("scalar", ld, bql)
                        p.wait("scalar", pe, pev)
                        sev = p.run("scalar", lambda e, dst=dst, ps=ps, bs=bs:
                                    e.add(dst, ps, bs), se)
                for tsub in range(TSUBS):
                    psv = self.psB(0, VW)
                    p.wait("tensor", se)
                    pev = None
                    for kc in range(c.DC):
                        lhs = self.cslot(0, kc, TBW)[:, tsub * P:(tsub + 1) * P]
                        rhs = wvslot(kc)
                        pev = p.run("tensor", lambda e, lhs=lhs, rhs=rhs,
                                    psv=psv, kc=kc: e.matmul(
                                        psv, lhs, rhs, start=(kc == 0),
                                        stop=False), pe)
                    bvrow = self.sb["rows"].ap()[0:1, 2048:][0:1, 0:VW]
                    pev = p.run("tensor", lambda e, psv=psv, bvrow=bvrow:
                                e.matmul(psv, orow, bvrow, start=False,
                                         stop=True), pe)
                    vs = self.sb["T1"].ap()[:, :VW]
                    p.wait("scalar", pe, pev)
                    p.wait("scalar", st)
                    sev = p.run("scalar", lambda e, vs=vs, psv=psv:
                                e.copy(vs, psv), se)
                    rb = tb * TBW + tsub * P
                    p.wait("gpsimd", se, sev)
                    p.dma("gpsimd", self.da("v_sp")[rb:rb + P,
                                                    hg * VW:(hg + 1) * VW],
                          vs, st)
                wqmv = wbuf[0][:, :NOCL * c.DC * P].rearrange(
                    "p (o k n) -> p o k n", o=NOCL, k=c.DC)
                bqrow = self.sb["rows"].ap()[0:1, 2048:][0:1, 256:256 + VW]
                if tb == 0:
                    bqv = p.dma("sync", bqrow,
                                self.da("bq_my_row")[0:1, hg * VW:(hg + 1) * VW], ld)
                    p.wait("tensor", ld, bqv)
                for tsub in range(TSUBS):
                    psq = self.psB(1024, 1024 + VW)
                    p.wait("tensor", se)
                    pev = None
                    for kc in range(c.DC):
                        lhs = self.cslot(0, kc, TBW)[:, tsub * P:(tsub + 1) * P]
                        rhs = wqmv[:, :, kc, :]
                        pev = p.run("tensor", lambda e, lhs=lhs, rhs=rhs,
                                    psq=psq, kc=kc: e.matmul(
                                        psq, lhs, rhs, start=(kc == 0),
                                        stop=False), pe)
                    pev = p.run("tensor", lambda e, psq=psq, bqrow=bqrow:
                                e.matmul(psq, orow, bqrow, start=False,
                                         stop=True), pe)
                    stg = self.sb["T1"].ap()[:, :VW]
                    p.wait("scalar", pe, pev)
                    p.wait("scalar", st)
                    sev = p.run("scalar", lambda e, stg=stg, psq=psq:
                                e.copy(stg, psq), se)
                    rowbase = hg * c.T + tb * TBW + tsub * P
                    p.wait("gpsimd", se, sev)
                    p.dma("gpsimd",
                          self.da("qtm_sp")[rowbase:rowbase + P, :], stg, st)
            if self.debug_stop == "l1a1":
                return
            smx = self.sb["T0"].ap()[:, :c.LC]
            for hh in range(GH):
                qa = self.sb["bigA0"].ap()
                ka = self.sb["bigA1"].ap()
                base = (hh % 2) * c.DK
                for lt in range(c.LC):
                    nb = min(4, c.NKT)
                    pev = None
                    mxv = self.sb["sm"].ap()[:, 48:49]
                    for half in range((c.NKT + nb - 1) // nb):
                        psr = (self.ps["A"] if half == 0 else self.ps["B"]).ap()
                        p.wait("tensor", se)
                        p.wait("tensor", ve)
                        for g in range(nb):
                            kn = half * nb + g
                            if kn >= c.NKT:
                                continue
                            lhs = qa[base:base + c.DK, lt * P:(lt + 1) * P]
                            rhs = ka[base:base + c.DK, kn * c.KN:(kn + 1) * c.KN]
                            psx = psr[0:P, g * c.KN:(g + 1) * c.KN]
                            pev = p.run("tensor", lambda e, lhs=lhs, rhs=rhs,
                                        psx=psx: e.matmul(psx, lhs, rhs,
                                                          start=True, stop=True), pe)
                        wid = min(nb * c.KN, c.T - half * nb * c.KN)
                        p.wait("vector", pe, pev)

                        def red(e, psr=psr, wid=wid, half=half, mxv=mxv,
                                smx=smx, lt=lt):
                            e.tensor_reduce(self.sb["sm"].ap()[:, 49:50],
                                            psr[0:P, 0:wid], AX.X, ALU.max)
                            e.drain()
                            if half == 0:
                                return e.tensor_copy(mxv, self.sb["sm"].ap()[:, 49:50])
                            e.tensor_max(mxv, mxv, self.sb["sm"].ap()[:, 49:50])
                            e.drain()
                            return e.tensor_copy(smx[:, lt:lt + 1], mxv)
                        vev = p.run("vector", red, ve)
                        if c.NKT <= nb:
                            pass
                    if c.NKT <= nb:
                        def cpmx(e, smx=smx, lt=lt, mxv=mxv):
                            e.drain()
                            return e.tensor_copy(smx[:, lt:lt + 1], mxv)
                        vev = p.run("vector", cpmx, ve)
                pst = self.psB(1024, 1024 + P, rows=c.LC)
                p.wait("tensor", ve, vev)
                pev = p.run("tensor", lambda e, pst=pst, smx=smx, eye=eye:
                            e.transpose(pst, smx, eye), pe)
                t1 = self.sb["T1"].ap()[:c.LC, :P]
                p.wait("vector", pe, pev)
                p.wait("vector", st)
                vev = p.run("vector", lambda e, t1=t1, pst=pst:
                            e.tensor_copy(t1, pst), ve)
                p.wait("gpsimd", ve, vev)
                fv = p.dma("gpsimd", smax[hh:hh + 1, :], t1, st)
            if self.debug_stop == "l1a2":
                return
            g0 = hg * GH
            p.wait("vector", p.sems["st"])

            def sel8(e, g0=g0):
                e.max(self.sb["selv"].ap()[0:GH, :], smax[0:GH, :])
                e.drain()
                return e.max_index(self.sb["seli"].ap()[0:GH, :],
                                   self.sb["selv"].ap()[0:GH, :],
                                   smax[0:GH, :])
            vev = p.run("vector", sel8, ve)
            p.wait("gpsimd", ve, vev)
            iv = p.dma("gpsimd", self.sb["selrow"].ap()[0:1, g0 * 8:(g0 + GH) * 8],
                       self.sb["seli"].ap()[0:GH, :], st)
            fvv = p.dma("gpsimd", self.sb["rows"].ap()[0:1, 2048:][0:1, 512 + g0 * 8:512 + (g0 + GH) * 8],
                        self.sb["selv"].ap()[0:GH, :], st)
            s32g = self.sb["s32g0"].ap()
            def cps32(e, s32g=s32g):
                e.drain()
                return e.tensor_copy(s32g[:GH, :], self.sb["seli"].ap()[0:GH, :])
            vev = p.run("vector", cps32, ve)
            fparr = self.sb["rows"].ap()[0:1, 2048:][0:1, 768:768 + 128]
            vev = p.run("vector", lambda e, fparr=fparr: e.memset(fparr, 0.0), ve)
            p.wait("gpsimd", ve, vev)
            fav = None
            for hh in range(GH):
                fav = p.dma("gpsimd", fparr[0:1, hh * 32:hh * 32 + 8],
                            s32g[hh:hh + 1, :], st)
            psqi = self.psB(1544, 1545, rows=P)
            p.wait("tensor", st, fav)
            pev = p.run("tensor", lambda e, psqi=psqi, fparr=fparr:
                        e.transpose(psqi, fparr, self.sb["consts"].ap()[0:1, 0:1]), pe)
            gidxc = self.sb["gidxc"].ap()
            p.wait("vector", pe, pev)
            vev = p.run("vector", lambda e, gidxc=gidxc, psqi=psqi, hg=hg:
                        e.tensor_scalar(gidxc, psqi, float(hg * c.T), None,
                                        ALU.add), ve)
            p.wait("gpsimd", ve, vev)
            p.wait("gpsimd", st)
            from concourse.bass_core_rust import IndirectOffsetOnAxis as IOA
            qg = self.sb["qg"].ap()
            gqs = p.run("gpsimd", lambda e, qg=qg, gidxc=gidxc: e.indirect_dma_start(
                qg[:, :VW], None, self.da("qtm_sp")[:, :],
                IOA(gidxc, 0)), p.sems["st"], k=16)
            gqv = p.run("gpsimd", lambda e: e.drain(), p.sem("gp"))
            self._gqs = gqs
            if self.debug_stop == "l1a3":
                return
            for hh in range(GH):
                hr = hg * GH + hh
                qa = self.sb["bigA0"].ap()
                ka = self.sb["bigA1"].ap()
                base = (hh % 2) * c.DK
                p.wait("vector", p.sem("gp"), gqv)
                p.wait("vector", p.sems["st"], self._gqs)
                tq = self.sb["T1"].ap()[0:32, 64:64 + c.DK]

                def gq(e, hh=hh, base=base, qsel=qsel, tq=tq):
                    src = self.sb["qg"].ap()[hh * 32:hh * 32 + 32,
                                             hh * c.DK:(hh + 1) * c.DK]
                    e.transpose(tq, src)
                    e.drain()
                    e.tensor_copy(qsel[base:base + 32, :], tq[0:32, 0:8])
                    return e.tensor_copy(qsel[base + 32:base + 64, :],
                                         tq[0:32, 32:40])
                vev = p.run("vector", gq, ve)
                nm = self.sb["rows"].ap()[0:1, 2048:][0:1, 640:648]
                p.wait("vector", st, fvv)
                vev = p.run("vector", lambda e, nm=nm, hr=hr: e.tensor_scalar(
                    nm, self.sb["rows"].ap()[0:1, 2048:][0:1, 512 + hr * 8:512 + hr * 8 + 8],
                    -1.0, None, ALU.mult), ve)
                p.wait("tensor", ve, vev)
                for kt in range(c.LC):
                    bank = kt % 2
                    psx = self.psA(0, 8, rows=P) if bank == 0 \
                        else self.psA(512, 520, rows=P)
                    p.wait("tensor", se)
                    lhs = ka[base:base + c.DK, kt * P:(kt + 1) * P]
                    p.run("tensor", lambda e, lhs=lhs, psx=psx, qsel=qsel,
                          base=base: e.matmul(psx, lhs,
                                              qsel[base:base + c.DK, :],
                                              start=True, stop=False), pe)
                    pev = p.run("tensor", lambda e, psx=psx, nm=nm: e.matmul(
                        psx, orow, nm, start=False, stop=True), pe)
                    dst = expb[:, kt * 8:(kt + 1) * 8]
                    p.wait("scalar", pe, pev)
                    sev = p.run("scalar", lambda e, dst=dst, psx=psx:
                                e.activation(dst, psx, ACT.Exp, scale=0.125), se)
                pss = self.psB(1536, 1537, rows=8)
                p.wait("tensor", se, sev)
                pev = None
                for kt in range(c.LC):
                    src = expb[:, kt * 8:(kt + 1) * 8]
                    pev = p.run("tensor", lambda e, src=src, pss=pss, kt=kt:
                                e.matmul(pss, src, ones, start=(kt == 0),
                                         stop=(kt == c.LC - 1)), pe)
                s81 = self.sb["T1"].ap()[0:8, 0:1]
                p.wait("vector", pe, pev)
                vev = p.run("vector", lambda e, s81=s81, pss=pss:
                            e.tensor_copy(s81, pss), ve)
                pst8 = self.psB(1544, 1552, rows=1)
                p.wait("tensor", ve, vev)
                pev = p.run("tensor", lambda e, pst8=pst8, s81=s81, eye=eye:
                            e.transpose(pst8, s81, eye[0:8, 0:8]), pe)
                r8 = self.sb["rows"].ap()[0:1, 2048:][0:1, 656:664]
                p.wait("vector", pe, pev)

                def rc(e, r8=r8, pst8=pst8):
                    e.tensor_copy(r8, pst8)
                    e.drain()
                    return e.reciprocal(r8, r8)
                vev = p.run("vector", rc, ve)
                psb8 = self.psB(1552, 1560)
                p.wait("tensor", ve, vev)
                pev = p.run("tensor", lambda e, psb8=psb8, r8=r8: e.matmul(
                    psb8, orow, r8, start=True, stop=True), pe)
                rs8 = self.sb["T1"].ap()[:, 8:16]
                p.wait("vector", pe, pev)
                vev = p.run("vector", lambda e, rs8=rs8, psb8=psb8:
                            e.tensor_copy(rs8, psb8), ve)
                vh = self.sb["C1"].ap()[:, c.TOK:c.TOK + c.LC * c.DK]
                vsrc = self.da("v_sp").rearrange(
                    "(kt p) m -> p kt m", p=P)[:, :, hg * VW + hh * c.DK:
                                               hg * VW + (hh + 1) * c.DK]
                p.wait("sync", pe)
                p.wait("sync", st)
                vhl = p.dma("sync", vh.rearrange("p (kt d) -> p kt d", d=c.DK),
                            vsrc, ld)
                psc = self.psB(1560, 1568, rows=c.DK)
                pev = None
                p.wait("tensor", se)
                p.wait("tensor", ld, vhl)
                for kt in range(c.LC):
                    vsl = vh[:, kt * c.DK:(kt + 1) * c.DK]
                    src = expb[:, kt * 8:(kt + 1) * 8]
                    pev = p.run("tensor", lambda e, vsl=vsl, src=src, psc=psc,
                                kt=kt: e.matmul(psc, vsl, src, start=(kt == 0),
                                                stop=(kt == c.LC - 1)), pe)
                dstc = ctxb[:, hr * 8:(hr + 1) * 8]
                p.wait("vector", pe, pev)
                vev = p.run("vector", lambda e, dstc=dstc, psc=psc, rs8=rs8:
                            e.tensor_tensor(dstc, psc, rs8[0:c.DK, :], ALU.mult), ve)
        if self.debug_stop == "l1a4":
            return
        lvs = []
        p.wait("sync", ve)
        for kc in range(c.DC):
            lvs.append(p.dma("sync", self.sb[f"T{2 + kc % 8}"].ap()[:, 0:1],
                             self.da("xn1_sp")[kc * P:(kc + 1) * P,
                                               c.TOK - 1:c.TOK], ld, slow=True))
        lw = None
        for kc in range(c.DC):
            lw = p.dma("sync", wbuf[0][:, kc * c.NAV:(kc + 1) * c.NAV],
                       self.da("Wp1")[kc * P:(kc + 1) * P, :], ld)
        lb1 = p.dma("sync", self.sb["T1"].ap()[:, 16:17], self.da("bp1")[:, :], ld)
        psh = self.psB(0, 1, rows=P)
        p.wait("tensor", ld, lw)
        p.wait("tensor", ld, lvs[-1])
        p.wait("tensor", se)
        p.wait("tensor", ve)
        pev = None
        for kc in range(c.DC):
            lhs = wbuf[0][:, kc * c.NAV:(kc + 1) * c.NAV]
            rhs = self.sb[f"T{2 + kc % 8}"].ap()[:, 0:1]
            pev = p.run("tensor", lambda e, lhs=lhs, rhs=rhs, psh=psh, kc=kc:
                        e.matmul(psh, lhs, rhs, start=(kc == 0),
                                 stop=(kc == c.DC - 1)), pe)
        h1 = self.sb["T1"].ap()[:, 17:18]
        p.wait("scalar", pe, pev)
        p.wait("scalar", ld, lb1)
        sev = p.run("scalar", lambda e, h1=h1, psh=psh: e.activation(
            h1, psh, ACT.Gelu, bias=self.sb["T1"].ap()[:, 16:17]), se)
        p.wait("sync", pe)
        lw2 = p.dma("sync", self.sb["T3"].ap()[:c.NAV, 0:c.E], self.da("Wp2")[:, :], ld)
        lb2 = p.dma("sync", self.sb["T3"].ap()[:c.E, 8:9], self.da("bp2")[:, :], ld)
        psl = self.psB(512, 513, rows=c.E)
        p.wait("tensor", se, sev)
        p.wait("tensor", ld, lw2)
        pev = p.run("tensor", lambda e, psl=psl, h1=h1: e.matmul(
            psl, self.sb["T3"].ap()[:c.NAV, 0:c.E], h1, start=True,
            stop=True), pe)
        lg8 = self.sb["T3"].ap()[:c.E, 9:10]
        p.wait("scalar", pe, pev)
        p.wait("scalar", ld, lb2)
        sev = p.run("scalar", lambda e, lg8=lg8, psl=psl: e.add(
            lg8, psl, self.sb["T3"].ap()[:c.E, 8:9]), se)
        pstl = self.psB(1024, 1024 + c.E, rows=1)
        p.wait("tensor", se, sev)
        pev = p.run("tensor", lambda e, pstl=pstl, lg8=lg8, eye=eye:
                    e.transpose(pstl, lg8, eye[0:c.E, 0:c.E]), pe)
        lrow = self.sb["rows"].ap()[0:1, 2048:][0:1, 672:672 + c.E]
        vrow = self.sb["rows"].ap()[0:1, 2048:][0:1, 688:696]
        irow = self.sb["selrow"].ap()[0:1, 4 * c.HPC * 8 - 8:4 * c.HPC * 8]
        p.wait("vector", pe, pev)

        def amax(e, lrow=lrow, vrow=vrow, irow=irow, pstl=pstl):
            e.tensor_copy(lrow, pstl)
            e.drain()
            e.max(vrow, lrow)
            e.drain()
            return e.max_index(irow.bitcast(U32), vrow, lrow)
        vev = p.run("vector", amax, ve)
        selslot = self.sb["rows"].ap()[0:1, 2048:][0:1, 704:705]
        def cpsel(e, selslot=selslot, irow=irow):
            e.drain()
            return e.tensor_copy(selslot, irow[0:1, 0:1])
        vev = p.run("vector", cpsel, ve)
        if self.debug_stop == "l1a5":
            return
        nctx_ctx = c.DK * c.HPC * 8
        nctx = self._ctx_pack_len()
        zrow = self.sb["rows"].ap()[64:65, 0:512]
        vev = p.run("vector", lambda e, zrow=zrow: e.memset(zrow, 0.0), ve)
        idxf = self.sb["rows"].ap()[0:1, 2048:][0:1, 896:896 + c.HPC * 8]
        p.wait("vector", st)
        vev = p.run("vector", lambda e, idxf=idxf: e.tensor_copy(
            idxf, self.sb["selrow"].ap()[0:1, :c.HPC * 8]), ve)
        p.wait("gpsimd", ve, vev)
        p.wait("gpsimd", se)
        cin = self.da("cc_ctx_in")
        pv1 = p.dma("gpsimd", cin[0:1, 0:nctx_ctx], ctxb, st)
        pv2 = p.dma("gpsimd", cin[0:1, nctx_ctx:nctx_ctx + c.HPC * 8], idxf, st)
        pv3 = p.dma("gpsimd", cin[0:1, nctx_ctx + c.HPC * 8:
                                 nctx_ctx + c.HPC * 8 + 1], selslot, st)
        padlen = nctx - (nctx_ctx + c.HPC * 8 + 1)
        v = pv3
        off = nctx_ctx + c.HPC * 8 + 1
        while padlen > 0:
            wdt = min(512, padlen)
            v = p.dma("gpsimd", cin[0:1, off:off + wdt], zrow[0:1, :wdt], st)
            off += wdt
            padlen -= wdt
        p.wait("gpsimd", st, v)
        self._cc7 = p.run("gpsimd", lambda e: e.collective_compute(
            "AllGather", ALU.bypass, replica_groups=[list(range(8))],
            ins=[self.dram["cc_ctx_in"].ap().opt()],
            outs=[self.dram["cc_ctx_out"].ap().opt()]), cc)
        if self.debug_stop in ("l1a6", "dumppack"):
            return
        nctx = self._ctx_pack_len()
        gath = self.sb["rows"].ap()[32:40, :]
        cpick = self.sb["rows"].ap()[32:40, self.RW - 8:self.RW - 4]
        p.wait("sync", cc, self._cc7)
        lv = p.dma("sync", gath[:, :nctx], self.da("cc_ctx_out")[:, :], ld)
        p.wait("tensor", ld, lv)
        p.wait("tensor", ve)
        p.wait("tensor", se)
        packs = []
        stgrow = self.sb["rows"].ap()[64:65, 512:1024]
        for which, pickcol in ((0, 2), (1, 3)):
            spn = f"pp{which}_sp"
            pev = None
            for j in range(0, nctx, 512):
                w = min(512, nctx - j)
                psx = self.psA(j % 2048, j % 2048 + w, rows=1)
                pev = p.run("tensor", lambda e, psx=psx, j=j, w=w,
                            pickcol=pickcol: e.matmul(
                                psx, cpick[:, pickcol:pickcol + 1],
                                gath[:, j:j + w], start=True, stop=True), pe)
                p.wait("vector", pe, pev)
                p.wait("vector", st)
                vev = p.run("vector", lambda e, psx=psx, w=w, stgrow=stgrow:
                            e.tensor_copy(stgrow[0:1, :w], psx), ve)
                p.wait("gpsimd", ve, vev)
                p.dma("gpsimd", self.da(spn)[0:1, j:j + w], stgrow[0:1, :w], st)
                p.wait("tensor", ve, vev)
            packs.append(spn)
        ctxs = self.sb["ctxs"].ap()
        NIT = 2 * c.HPC * 8
        fr = self.sb["rows"].ap()[0:1, 2048:]
        ltrow_f = fr[0:1, 1024:1024 + NIT]
        nctx_ctx2 = c.DK * c.HPC * 8
        for sidx in range(2):
            spn = packs[sidx]
            p.wait("gpsimd", ve)
            p.wait("gpsimd", st)
            cvv = p.dma("gpsimd", ctxs[:, sidx * c.HPC * 8:(sidx + 1) * c.HPC * 8],
                        self.da(spn)[0:1, 0:nctx_ctx2], st)
            idxrow = self.sb[f"idxr{sidx}"].ap()[0:1, :c.HPC * 8]
            p.wait("sync", st)
            iv2 = p.dma("sync", idxrow,
                        self.da(spn)[0:1, nctx_ctx2:nctx_ctx2 + c.HPC * 8], ld)
            p.wait("vector", ld, iv2)
            ltf = ltrow_f[0:1, sidx * c.HPC * 8:(sidx + 1) * c.HPC * 8]

            def mklt(e, idxrow=idxrow, ltf=ltf):
                toff = self.sb["crole"].ap()[0:1, 1:2]
                return e.scalar_tensor_tensor(
                    ltf, idxrow, 0.0, toff.to_broadcast([1, c.HPC * 8]),
                    ALU.add, ALU.subtract)
            vev = p.run("vector", mklt, ve)
            if c.U < 8:
                def drop(e, ltf=ltf):
                    r = None
                    for h in range(c.HPC):
                        r = e.memset(ltf[0:1, h * 8 + c.U:h * 8 + 8], -1.0)
                    return r
                vev = p.run("vector", drop, ve)
        psltc = self.psB(1024, 1025, rows=NIT)
        eye = self.sb["consts"].ap()
        p.wait("tensor", ve, vev)
        pev = p.run("tensor", lambda e, psltc=psltc, ltrow_f=ltrow_f, eye=eye:
                    e.transpose(psltc, ltrow_f, eye[0:1, 0:1]), pe)
        ltcol = self.sb["sm"].ap()[:NIT, 54:55]
        p.wait("vector", pe, pev)
        p.wait("vector", st)
        vev = p.run("vector", lambda e, ltcol=ltcol, psltc=psltc:
                    e.tensor_copy(ltcol, psltc), ve)

        def pcols(ci):
            return self.sb[f"T{2 + ci}"].ap()[:, :NIT]

        def pcolsT(ci):
            return self.sb["rows"].ap()[0:NIT, 5120 + ci * P:5120 + (ci + 1) * P]
        def memz(e):
            r = None
            for ci in range(c.DC):
                r = e.memset(pcols(ci), 0.0)
            return r
        vev2 = p.run("vector", memz, ve)

        def asm(e):
            r = None
            for sidx in range(2):
                for h in range(c.HPC):
                    ghead = sidx * c.HPC + h
                    drow = ghead * c.DK
                    ci, bs = drow // P, drow % P
                    cols = slice((sidx * c.HPC + h) * 8, (sidx * c.HPC + h) * 8 + 8)
                    r = e.tensor_copy(pcols(ci)[bs:bs + c.DK, cols],
                                      ctxs[:, cols])
            return r
        vev = p.run("vector", asm, ve)
        p.wait("tensor", ve, vev)
        for ci in range(c.DC):
            pst = self.psB(0, P, rows=NIT)
            pev = p.run("tensor", lambda e, pst=pst, ci=ci, eye=eye:
                        e.transpose(pst, pcols(ci), eye), pe)
            p.wait("vector", pe, pev)
            vev = p.run("vector", lambda e, pst=pst, ci=ci:
                        e.tensor_copy(pcolsT(ci), pst), ve)
            p.wait("tensor", ve, vev)
        iota = self.da("iotarow")
        SLW = min(512, c.TOK)
        iosb = self.sb["T0"].ap()[0:1, :SLW]
        for sl in range(c.TOK // SLW):
            ls, lw = sl * SLW, SLW
            lio = p.dma("sync", iosb, iota[0:1, ls:ls + lw], ld)
            psio = self.psB(512, 512 + lw)
            p.wait("tensor", ld, lio)
            p.wait("tensor", ve)
            pev = p.run("tensor", lambda e, psio=psio, iosb=iosb: e.matmul(
                psio, self.sb["conesrow"].ap(), iosb, start=True, stop=True), pe)
            oh = self.sb["qg"].ap()[:, :lw]
            p.wait("vector", pe, pev)
            vev = p.run("vector", lambda e, oh=oh, psio=psio, ltcol=ltcol:
                        e.tensor_scalar(oh[:NIT, :], psio[:NIT, :], ltcol, None,
                                        ALU.is_equal), ve)
            p.wait("tensor", ve, vev)
            for ci in range(c.DC):
                psp = self.psA(0, lw) if ci % 2 == 0 else self.psA(512, 512 + lw)
                p.wait("tensor", se)
                pev = p.run("tensor", lambda e, psp=psp, ci=ci, oh=oh: e.matmul(
                    psp, pcolsT(ci)[:NIT, :], oh[:NIT, :], start=True,
                    stop=True), pe)
                dst = self.pslot(ci)[:, ls:ls + lw]
                p.wait("scalar", pe, pev)
                sev = p.run("scalar", lambda e, dst=dst, psp=psp:
                            e.copy(dst, psp), se)
        p.wait("tensor", se, sev)
        wbufs = [self.sb["WB0"].ap(), self.sb["WB1"].ap()]
        p.wait("sync", ve, vev)
        for tt in range(c.NTT):
            lo, hi = tt * c.TT, (tt + 1) * c.TT
            wld = [None, None]
            p.wait("sync", pe)
            wld[0] = self.colslice_load("woT", 0, wbufs[0], c.DC, ld)
            if c.DC > 1:
                wld[1] = self.colslice_load("woT", 1, wbufs[1], c.DC, ld)
            for oc in range(c.DC):
                b = oc % 2
                ps = self.psA(0, c.TT) if b == 0 else self.psA(512, 512 + c.TT)
                p.wait("tensor", ld, wld[b])
                p.wait("tensor", ve)
                pev = None
                for kc in range(c.DC):
                    lhs = wbufs[b][:, kc * P:(kc + 1) * P]
                    rhs = self.pslot(kc)[:, lo:hi]
                    pev = p.run("tensor", lambda e, lhs=lhs, rhs=rhs, ps=ps,
                                kc=kc: e.matmul(ps, lhs, rhs, start=(kc == 0),
                                                stop=(kc == c.DC - 1)), pe)
                if oc + 2 < c.DC:
                    p.wait("sync", pe, pev)
                    wld[b] = self.colslice_load("woT", oc + 2, wbufs[b], c.DC, ld)
                at = self.sb[f"acc{oc}"].ap()[:, lo:hi]
                bo = self.da("bo_t")[:, oc:oc + 1]
                bos = self.sb["sm"].ap()[:, 50:51]
                p.wait("sync", ve)
                lb = p.dma("sync", bos, bo, ld, slow=True)
                p.wait("vector", pe, pev)
                p.wait("vector", ld, lb)
                vev = p.run("vector", lambda e, at=at, ps=ps, bos=bos:
                            e.scalar_tensor_tensor(at, ps, bos, at, ALU.add,
                                                   ALU.add), ve)

    def ph_l1_ffn(self):
        c, p = self.cfg, self.p
        ld, pe, ve, se, st, cc = (p.sem(n) for n in
                                  ("ld", "pe", "ve", "se", "st", "cc"))
        C0 = self.sb["C0"].ap()
        C1 = self.sb["C1"].ap()
        nctx_ctx = c.DK * c.HPC * 8
        selr = self.sb["selr1"].ap()[0:1, 0:1]
        p.wait("sync", st)
        lv = p.dma("sync", selr,
                   self.da("pp1_sp")[0:1, nctx_ctx + c.HPC * 8:
                                     nctx_ctx + c.HPC * 8 + 1], ld)
        p.wait("tensor", ld, lv)
        if DEBUG_FORCE_SEL0:
            p.wait("vector", ld, lv)
            p.run("vector", lambda e: e.memset(selr, 0.0), p.sems["ve"])
            p.wait("tensor", p.sems["ve"])
        from concourse.bass_core_rust import IndirectOffsetOnAxis as IOA
        orow = self.sb["conesrow"].ap()
        selps = self.psB(1544, 1545)
        pev = p.run("tensor", lambda e: e.matmul(selps, orow, selr,
                                                 start=True, stop=True), pe)
        selc = self.sb["sm"].ap()[:, 60:61]
        idxf = self.sb["sm"].ap()[:, 61:62]
        gidx = self.sb["gidxc"].ap()
        iop = self.sb["ciotaT"].ap()[:, 0:1]
        p.wait("vector", pe, pev)
        vev = p.run("vector", lambda e: e.tensor_copy(selc, selps), ve)

        W1f = self.da("W1").rearrange("e d f -> (e d) f")
        W2f = self.da("W2").rearrange("e f d -> (e f) d")
        b1f = self.da("b1_t").rearrange("e p f -> (e p) f")
        b2f = self.da("b2_t").rearrange("e p d -> (e p) d")
        wb0 = self.sb["WB0"].ap()
        b1s = self.sb["bffn"].ap()[:, :c.FC]
        b2s = self.sb["bffn2"].ap()[:, :]
        p.wait("gpsimd", pe)
        p.wait("gpsimd", se)
        p.wait("gpsimd", ld)

        def gather_rows(srcf, nr, kc, dst_sb):
            def mkidx(e, kc=kc, nr=nr):
                e.drain()
                e.tensor_scalar(idxf, selc, float(nr), float(kc * P),
                                ALU.mult, ALU.add)
                e.drain()
                e.tensor_tensor(idxf, idxf, iop, ALU.add)
                e.drain()
                return e.tensor_copy(gidx, idxf)
            mv = p.run("vector", mkidx, ve)
            p.wait("gpsimd", ve, mv)
            gsv = p.run("gpsimd", lambda e, dst_sb=dst_sb, srcf=srcf:
                        e.indirect_dma_start(dst_sb, None, srcf,
                                             IOA(gidx, 0)), st, k=16)
            gv = p.run("gpsimd", lambda e: e.drain(), p.sem("gp"))
            p.wait("vector", p.sem("gp"), gv)
            p.wait("vector", st, gsv)
            return (gv, gsv)

        vw1 = vw2 = None
        for (srcf, nrch, width, spname) in ((W1f, c.DC, c.F, "w1_sp"),
                                            (W2f, c.FC, c.D, "w2_sp")):
            for kc in range(nrch):
                gv, gsv = gather_rows(srcf, nrch * P, kc, wb0[:, :width])
                p.wait("sync", p.sem("gp"), gv)
                p.wait("sync", st, gsv)
                lv = p.dma("sync", self.da(spname)[kc * P:(kc + 1) * P, :],
                           wb0[:, :width], ld)
                p.wait("gpsimd", ld, lv)
            if spname == "w1_sp":
                vw1 = lv
            else:
                vw2 = lv
        v1, v1s = gather_rows(b1f, P, 0, b1s)
        v2, v2s = gather_rows(b2f, P, 0, b2s)
        if self.debug_stop == "l1f1":
            return
        wbufs = [self.sb["WB0"].ap(), self.sb["WB1"].ap()]
        arenas = [self.sb["C1"].ap(), self.sb["bigA0"].ap(),
                  self.sb["bigA1"].ap()]
        per = (2 * c.TOK) // c.TT

        def hslot(i):
            a, sidx = divmod(i, per)
            return arenas[a][:, sidx * c.TT:(sidx + 1) * c.TT]

        for tt in range(c.NTT):
            if self.debug_stop == f"l1f2_{tt}":
                return
            lo, hi = tt * c.TT, (tt + 1) * c.TT
            p.wait("sync", pe)
            p.wait("sync", ve)
            xl = None
            for i in range(c.DC):
                xl = p.dma("sync", C0[:, i * c.TT:(i + 1) * c.TT],
                           self.da("xn1_sp")[i * P:(i + 1) * P, lo:hi], ld)
            for oc in range(c.FC):
                b = oc % 2
                p.wait("sync", pe)
                if tt == 0 and oc == 0:
                    p.wait("sync", ld, vw1)
                wl = self.colslice_load("w1_sp", oc, wbufs[b], c.DC, ld)
                ps = self.psA(0, c.TT) if b == 0 else self.psA(512, 512 + c.TT)
                p.wait("tensor", ld, wl)
                p.wait("tensor", ld, xl)
                p.wait("tensor", se)
                p.wait("tensor", ve)
                pev = None
                for kc in range(c.DC):
                    lhs = wbufs[b][:, kc * P:(kc + 1) * P]
                    rhs = C0[:, kc * c.TT:(kc + 1) * c.TT]
                    pev = p.run("tensor", lambda e, lhs=lhs, rhs=rhs, ps=ps,
                                kc=kc: e.matmul(ps, lhs, rhs, start=(kc == 0),
                                                stop=(kc == c.DC - 1)), pe)
                dst = hslot(oc)
                p.wait("scalar", pe, pev)
                p.wait("scalar", p.sem("gp"), v1)
                if tt == 0 and oc == 0:
                    p.wait("scalar", st, v1s)
                sev = p.run("scalar", lambda e, dst=dst, ps=ps, oc=oc, b1s=b1s:
                            e.activation(dst, ps, ACT.Gelu,
                                         bias=b1s[:, oc:oc + 1]), se)
            rbase = c.FC

            def rslot(i):
                return hslot(rbase + i)
            for oc in range(c.DC):
                b = oc % 2
                p.wait("sync", pe)
                if tt == 0 and oc == 0:
                    p.wait("sync", ld, vw2)
                wl = self.colslice_load("w2_sp", oc, wbufs[b], c.FC, ld)
                ps = self.psB(0, c.TT) if b == 0 else self.psB(512, 512 + c.TT)
                p.wait("tensor", ld, wl)
                p.wait("tensor", se)
                pev = None
                for kc in range(c.FC):
                    lhs = wbufs[b][:, kc * P:(kc + 1) * P]
                    pev = p.run("tensor", lambda e, lhs=lhs, kc=kc, ps=ps:
                                e.matmul(ps, lhs, hslot(kc), start=(kc == 0),
                                         stop=(kc == c.FC - 1)), pe)
                    dst = rslot(oc)
                p.wait("scalar", pe, pev)
                p.wait("scalar", p.sem("gp"), v2)
                if tt == 0 and oc == 0:
                    p.wait("scalar", st, v2s)
                sev = p.run("scalar", lambda e, dst=dst, ps=ps, oc=oc, b2s=b2s:
                            e.add(dst, ps, b2s[:, oc:oc + 1]), se)
            wld = [None, None]
            p.wait("sync", pe)
            p.wait("sync", se)
            wld[0] = self.colslice_load("Wo1", 0, wbufs[0], c.DC, ld)
            if c.DC > 1:
                wld[1] = self.colslice_load("Wo1", 1, wbufs[1], c.DC, ld)
            for oc in range(c.DC):
                b = oc % 2
                ps = self.psA(0, c.TT) if b == 0 else self.psA(512, 512 + c.TT)
                p.wait("tensor", ld, wld[b])
                p.wait("tensor", ve)
                p.wait("tensor", se)
                pev = None
                for kc in range(c.DC):
                    lhs = wbufs[b][:, kc * P:(kc + 1) * P]
                    pev = p.run("tensor", lambda e, lhs=lhs, kc=kc, ps=ps:
                                e.matmul(ps, lhs, rslot(kc), start=(kc == 0),
                                         stop=(kc == c.DC - 1)), pe)
                if oc + 2 < c.DC:
                    p.wait("sync", pe, pev)
                    wld[b] = self.colslice_load("Wo1", oc + 2, wbufs[b], c.DC, ld)
                at = self.sb[f"acc{oc}"].ap()[:, lo:hi]
                p.wait("vector", pe, pev)
                vev = p.run("vector", lambda e, at=at, ps=ps:
                            e.scalar_tensor_tensor(at, ps, 0.3, at, ALU.mult,
                                                   ALU.add), ve)
        for i in range(c.DC):
            bos = self.sb["sm"].ap()[:, 51:52]
            p.wait("sync", ve)
            lb = p.dma("sync", bos, self.da("bo1_t")[:, i:i + 1], ld, slow=True)
            at = self.sb[f"acc{i}"].ap()
            p.wait("vector", ld, lb)
            p.wait("vector", pe)

            def bo1add(e, at=at, bos=bos):
                e.tensor_scalar(self.sb["sm"].ap()[:, 52:53], bos, 0.3, None,
                                ALU.mult)
                e.drain()
                return e.tensor_scalar(at[:, :], at[:, :],
                                       self.sb["sm"].ap()[:, 52:53], None,
                                       ALU.add)
            vev = p.run("vector", bo1add, ve)

    def fin_or_dump(self):
        if getattr(self, "_dump", None):
            return self.finish_dump(self._dump)
        return self.finish()

    def finish_dump(self, spec):
        c, p = self.cfg, self.p
        st = p.sem("st")
        p.wait("gpsimd", p.sem("cc"))
        for eng, sname in (("gpsimd", "ve"), ("gpsimd", "se"), ("gpsimd", "pe")):
            p.wait(eng, p.sem(sname))
        outrow = 0
        v = None
        for part in spec.split(","):
            if "@" in part:
                name, rr = part.split("@")
                lo, hi = (int(t) for t in rr.split(":"))
            else:
                name, lo, hi = part, 0, None
            src = self.da(name)
            nrow, ncol = src.shape[0], src.shape[1]
            if hi is None:
                hi = nrow
            per = (ncol + c.TOK - 1) // c.TOK
            for r in range(lo, hi):
                for j in range(per):
                    w = min(c.TOK, ncol - j * c.TOK)
                    v = p.dma("gpsimd",
                              self.da("out_fm")[outrow:outrow + 1, :w],
                              src[r:r + 1, j * c.TOK:j * c.TOK + w], st)
                    outrow += 1
        assert outrow <= c.D, f"dump too big: {outrow}"
        p.wait("gpsimd", st, v)
        self.p.emit()
        return self.nc

    def finish(self):
        c, p = self.cfg, self.p
        st = p.sem("st")
        for eng, s in (("gpsimd", "ve"), ("gpsimd", "se"), ("gpsimd", "pe")):
            p.wait(eng, p.sem(s))
        v = None
        for i in range(c.DC):
            v = p.dma("gpsimd", self.da("out_fm")[i * P:(i + 1) * P, :],
                      self.sb[f"acc{i}"].ap()[:, :], st)
        p.wait("gpsimd", st, v)
        self.p.emit()
        return self.nc



def host_inmaps(cfg: Cfg, inputs: dict) -> list:
    c = cfg
    f32 = lambda a: np.ascontiguousarray(a, dtype=np.float32)
    tr = lambda a: f32(np.asarray(a).T)
    nat = lambda a: f32(np.asarray(a))

    shared = {
        "wqkvT": tr(inputs["L0_attn_w_qkv"]),
        "wswT": tr(inputs["L0_attn_w_swiglu"]),
        "woutT": tr(inputs["L0_attn_w_out"]),
        "wupT": tr(inputs["L0_ffn_w_up"]),
        "wgateT": tr(inputs["L0_ffn_w_gate"]),
        "wdownT": tr(inputs["L0_ffn_w_down"]),
        "wup_n": nat(inputs["L0_ffn_w_up"]),
        "wgate_n": nat(inputs["L0_ffn_w_gate"]),
        "wdown_n": nat(inputs["L0_ffn_w_down"]),
        "woT": tr(inputs["L1_attn_wo"]),
        "bo_t": f32(np.asarray(inputs["L1_attn_bo"]).reshape(c.DC, P).T),
        "Wp1": nat(inputs["L1_ffn_Wp1"]),
        "bp1": f32(np.asarray(inputs["L1_ffn_bp1"]).reshape(c.NAV, 1)),
        "Wp2": nat(inputs["L1_ffn_Wp2"]),
        "bp2": f32(np.asarray(inputs["L1_ffn_bp2"]).reshape(c.E, 1)),
        "W1": nat(inputs["L1_ffn_W1"]),
        "b1_t": f32(np.asarray(inputs["L1_ffn_b1"]).reshape(c.E, c.FC, P)
                    .transpose(0, 2, 1)),
        "W2": nat(inputs["L1_ffn_W2"]),
        "b2_t": f32(np.asarray(inputs["L1_ffn_b2"]).reshape(c.E, c.DC, P)
                    .transpose(0, 2, 1)),
        "Wo1": nat(inputs["L1_ffn_Wo"]),
        "bo1_t": f32(np.asarray(inputs["L1_ffn_bo"]).reshape(c.DC, P).T),
        "ones": np.ones((P, 1), np.float32),
        "onesrow": np.ones((1, P), np.float32),
        "eye": np.eye(P, dtype=np.float32),
        "iotaT": f32(np.arange(c.T, dtype=np.float32).reshape(c.LC, P).T),
        "iotarow": f32(np.arange(c.TOK, dtype=np.float32).reshape(1, c.TOK)),
    }

    x = np.asarray(inputs["L0_attn_x"])
    wqT = tr(inputs["L1_attn_wq"])
    wkT = tr(inputs["L1_attn_wk"])
    wvT = tr(inputs["L1_attn_wv"])
    bq = np.asarray(inputs["L1_attn_bq"], dtype=np.float32)
    bk = np.asarray(inputs["L1_attn_bk"], dtype=np.float32)
    bv = np.asarray(inputs["L1_attn_bv"], dtype=np.float32)
    MH = c.HPC * c.DK
    maps = []
    for core in range(8):
        b, h = core // 2, core % 2
        m = dict(shared)
        m["x_fm"] = f32(x[b, h * c.TOK:(h + 1) * c.TOK, :].T)
        hsl = slice(h * MH, (h + 1) * MH)
        m["wqT_my"] = f32(wqT[:, hsl])
        m["wkT_my"] = f32(wkT[:, hsl])
        m["wvT_my"] = f32(wvT[:, hsl])
        m["bq_my_t"] = f32(bq[hsl].reshape(MH // P, P).T)
        m["bk_my_t"] = f32(bk[hsl].reshape(MH // P, P).T)
        m["bv_my_row"] = f32(bv[hsl].reshape(1, MH))
        m["bq_my_row"] = f32(bq[hsl].reshape(1, MH))
        pc = np.zeros((8, 1), np.float32)
        if h == 1:
            pc[2 * b, 0] = 1.0
        psel = np.zeros((8, 1), np.float32)
        psel[2 * b + 1, 0] = 1.0
        pe_ = np.zeros((8, 1), np.float32)
        pe_[2 * b, 0] = 1.0
        po_ = np.zeros((8, 1), np.float32)
        po_[2 * b + 1, 0] = 1.0
        m["pick_carry"] = pc
        m["pick_sel"] = psel
        m["pick_even"] = pe_
        m["pick_odd"] = po_
        m["role"] = np.array([[float(h), float(h * c.TOK), float(core),
                               float(core ^ 1), 0, 0, 0, 0]], np.float32)
        maps.append(m)
    return maps


def host_gather(cfg: Cfg, results) -> np.ndarray:
    c = cfg
    out = np.empty((4, c.T, c.D), np.float32)
    for core in range(8):
        b, h = core // 2, core % 2
        out[b, h * c.TOK:(h + 1) * c.TOK, :] = results[core]["out_fm"].T
    return out


def build_nc(cfg: Cfg = CFG, debug_stop=None):
    m1 = MK(cfg, debug_stop=debug_stop)
    m1.build()
    m2 = MK(cfg, debug_stop=debug_stop, waited=m1.p.rec_waited)
    return m2.build()


def kernel(**inputs):
    cfg = CFG
    nc = build_nc(cfg)
    in_maps = host_inmaps(cfg, inputs)
    res = run_bass_kernel_spmd(nc, in_maps, core_ids=list(range(8)))
    return host_gather(cfg, res.results)


# version 06 (final)
# speedup vs baseline: 1.0729x; 1.0322x over previous
import math
from dataclasses import dataclass

import numpy as np

import concourse.bass as bass
import concourse.mybir as mybir
from concourse.bass_utils import run_bass_kernel_spmd

F32 = mybir.dt.float32
U32 = mybir.dt.uint32
ALU = mybir.AluOpType
ACT = mybir.ActivationFunctionType
AX = mybir.AxisListType

P = 128
DEBUG_FORCE_SEL0 = False
DEBUG_RACES = False
DEBUG_DRAIN_ALL = False


class _DrainEng:

    def __init__(self, e):
        self._e = e

    def __getattr__(self, n):
        a = getattr(self._e, n)
        if not callable(a) or n in ("drain", "wait_ge"):
            return a

        def w(*args, **kw):
            r = a(*args, **kw)
            self._e.drain()
            return r
        return w


@dataclass(frozen=True)
class Cfg:
    D: int = 1024
    TOK: int = 2048
    TT: int = 512
    H: int = 16
    DK: int = 64
    E: int = 8
    NAV: int = 128
    QBLK: int = 32

    @property
    def DC(self):
        return self.D // P

    @property
    def T(self):
        return 2 * self.TOK

    @property
    def FI(self):
        return 3 * self.D

    @property
    def FIC(self):
        return self.FI // P

    @property
    def SW(self):
        return 2 * self.D

    @property
    def SWC(self):
        return self.SW // P

    @property
    def F(self):
        return 2 * self.D

    @property
    def FC(self):
        return self.F // P

    @property
    def NTT(self):
        return self.TOK // self.TT

    @property
    def HPC(self):
        return self.H // 2

    @property
    def U(self):
        return min(self.T, max(1, int(math.log(self.T))))

    @property
    def KN(self):
        return min(512, self.T)

    @property
    def NKT(self):
        return self.T // self.KN

    @property
    def LC(self):
        return self.T // P


CFG = Cfg()


class Sem:
    def __init__(self, h, name=""):
        self.h = h
        self.name = name
        self.n = 0
        self.is_dma = False
        self.guarded = {}

    def bump(self, k):
        if k == 16:
            self.is_dma = True
        self.n += k
        return self.n


class Prog:
    ENGINES = ("sync", "tensor", "vector", "scalar", "gpsimd")

    def __init__(self, nc, waited=None):
        self.nc = nc
        self.ops = {e: [] for e in self.ENGINES}
        self._sem_cms = []
        self.sems = {}
        self.rec_waited = {}
        self.waited = waited or {}

    def sem(self, name) -> Sem:
        if name not in self.sems:
            cm = self.nc.semaphore(name)
            h = cm.__enter__()
            self._sem_cms.append(cm)
            self.sems[name] = Sem(h, name)
        return self.sems[name]

    def on(self, engine, fn):
        self.ops[engine].append(fn)

    def wait(self, engine, sem: Sem, val=None):
        v = sem.n if val is None else val
        if v > 0:
            if sem.is_dma:
                self.rec_waited.setdefault(sem.name, set()).add(v)
            self.on(engine, lambda e: e.wait_ge(sem.h, v))

    def _dma_guard(self, engine, sem: Sem):
        w = sem.n
        if (w in self.waited.get(sem.name, ()) and
                w > sem.guarded.get(engine, 0)):
            self.on(engine, lambda e: e.wait_ge(sem.h, w))
            sem.guarded[engine] = w

    def dma(self, engine, out, in_, sem: Sem, slow=False):
        self._dma_guard(engine, sem)
        v = sem.bump(16)
        if slow:
            self.on(engine, lambda e: e.dma_start(
                out=out, in_=in_, allow_slow_non_contiguous=True).then_inc(sem.h, 16))
        else:
            self.on(engine, lambda e: e.dma_start(out=out, in_=in_).then_inc(sem.h, 16))
        return v

    def dma_lazy(self, engine, out, in_fn, sem: Sem):
        self._dma_guard(engine, sem)
        v = sem.bump(16)
        self.on(engine, lambda e: e.dma_start(out=out, in_=in_fn()).then_inc(
            sem.h, 16))
        return v

    def run(self, engine, fn, sem: Sem = None, k=1):
        if DEBUG_DRAIN_ALL and engine in ("vector", "scalar"):
            inner = fn
            fn = lambda e, _f=inner: _f(_DrainEng(e))
        if sem is None:
            self.on(engine, fn)
            return None
        if k == 16:
            self._dma_guard(engine, sem)
        v = sem.bump(k)
        self.on(engine, lambda e: fn(e).then_inc(sem.h, k))
        return v

    def emit(self):
        with self.nc.Block() as blk:
            for name in self.ENGINES:
                ops = self.ops[name]

                def make(ops):
                    def body(eng):
                        for f in ops:
                            f(eng)
                    return body

                getattr(blk, name)(make(ops))


class MK:
    def __init__(self, cfg: Cfg, debug_stop=None, waited=None):
        self.cfg = cfg
        self.debug_stop = debug_stop
        self.nc = bass.Bass("TRN2", target_bir_lowering=False,
                            detect_race_conditions=DEBUG_RACES)
        self.p = Prog(self.nc, waited=waited)
        self.dram = {}
        self.sb = {}
        self.ps = {}
        self._declare_io()
        self._alloc_onchip()

    def dr(self, name, shape, dtype=F32, kind="Internal"):
        t = self.nc.dram_tensor(name, list(shape), dtype, kind=kind)
        self.dram[name] = t
        return t

    def da(self, name):
        return self.dram[name].ap()

    def sbuf(self, name, shape, dtype=F32):
        t = self.nc.alloc_sbuf_tensor(name, list(shape), dtype)
        self.sb[name] = t
        return t

    def _declare_io(self):
        c = self.cfg
        D, TOK, FI = c.D, c.TOK, c.FI
        inp = lambda n, s, dt=F32: self.dr(n, s, dt, kind="ExternalInput")
        inp("x_fm", [D, TOK])
        self.dr("out_fm", [D, TOK], kind="ExternalOutput")
        inp("wqkvT", [D, 3 * D])
        inp("wswT", [D, c.SW])
        inp("woutT", [D, D])
        inp("wupT", [D, FI])
        inp("wgateT", [D, FI])
        inp("wdownT", [FI, D])
        inp("wup_n", [FI, D])
        inp("wgate_n", [FI, D])
        inp("wdown_n", [D, FI])
        inp("woT", [D, D])
        inp("bo_t", [P, c.DC])
        MH = c.HPC * c.DK
        inp("wqT_my", [D, MH])
        inp("wkT_my", [D, MH])
        inp("wvT_my", [D, MH])
        inp("bq_my_t", [P, MH // P])
        inp("bk_my_t", [P, MH // P])
        inp("bv_my_row", [1, MH])
        inp("bq_my_row", [1, MH])
        inp("pick_even", [8, 1])
        inp("pick_odd", [8, 1])
        inp("Wp1", [D, c.NAV])
        inp("bp1", [c.NAV, 1])
        inp("Wp2", [c.NAV, c.E])
        inp("bp2", [c.E, 1])
        inp("W1", [c.E, D, c.F])
        inp("b1_t", [c.E, P, c.FC])
        inp("W2", [c.E, c.F, D])
        inp("b2_t", [c.E, P, c.DC])
        inp("Wo1", [D, D])
        inp("bo1_t", [P, c.DC])
        inp("ones", [P, 1])
        inp("onesrow", [1, P])
        inp("eye", [P, P])
        inp("pick_carry", [8, 1])
        inp("pick_sel", [8, 1])
        inp("role", [1, 8])
        inp("iotaT", [P, c.LC])
        inp("iotarow", [1, TOK])

        self.dr("xn_sp", [D, TOK])
        self.dr("xh_sp", [D, TOK])
        self.dr("xq_sp", [D, TOK])
        self.dr("wupdq_sp", [D, FI])
        self.dr("wgatedq_sp", [D, FI])
        self.dr("v_sp", [2 * TOK, c.HPC * c.DK])
        self.dr("w1_sp", [D, c.F])
        self.dr("w2_sp", [c.F, D])
        self.dr("pp0_sp", [1, self._ctx_pack_len()])
        self.dr("pp1_sp", [1, self._ctx_pack_len()])
        self.dr("qkv_sp", [3 * D, TOK])
        self.dr("gate_sp", [D, TOK])
        self.dr("w_sp", [D, TOK])
        self.dr("kv_sp", [D, TOK])
        self.dr("upraw_sp", [FI, TOK])
        self.dr("gateraw_sp", [FI, TOK])
        self.dr("oraw_sp", [FI, TOK])
        self.dr("wdowndq_sp", [FI, D])
        self.dr("xn1_sp", [D, TOK])
        self.dr("qtm_sp", [(c.HPC // min(2, c.HPC)) * 2 * TOK, min(2, c.HPC) * c.DK])
        self.dr("dbg_y", [D, TOK])
        self.dr("dbg_z", [c.SW, TOK])
        self.dr("dbg_u", [D, TOK])
        self.dr("dbg_s", [D, TOK])
        for nm in ("max1", "max23", "max4", "max5"):
            self.dr(f"cc_{nm}_in", [1, 8])
            self.dr(f"cc_{nm}_out", [1, 8])
        self.dr("cc_carry_in", [1, 2 * D])
        self.dr("cc_carry_out", [8, 2 * D])
        self.dr("cc_xn1_in", [D, TOK])
        self.dr("cc_xn1_out", [2 * D, TOK])
        nctx = self._ctx_pack_len()
        self.dr("cc_ctx_in", [1, nctx])
        self.dr("cc_ctx_out", [8, nctx])

    def _ctx_pack_len(self):
        c = self.cfg
        n = c.HPC * 8 * c.DK + c.HPC * 8 + 8
        return ((n + 511) // 512) * 512

    def _alloc_onchip(self):
        c = self.cfg
        for i in range(c.DC):
            self.sbuf(f"acc{i}", [P, c.TOK])
        for i in range(2):
            self.sbuf(f"bigA{i}", [P, 2 * c.TOK])
        for i in range(2):
            self.sbuf(f"C{i}", [P, 2 * c.TOK])
        for i in range(2):
            self.sbuf(f"WB{i}", [P, min(2048, c.FI)])
        for i in range(10):
            self.sbuf(f"T{i}", [P, max(512, c.TT)])
        RW = max(6656, 2 * c.TOK + 512, self._ctx_pack_len() + 2048)
        self.sbuf("rows", [P, RW])
        self.RW = RW
        self.sbuf("sm", [P, 64])
        self.sbuf("selv", [c.HPC, 8])
        self.sbuf("seli", [c.HPC, 8], U32)
        self.sbuf("consts", [P, P])
        self.sbuf("cones", [P, 1])
        self.sbuf("conesrow", [1, P])
        self.sbuf("cpick", [8, 4])
        self.sbuf("crole", [1, 8])
        self.sbuf("ciotaT", [P, c.LC])
        self.sbuf("qg", [P, 512])
        self.sbuf("gidxc", [P, 1], mybir.dt.int32)
        self.sbuf("qsel", [P, 8])
        self.sbuf("expb", [P, c.LC * 8])
        self.sbuf("ctxb", [c.DK, c.HPC * 8])
        self.sbuf("selrow", [1, 4 * c.HPC * 8], mybir.dt.int32)
        self.sbuf("ctxs", [c.DK, 2 * c.HPC * 8])
        self.sbuf("bffn", [P, c.FC])
        self.sbuf("bffn2", [P, c.DC])
        self.sbuf("qint", [P, 512], mybir.dt.int32)
        self.sbuf("chalf", [P, 1])
        self.sbuf("idxr0", [1, 64])
        self.sbuf("idxr1", [1, 64])
        self.sbuf("selr1", [1, 4])
        self.sbuf("s32g0", [c.HPC, 8])
        self.sbuf("csum", [P, 2 * c.DC])
        self.sbuf("csum2", [P, 4 * c.DC])
        self.ps["A"] = self.nc.alloc_psum_tensor("psA", [P, 2048], F32)
        self.ps["B"] = self.nc.alloc_psum_tensor("psB", [P, 2048], F32)

    def row(self, idx, lo, hi):
        return self.sb["rows"].ap()[idx:idx + 1, lo:hi]

    def rows2(self, r0, r1, lo, hi):
        return self.sb["rows"].ap()[r0:r1, lo:hi]

    def bigA(self, chunk, lo=0, hi=None):
        c = self.cfg
        t = self.sb[f"bigA{chunk // 2}"]
        off = (chunk % 2) * c.TOK
        hi = c.TOK if hi is None else hi
        return t[:, off + lo: off + hi]

    def tslot(self, j, w=None):
        w = (self.cfg.TT if w is None else w)
        return self.sb[f"T{2 + j}"].ap()[:, :w]

    def aslot(self, j, w=None):
        c = self.cfg
        w = c.TT if w is None else w
        spt = (2 * c.TOK) // w
        return self.sb[f"bigA{j // spt}"].ap()[:, (j % spt) * w:(j % spt + 1) * w]

    def cslot(self, which, j, w=None):
        c = self.cfg
        w = c.TT if w is None else w
        return self.sb[f"C{which}"].ap()[:, j * w:(j + 1) * w]

    def oslot(self, j, w=None):
        c = self.cfg
        w = c.TT if w is None else w
        spt = (2 * c.TOK) // w
        ar = [self.sb["C0"], self.sb["C1"], self.sb["bigA0"], self.sb["bigA1"]]
        return ar[j // spt].ap()[:, (j % spt) * w:(j % spt + 1) * w]

    def pslot(self, ch):
        c = self.cfg
        ar = [self.sb["C0"], self.sb["C1"], self.sb["bigA0"], self.sb["bigA1"]]
        return ar[ch // 2].ap()[:, (ch % 2) * c.TOK:(ch % 2 + 1) * c.TOK]

    def psA(self, lo, hi, rows=P):
        return self.ps["A"].ap()[:rows, lo:hi]

    def psB(self, lo, hi, rows=P):
        return self.ps["B"].ap()[:rows, lo:hi]

    def build(self):
        ds = self.debug_stop
        self._dump = None
        if ds and "+dump:" in ds:
            ds, self._dump = ds.split("+dump:")
            self.debug_stop = ds or None
        self.ph_consts()
        self.ph_p0()
        if self.debug_stop == "p0":
            return self.fin_or_dump()
        self.ph_l0_attn_A()
        self.ph_l0_attn_A2()
        if self.debug_stop == "a2":
            return self.fin_or_dump()
        self.ph_l0_ffn_F1()
        self.ph_l0_attn_B()
        if self.debug_stop == "b":
            return self.fin_or_dump()
        self.ph_l0_ffn_F23()
        self.ph_l0_ffn_F4()
        if self.debug_stop == "l0":
            return self.fin_or_dump()
        self.ph_l1_norm()
        self.ph_l1_attn()
        if self.debug_stop == "dumppack":
            return self.finish_dump("cc_ctx_out")
        if self.debug_stop is not None and self.debug_stop.startswith("l1a"):
            return self.fin_or_dump()
        self.ph_l1_ffn()
        return self.fin_or_dump()

    def ph_consts(self):
        p, d = self.p, self.dram
        ld = p.sem("ld")
        p.dma("sync", self.sb["consts"].ap()[:, :], self.da("eye")[:, :], ld)
        p.dma("sync", self.sb["cones"].ap()[:, :], self.da("ones")[:, :], ld)
        p.dma("sync", self.sb["conesrow"].ap()[:, :], self.da("onesrow")[:, :], ld)
        cp = self.sb["rows"].ap()[32:40, self.RW - 8:self.RW - 4]
        p.dma("sync", cp[:, 0:1], self.da("pick_carry")[:, :], ld)
        p.dma("sync", cp[:, 1:2], self.da("pick_sel")[:, :], ld)
        p.dma("sync", cp[:, 2:3], self.da("pick_even")[:, :], ld)
        p.dma("sync", cp[:, 3:4], self.da("pick_odd")[:, :], ld)
        p.dma("sync", self.sb["crole"].ap()[:, :], self.da("role")[:, :], ld)
        p.dma("sync", self.sb["ciotaT"].ap()[:, :], self.da("iotaT")[:, :], ld)
        ch = self.sb["chalf"].ap()
        qi1 = self.sb["qint"].ap()[:, 0:1]

        def mkch(e, ch=ch, qi1=qi1):
            e.memset(ch, 0.75)
            e.drain()
            e.tensor_copy(qi1, ch)
            e.drain()
            e.tensor_copy(ch, qi1)
            e.drain()
            return e.tensor_scalar(ch, ch, -0.5, 0.5, ALU.mult, ALU.add)
        p.run("vector", mkch, p.sem("ve"))

    def ph_p0(self):
        c, p = self.cfg, self.p
        ld, pe, ve, se, st, cc = (p.sem(n) for n in
                                  ("ld", "pe", "ve", "se", "st", "cc"))
        ones = self.sb["cones"].ap()
        orow = self.sb["conesrow"].ap()
        C0 = self.sb["C0"].ap()
        mx = self.sb["sm"].ap()[:, 0:1]
        p.wait("vector", ld)
        p.run("vector", lambda e: e.memset(mx, 0.0))
        for tt in range(c.NTT):
            lo, hi = tt * c.TT, (tt + 1) * c.TT
            pss0 = self.ps["A"].ap()[0:1, 0:c.TT]
            pss1 = self.ps["A"].ap()[0:1, 512:512 + c.TT]
            p.wait("sync", pe)
            p.wait("sync", se)
            p.wait("sync", ve)
            xall = []
            for i in range(c.DC):
                xall.append(p.dma("sync", self.tslot(i),
                                  self.da("x_fm")[i * P:(i + 1) * P, lo:hi], ld))
            p.wait("scalar", ld, xall[-1])
            p.wait("scalar", pe)
            sev = None
            for i in range(c.DC):
                sq = C0[:, i * c.TT:(i + 1) * c.TT]
                xv = self.tslot(i)
                sev = p.run("scalar", lambda e, sq=sq, xv=xv: e.activation(
                    sq, xv, ACT.Square), se)
            p.wait("tensor", ld, xall[-1])
            p.wait("tensor", ve)
            for i in range(c.DC):
                xv = self.tslot(i)
                p.run("tensor", lambda e, i=i, pss0=pss0, xv=xv: e.matmul(
                    pss0, ones, xv, start=(i == 0), stop=(i == c.DC - 1)))
            p.wait("tensor", se, sev)
            pev = None
            for i in range(c.DC):
                sq = C0[:, i * c.TT:(i + 1) * c.TT]
                pev = p.run("tensor", lambda e, i=i, sq=sq, pss1=pss1: e.matmul(
                    pss1, ones, sq, start=(i == 0), stop=(i == c.DC - 1)), pe)
            p.wait("vector", pe, pev)
            s_m = self.row(0, lo, hi)
            s_r = self.sb["rows"].ap()[0:1, 4096 + lo:4096 + hi]
            Df = float(c.D)

            def stat2(e, pss0=pss0, pss1=pss1, s_m=s_m, s_r=s_r, Df=Df):
                e.tensor_scalar(s_m, pss0, 1.0 / Df, None, ALU.mult)
                e.tensor_scalar(s_r, pss1, 1.0 / Df, None, ALU.mult)
                e.tensor_mul(pss0, s_m, s_m)
                e.tensor_sub(s_r, s_r, pss0)
                return e.tensor_scalar(s_r, s_r, 1e-5, None, ALU.add)
            vev = p.run("vector", stat2, ve)
            p.wait("scalar", ve, vev)
            sev = p.run("scalar", lambda e, s_r=s_r: e.activation(s_r, s_r, ACT.Sqrt), se)
            p.wait("vector", se, sev)
            vev = p.run("vector", lambda e, s_r=s_r: e.reciprocal(s_r, s_r), ve)
            bcm = self.psA(1024, 1024 + c.TT)
            bcr = self.psB(0, c.TT)
            p.wait("tensor", ve, vev)
            p.run("tensor", lambda e, s_m=s_m, bcm=bcm: e.matmul(
                bcm, orow, s_m, start=True, stop=True))
            pev = p.run("tensor", lambda e, s_r=s_r, bcr=bcr: e.matmul(
                bcr, orow, s_r, start=True, stop=True), pe)
            p.wait("vector", pe, pev)
            for i in range(c.DC):
                tn = self.sb[f"T{i % 2}"].ap()[:, :c.TT]
                xt = self.tslot(i)
                at = self.sb[f"acc{i}"].ap()[:, lo:hi]
                p.wait("vector", st)

                def xn_ops(e, xt=xt, at=at, tn=tn, bcm=bcm, bcr=bcr):
                    e.tensor_sub(tn, xt, bcm)
                    e.tensor_mul(tn, tn, bcr)
                    e.tensor_add(at, xt, tn)
                    e.tensor_reduce(self.sb["sm"].ap()[:, 1:2], tn, AX.X,
                                    ALU.max, apply_absolute_value=True)
                    e.drain()
                    return e.tensor_max(mx, mx, self.sb["sm"].ap()[:, 1:2])
                vev = p.run("vector", xn_ops, ve)
                p.wait("gpsimd", ve, vev)
                p.dma("gpsimd", self.da("xn_sp")[i * P:(i + 1) * P, lo:hi], tn, st)

        row = self.sb["rows"].ap()[0:1, 6144:][:, :P]
        p.wait("gpsimd", ve)
        p.dma("gpsimd", row, mx, st)
        p.wait("vector", st)
        g = self.sb["sm"].ap()[0:1, 2:3]
        p.run("vector", lambda e: e.tensor_reduce(g, row, AX.X, ALU.max))
        b8 = self.sb["rows"].ap()[0:1, 6144:][:, 128:136]

        def mk8(e, b8=b8, g=g):
            e.memset(b8, 0.0)
            e.drain()
            return e.tensor_copy(b8[:, 0:1], g)
        vev = p.run("vector", mk8, ve)
        p.wait("gpsimd", ve, vev)
        p.dma("gpsimd", self.da("cc_max1_in")[:, :], b8, st)
        p.wait("gpsimd", st)
        p.run("gpsimd", lambda e: e.collective_compute(
            "AllReduce", ALU.max, replica_groups=[list(range(8))],
            ins=[self.dram["cc_max1_in"].ap().opt()],
            outs=[self.dram["cc_max1_out"].ap().opt()]), cc)

    def ph_l0_attn_A(self):
        c, p = self.cfg, self.p
        ld, pe, ve, se, st = (p.sem(n) for n in ("ld", "pe", "ve", "se", "st"))
        ones = self.sb["cones"].ap()
        orow = self.sb["conesrow"].ap()
        C0 = self.sb["C0"].ap()
        NOC = 3 * c.DC
        for tt in range(c.NTT):
            lo, hi = tt * c.TT, (tt + 1) * c.TT
            pss1 = self.ps["A"].ap()[0:1, 512:512 + c.TT]
            p.wait("sync", p.sems["st"])
            p.wait("sync", pe)
            p.wait("sync", ve)
            p.wait("sync", se)
            xnv = []
            for i in range(c.DC):
                xnv.append(p.dma("sync", self.tslot(i),
                                 self.da("xn_sp")[i * P:(i + 1) * P, lo:hi], ld))
            p.wait("scalar", ld, xnv[-1])
            p.wait("scalar", pe)
            sev = None
            for i in range(c.DC):
                sq = C0[:, i * c.TT:(i + 1) * c.TT]
                xv = self.tslot(i)
                sev = p.run("scalar", lambda e, sq=sq, xv=xv: e.activation(
                    sq, xv, ACT.Square), se)
            p.wait("tensor", se, sev)
            p.wait("tensor", ve)
            pev = None
            for i in range(c.DC):
                sq = C0[:, i * c.TT:(i + 1) * c.TT]
                pev = p.run("tensor", lambda e, i=i, sq=sq, pss1=pss1: e.matmul(
                    pss1, ones, sq, start=(i == 0), stop=(i == c.DC - 1)), pe)
            p.wait("vector", pe, pev)
            s_r = self.sb["rows"].ap()[0:1, 4096 + lo:4096 + hi]
            Df = float(c.D)
            vev = p.run("vector", lambda e, pss1=pss1, s_r=s_r, Df=Df:
                        e.tensor_scalar(s_r, pss1, 1.0 / Df, 1.1920929e-07,
                                        ALU.mult, ALU.add), ve)
            p.wait("scalar", ve, vev)
            sev = p.run("scalar", lambda e, s_r=s_r: e.activation(s_r, s_r, ACT.Sqrt), se)
            p.wait("vector", se, sev)
            vev = p.run("vector", lambda e, s_r=s_r: e.reciprocal(s_r, s_r), ve)
            bcr = self.psB(0, c.TT)
            p.wait("tensor", ve, vev)
            pev = p.run("tensor", lambda e, s_r=s_r, bcr=bcr: e.matmul(
                bcr, orow, s_r, start=True, stop=True), pe)
            p.wait("vector", pe, pev)
            for i in range(c.DC):
                xv = self.tslot(i)
                vev = p.run("vector", lambda e, xv=xv, bcr=bcr: e.tensor_mul(
                    xv, xv, bcr), ve)
                p.wait("gpsimd", ve, vev)
                p.dma("gpsimd", self.da("xh_sp")[i * P:(i + 1) * P, lo:hi], xv, st)
        wq = self.da("wqkvT")
        wbuf = [self.sb["WB0"].ap(), self.sb["WB1"].ap()]
        for ts in range(c.NTT):
            lo, hi = ts * c.TT, (ts + 1) * c.TT
            p.wait("sync", pe)
            p.wait("sync", ve)
            p.wait("sync", st)
            xl = None
            for i in range(c.DC):
                xl = p.dma("sync", self.tslot(i),
                           self.da("xh_sp")[i * P:(i + 1) * P, lo:hi], ld)
            wld = [None, None]

            def wslice(b, oc):
                v = None
                for kc in range(c.DC):
                    v = p.dma("sync", wbuf[b][:, kc * P:(kc + 1) * P],
                              wq[kc * P:(kc + 1) * P, oc * P:(oc + 1) * P],
                              p.sems["ld"])
                return v
            wld[0] = wslice(0, 0)
            if NOC > 1:
                wld[1] = wslice(1, 1)
            for oc in range(NOC):
                b = oc % 2
                bank = oc % 2
                ps = self.psA(0, c.TT) if bank == 0 else self.psA(512, 512 + c.TT)
                p.wait("tensor", ld, wld[b])
                p.wait("tensor", ld, xl)
                p.wait("tensor", se)
                pev = None
                for kc in range(c.DC):
                    lhs = wbuf[b][:, kc * P:(kc + 1) * P]
                    rhs = self.tslot(kc)
                    pev = p.run("tensor", lambda e, lhs=lhs, rhs=rhs, ps=ps,
                                kc=kc: e.matmul(ps, lhs, rhs, start=(kc == 0),
                                                stop=(kc == c.DC - 1)), pe)
                if oc + 2 < NOC:
                    p.wait("sync", pe, pev)
                    wld[b] = wslice(b, oc + 2)
                sg = oc % 2
                stg = self.sb[f"T{sg}"].ap()[:, :c.TT]
                p.wait("scalar", pe, pev)
                p.wait("scalar", st)
                sev = p.run("scalar", lambda e, ps=ps, stg=stg: e.copy(stg, ps), se)
                p.wait("gpsimd", se, sev)
                p.dma("gpsimd", self.da("qkv_sp")[oc * P:(oc + 1) * P, lo:hi],
                      stg, st)

    def ph_l0_attn_A2(self):
        c, p = self.cfg, self.p
        ld, pe, ve, se, st, cc = (p.sem(n) for n in
                                  ("ld", "pe", "ve", "se", "st", "cc"))
        ones = self.sb["cones"].ap()
        orow = self.sb["conesrow"].ap()
        C0 = self.sb["C0"].ap()
        C1 = self.sb["C1"].ap()
        csum = self.sb["csum"].ap()
        p.wait("sync", p.sems["st"])
        p.wait("vector", p.sems["st"])
        vev = p.run("vector", lambda e: e.memset(csum, 0.0), ve)
        qkv = self.da("qkv_sp")
        for tt in range(c.NTT):
            lo, hi = tt * c.TT, (tt + 1) * c.TT
            p.wait("sync", ve)
            p.wait("sync", pe)
            qv = kv_ = vv = None
            for i in range(c.DC):
                qv = p.dma("sync", C0[:, i * c.TT:(i + 1) * c.TT],
                           qkv[i * P:(i + 1) * P, lo:hi], ld)
            for i in range(c.DC):
                kv_ = p.dma("sync", C1[:, i * c.TT:(i + 1) * c.TT],
                            qkv[(c.DC + i) * P:(c.DC + i + 1) * P, lo:hi], ld)
            for i in range(c.DC):
                vv = p.dma("sync", self.aslot(i),
                           qkv[(2 * c.DC + i) * P:(2 * c.DC + i + 1) * P, lo:hi], ld)
            pss0 = self.ps["A"].ap()[0:1, 0:c.TT]
            pss1 = self.ps["A"].ap()[0:1, 512:512 + c.TT]
            sq = self.sb["T3"].ap()[:, :c.TT]
            p.wait("scalar", ld, qv)
            p.wait("tensor", ld, kv_)
            pev = sev = None
            for which, (buf, pss) in enumerate(((C0, pss0), (C1, pss1))):
                for i in range(c.DC):
                    xv = buf[:, i * c.TT:(i + 1) * c.TT]
                    p.wait("scalar", pe, pev) if (i or which) else None
                    sev = p.run("scalar", lambda e, sq=sq, xv=xv: e.activation(
                        sq, xv, ACT.Square), se)
                    p.wait("tensor", se, sev)
                    pev = p.run("tensor", lambda e, sq=sq, pss=pss, i=i: e.matmul(
                        pss, ones, sq, start=(i == 0), stop=(i == c.DC - 1)), pe)
            s_q = self.sb["rows"].ap()[0:1, 4096 + lo:4096 + hi]
            s_k = self.row(0, lo, hi)
            Df = float(c.D)
            p.wait("vector", pe, pev)

            def rr(e, pss0=pss0, pss1=pss1, s_q=s_q, s_k=s_k, Df=Df):
                e.tensor_scalar(s_q, pss0, 1.0 / Df, 1.1920929e-07, ALU.mult, ALU.add)
                return e.tensor_scalar(s_k, pss1, 1.0 / Df, 1.1920929e-07,
                                       ALU.mult, ALU.add)
            vev = p.run("vector", rr, ve)
            p.wait("scalar", ve, vev)

            def sq2(e, s_q=s_q, s_k=s_k):
                e.activation(s_q, s_q, ACT.Sqrt)
                return e.activation(s_k, s_k, ACT.Sqrt)
            sev = p.run("scalar", sq2, se)
            p.wait("vector", se, sev)

            def rc2(e, s_q=s_q, s_k=s_k):
                e.reciprocal(s_q, s_q)
                return e.reciprocal(s_k, s_k)
            vev = p.run("vector", rc2, ve)
            bcq = self.psB(0, c.TT)
            bck = self.psB(512, 512 + c.TT)
            p.wait("tensor", ve, vev)
            p.run("tensor", lambda e, s_q=s_q, bcq=bcq: e.matmul(
                bcq, orow, s_q, start=True, stop=True))
            pev = p.run("tensor", lambda e, s_k=s_k, bck=bck: e.matmul(
                bck, orow, s_k, start=True, stop=True), pe)
            p.wait("vector", pe, pev)
            p.wait("vector", ld, vv)
            for i in range(c.DC):
                qv_ = C0[:, i * c.TT:(i + 1) * c.TT]
                kvv = C1[:, i * c.TT:(i + 1) * c.TT]
                vv_ = self.aslot(i)
                tg = self.sb["T4"].ap()[:, :c.TT]
                tw = self.sb["T5"].ap()[:, :c.TT]
                tkv = self.sb["T6"].ap()[:, :c.TT]
                p.wait("vector", st)
                vev = p.run("vector", lambda e, tg=tg, qv_=qv_, bcq=bcq:
                            e.tensor_mul(tg, qv_, bcq), ve)
                p.wait("scalar", ve, vev)
                sev = p.run("scalar", lambda e, tg=tg: e.activation(
                    tg, tg, ACT.Sigmoid), se)
                p.wait("vector", st)
                vev = p.run("vector", lambda e, tw=tw, kvv=kvv, bck=bck:
                            e.tensor_mul(tw, kvv, bck), ve)
                p.wait("scalar", ve, vev)
                sev = p.run("scalar", lambda e, tw=tw: e.activation(
                    tw, tw, ACT.Exp), se)
                p.wait("vector", se, sev)

                def kvops(e, tkv=tkv, tw=tw, vv_=vv_, csum=csum, i=i):
                    e.tensor_mul(tkv, tw, vv_)
                    e.tensor_reduce(self.sb["sm"].ap()[:, 3:4], tw, AX.X, ALU.add)
                    e.drain()
                    e.tensor_add(csum[:, i:i + 1], csum[:, i:i + 1],
                                 self.sb["sm"].ap()[:, 3:4])
                    e.tensor_reduce(self.sb["sm"].ap()[:, 3:4], tkv, AX.X, ALU.add)
                    e.drain()
                    return e.tensor_add(csum[:, c.DC + i:c.DC + i + 1],
                                        csum[:, c.DC + i:c.DC + i + 1],
                                        self.sb["sm"].ap()[:, 3:4])
                vev = p.run("vector", kvops, ve)
                p.wait("gpsimd", se, sev)
                p.wait("gpsimd", ve, vev)
                p.dma("gpsimd", self.da("gate_sp")[i * P:(i + 1) * P, lo:hi], tg, st)
                p.dma("gpsimd", self.da("w_sp")[i * P:(i + 1) * P, lo:hi], tw, st)
                p.dma("gpsimd", self.da("kv_sp")[i * P:(i + 1) * P, lo:hi], tkv, st)
        src = csum
        dst = self.da("cc_carry_in").rearrange("o (p c) -> (o p) c", c=2 * c.DC)
        p.wait("gpsimd", ve)
        v = p.dma("gpsimd", dst, src, st)
        p.wait("gpsimd", st, v)
        self._cc5 = p.run("gpsimd", lambda e: e.collective_compute(
            "AllGather", ALU.bypass, replica_groups=[list(range(8))],
            ins=[self.dram["cc_carry_in"].ap().opt()],
            outs=[self.dram["cc_carry_out"].ap().opt()]), cc)

    def colslice_load(self, wname, oc, buf, nchunk, ld):
        w = self.da(wname)
        v = None
        for kc in range(nchunk):
            v = self.p.dma("sync", buf[:, kc * P:(kc + 1) * P],
                           w[kc * P:(kc + 1) * P, oc * P:(oc + 1) * P],
                           self.p.sems["ld"])
        return v

    def bc_scalar(self, src11, dst_ps):
        orow = self.sb["conesrow"].ap()
        return lambda e: e.matmul(dst_ps, orow, src11, start=True, stop=True)

    def qdq_chain(self, e, x_ap, out_ap, rbc, sbc, tmp):
        w = tmp.shape[-1]
        qi = self.sb["qint"].ap()[:, :w]
        yi = out_ap.bitcast(U32)
        ti = tmp.bitcast(U32)
        e.tensor_scalar(out_ap, x_ap, rbc, None, ALU.mult)
        e.tensor_single_scalar(ti, yi, 0x7FFFFFFF, ALU.bitwise_and)
        e.tensor_scalar(tmp, tmp, self.sb["chalf"].ap(), None, ALU.add)
        e.tensor_copy(qi, tmp)
        e.tensor_copy(tmp, qi)
        e.tensor_single_scalar(yi, yi, 0x80000000, ALU.bitwise_and)
        e.tensor_tensor(yi, yi, ti, ALU.bitwise_or)
        return e.tensor_scalar(out_ap, out_ap, sbc, None, ALU.mult)

    def scale_from_cc(self, ccname, slot, colbase):
        c, p = self.cfg, self.p
        ld, pe, ve, se, cc = (p.sem(n) for n in ("ld", "pe", "ve", "se", "cc"))
        orow = self.sb["conesrow"].ap()
        row = self.sb["rows"].ap()[0:1, 140 + 16 * slot:148 + 16 * slot]
        p.wait("sync", cc)
        lv = p.dma("sync", row, self.da(ccname)[:, :], ld)
        g = row[:, slot:slot + 1] if False else row[:, 0:1]
        p.wait("vector", ld, lv)
        s11 = self.sb["rows"].ap()[0:1, 300 + 8 * slot:301 + 8 * slot]
        r11 = self.sb["rows"].ap()[0:1, 364 + 8 * slot:365 + 8 * slot]

        def mk(e, row=row, s11=s11, r11=r11, slot=slot):
            e.tensor_scalar(s11, row[:, slot:slot + 1], 1.0 / 32767.0, 1e-12,
                            ALU.mult, ALU.max)
            e.drain()
            return e.reciprocal(r11, s11)
        vev = p.run("vector", mk, ve)
        psr = self.psB(1536, 1537)
        pss = self.psB(1537, 1538)
        p.wait("tensor", ve, vev)
        p.run("tensor", lambda e, r11=r11, psr=psr: e.matmul(
            psr, orow, r11, start=True, stop=True))
        pev = p.run("tensor", lambda e, s11=s11, pss=pss: e.matmul(
            pss, orow, s11, start=True, stop=True), pe)
        sbc = self.sb["sm"].ap()[:, 8 + 2 * colbase:9 + 2 * colbase]
        rbc = self.sb["sm"].ap()[:, 9 + 2 * colbase:10 + 2 * colbase]
        p.wait("vector", pe, pev)

        def cp(e, sbc=sbc, rbc=rbc, psr=psr, pss=pss):
            e.tensor_copy(sbc, pss)
            return e.tensor_copy(rbc, psr)
        vev = p.run("vector", cp, ve)
        return sbc, rbc, vev

    def dequant_matrix(self, wT_name, wN_name, n_in_chunks, n_out,
                       natbuf_fn, rsT, nsT):
        c, p = self.cfg, self.p
        ld, pe, ve, se, st = (p.sem(n) for n in ("ld", "pe", "ve", "se", "st"))
        eye = self.sb["consts"].ap()
        n_row_tiles = n_out // P
        d_in = n_in_chunks * P
        nblk = d_in // c.QBLK
        wn = self.da(wN_name)
        for rt in range(n_row_tiles):
            bslot = natbuf_fn(rt)
            p.wait("sync", pe)
            p.wait("sync", ve)
            lv = p.dma("sync", bslot, wn[rt * P:(rt + 1) * P, :], ld)
            stile = self.sb["T9"].ap()[:, :nblk]
            p.wait("vector", ld, lv)
            bv = bslot.rearrange("p (nb b) -> p nb b", b=c.QBLK)
            vev = p.run("vector", lambda e, stile=stile, bv=bv: e.tensor_reduce(
                stile, bv, AX.X, ALU.max, apply_absolute_value=True), ve)
            pst = self.ps["B"].ap()[0:nblk, 1024:1024 + P]
            p.wait("tensor", ve, vev)
            pev = p.run("tensor", lambda e, pst=pst, stile=stile, eye=eye:
                        e.transpose(pst, stile, eye), pe)
            p.wait("vector", pe, pev)

            def ev(e, pst=pst, rt=rt, rsT=rsT, nblk=nblk):
                v = None
                for g in range(0, nblk, 32):
                    gg = min(32, nblk - g)
                    v = e.tensor_copy(rsT[g:g + gg, rt * P:(rt + 1) * P],
                                      pst[g:g + gg, :])
                return v
            vev = p.run("vector", ev, ve)

        def fin(e, rsT=rsT, nsT=nsT, nblk=nblk):
            v = None
            for g in range(0, nblk, 32):
                gg = min(32, nblk - g)
                rs = rsT[g:g + gg, :]
                ns = nsT[g:g + gg, :]
                e.tensor_scalar(rs, rs, 1.0 / 7.0, 1e-12, ALU.mult, ALU.max)
                e.tensor_scalar(ns, rs, -1.0, None, ALU.mult)
                v = e.reciprocal(rs, rs)
            return v
        vev = self.p.run("vector", fin, ve)
        return vev

    def dequant_group(self, wT_name, rsT, nsT, kc, og, og_cols, dst):
        c, p = self.cfg, self.p
        ld, pe, ve, se, st = (p.sem(n) for n in ("ld", "pe", "ve", "se", "st"))
        nrep = P // c.QBLK
        lo, hi = og * og_cols, (og + 1) * og_cols
        lv = p.dma("sync", dst, self.da(wT_name)[kc * P:(kc + 1) * P, lo:hi],
                   self.p.sems["ld"])
        rrep = self.sb["T9"].ap()[:, :og_cols]
        srep = self.sb["T3"].ap()[:, :og_cols]
        src_r = rsT[nrep * kc:nrep * kc + nrep, lo:hi].unsqueeze(1) \
            .to_broadcast([nrep, c.QBLK, og_cols])
        src_s = nsT[nrep * kc:nrep * kc + nrep, lo:hi].unsqueeze(1) \
            .to_broadcast([nrep, c.QBLK, og_cols])
        p.wait("gpsimd", ve)
        p.wait("gpsimd", pe)
        r1 = p.dma("gpsimd", rrep, src_r, st)
        r2 = p.dma("gpsimd", srep, src_s, st)
        tmp = self.sb["T4"].ap()[:, :og_cols]
        p.wait("vector", ld, lv)
        p.wait("vector", st, r2)

        def dq(e, dst=dst, rrep=rrep, srep=srep, tmp=tmp):
            w = tmp.shape[-1]
            qi = self.sb["qint"].ap()[:, :w]
            di = dst.bitcast(U32)
            ti = tmp.bitcast(U32)
            e.tensor_mul(dst, dst, rrep)
            e.tensor_single_scalar(ti, di, 0x7FFFFFFF, ALU.bitwise_and)
            e.tensor_scalar(tmp, tmp, self.sb["chalf"].ap(), None, ALU.add)
            e.tensor_copy(qi, tmp)
            e.tensor_copy(tmp, qi)
            e.tensor_single_scalar(di, di, 0x80000000, ALU.bitwise_and)
            e.tensor_single_scalar(di, di, 0x80000000, ALU.bitwise_xor)
            e.tensor_tensor(di, di, ti, ALU.bitwise_or)
            return e.tensor_mul(dst, dst, srep)
        return p.run("vector", dq, ve)

    def dequant_spill(self, wT, wN, n_in_chunks, n_out, natbuf_fn, spname,
                      srows):
        c, p = self.cfg, self.p
        ld, pe, ve, se, st = (p.sem(n) for n in ("ld", "pe", "ve", "se", "st"))
        rsT = self.sb["rows"].ap()[32:32 + srows, 0:n_out]
        nsT = self.sb["rows"].ap()[32:32 + srows, 3328:3328 + n_out]
        vev = self.dequant_matrix(wT, wN, n_in_chunks, n_out, natbuf_fn,
                                  rsT, nsT)
        OG = 512 if n_out % 512 == 0 else 256
        nog = n_out // OG
        for og in range(nog):
            for kc in range(n_in_chunks):
                dst = self.sb["T6"].ap()[:, :OG]
                p.wait("sync", st)
                p.wait("sync", ve)
                dvev = self.dequant_group(wT, rsT, nsT, kc, og, OG, dst)
                p.wait("gpsimd", ve, dvev)
                p.dma("gpsimd",
                      self.da(spname)[kc * P:(kc + 1) * P,
                                      og * OG:(og + 1) * OG], dst, st)

    def ph_l0_ffn_F1(self):
        c, p = self.cfg, self.p
        ld, pe, ve, se, st, cc = (p.sem(n) for n in
                                  ("ld", "pe", "ve", "se", "st", "cc"))
        sbc, rbc, vev = self.scale_from_cc("cc_max1_out", 0, 0)
        p.wait("sync", ve, vev)
        for tt in range(c.NTT):
            lo, hi = tt * c.TT, (tt + 1) * c.TT
            p.wait("sync", st)
            p.wait("sync", ve)
            for i in range(c.DC):
                lv = p.dma("sync", self.tslot(i),
                           self.da("xn_sp")[i * P:(i + 1) * P, lo:hi], ld)
                xv = self.tslot(i)
                tmp = self.sb["T0"].ap()[:, :c.TT]
                p.wait("vector", ld, lv)
                vev = p.run("vector", lambda e, xv=xv, tmp=tmp, rbc=rbc,
                            sbc=sbc: self.qdq_chain(e, xv, xv, rbc, sbc, tmp),
                            ve)
                p.wait("gpsimd", ve, vev)
                p.dma("gpsimd", self.da("xq_sp")[i * P:(i + 1) * P, lo:hi],
                      xv, st)
        def natbuf(rt):
            return self.sb[f"WB{rt % 2}"].ap()[:, :c.D]
        self.dequant_spill("wupT", "wup_n", c.DC, c.FI, natbuf, "wupdq_sp",
                           c.D // c.QBLK)
        self.dequant_spill("wgateT", "wgate_n", c.DC, c.FI, natbuf,
                           "wgatedq_sp", c.D // c.QBLK)
        mx2 = self.sb["sm"].ap()[:, 16:17]
        mx3 = self.sb["sm"].ap()[:, 17:18]
        p.run("vector", lambda e: e.memset(mx2, 0.0))
        p.run("vector", lambda e: e.memset(mx3, 0.0))
        wbuf = [self.sb["WB0"].ap(), self.sb["WB1"].ap()]
        for mi, (dqsp, rawsp, mxc) in enumerate((
                ("wupdq_sp", "upraw_sp", mx2),
                ("wgatedq_sp", "gateraw_sp", mx3))):
            for ts in range(c.NTT):
                lo, hi = ts * c.TT, (ts + 1) * c.TT
                p.wait("sync", pe)
                p.wait("sync", ve)
                p.wait("sync", st)
                xl = None
                for i in range(c.DC):
                    xl = p.dma("sync", self.tslot(i),
                               self.da("xq_sp")[i * P:(i + 1) * P, lo:hi], ld)
                wld = [None, None]
                wld[0] = self.colslice_load(dqsp, 0, wbuf[0], c.DC, ld)
                wld[1] = self.colslice_load(dqsp, 1, wbuf[1], c.DC, ld)
                for oc in range(c.FIC):
                    b = oc % 2
                    bank = oc % 2
                    ps = self.psA(0, c.TT) if bank == 0 else self.psA(512, 512 + c.TT)
                    p.wait("tensor", ld, wld[b])
                    p.wait("tensor", ld, xl)
                    p.wait("tensor", se)
                    pev = None
                    for kc in range(c.DC):
                        lhs = wbuf[b][:, kc * P:(kc + 1) * P]
                        rhs = self.tslot(kc)
                        pev = p.run("tensor", lambda e, lhs=lhs, rhs=rhs,
                                    ps=ps, kc=kc: e.matmul(
                                        ps, lhs, rhs, start=(kc == 0),
                                        stop=(kc == c.DC - 1)), pe)
                    if oc + 2 < c.FIC:
                        p.wait("sync", pe, pev)
                        wld[b] = self.colslice_load(dqsp, oc + 2, wbuf[b],
                                                    c.DC, ld)
                    sg = oc % 2
                    stg = self.sb[f"T{sg}"].ap()[:, :c.TT]
                    p.wait("scalar", pe, pev)
                    p.wait("scalar", st)
                    p.wait("scalar", ve)
                    sev = p.run("scalar", lambda e, ps=ps, stg=stg: e.copy(
                        stg, ps), se)
                    p.wait("vector", se, sev)

                    def mxop(e, stg=stg, mxc=mxc):
                        e.tensor_reduce(self.sb["sm"].ap()[:, 20:21], stg,
                                        AX.X, ALU.max, apply_absolute_value=True)
                        e.drain()
                        return e.tensor_max(mxc, mxc, self.sb["sm"].ap()[:, 20:21])
                    vev = p.run("vector", mxop, ve)
                    p.wait("gpsimd", se, sev)
                    p.wait("gpsimd", ve, vev)
                    p.dma("gpsimd", self.da(rawsp)[oc * P:(oc + 1) * P, lo:hi],
                          stg, st)
        row = self.sb["rows"].ap()[0:1, 6144:][:, :P]
        b8 = self.sb["rows"].ap()[0:1, 6144:][:, 128:136]
        p.wait("gpsimd", ve)
        v1 = p.dma("gpsimd", row, mx2, st)
        p.wait("vector", st, v1)
        g2 = self.sb["sm"].ap()[0:1, 21:22]
        vev = p.run("vector", lambda e, g2=g2, row=row: e.tensor_reduce(
            g2, row, AX.X, ALU.max), ve)
        p.wait("gpsimd", ve, vev)
        v2 = p.dma("gpsimd", row, mx3, st)
        p.wait("vector", st, v2)
        g3 = self.sb["sm"].ap()[0:1, 22:23]

        def pack(e, g3=g3, row=row, b8=b8, g2=g2):
            e.tensor_reduce(g3, row, AX.X, ALU.max)
            e.memset(b8, 0.0)
            e.drain()
            e.tensor_copy(b8[:, 0:1], g2)
            return e.tensor_copy(b8[:, 1:2], g3)
        vev = p.run("vector", pack, ve)
        p.wait("gpsimd", ve, vev)
        v3 = p.dma("gpsimd", self.da("cc_max23_in")[:, :], b8, st)
        p.wait("gpsimd", st, v3)
        p.run("gpsimd", lambda e: e.collective_compute(
            "AllReduce", ALU.max, replica_groups=[list(range(8))],
            ins=[self.dram["cc_max23_in"].ap().opt()],
            outs=[self.dram["cc_max23_out"].ap().opt()]), cc)

    def ph_l0_attn_B(self):
        c, p = self.cfg, self.p
        ld, pe, ve, se, st, cc = (p.sem(n) for n in
                                  ("ld", "pe", "ve", "se", "st", "cc"))
        orow = self.sb["conesrow"].ap()
        C0 = self.sb["C0"].ap()
        C1 = self.sb["C1"].ap()
        carry = self.sb["csum2"].ap()[:, :2 * self.cfg.DC]
        runc = self.sb["csum2"].ap()[:, 2 * self.cfg.DC:]
        gath = self.sb["rows"].ap()[32:40, :]
        cpick = self.sb["rows"].ap()[32:40, self.RW - 8:self.RW - 4]
        p.wait("sync", p.sems["cc"], self._cc5)
        lv = p.dma("sync", gath[:, :2 * c.D], self.da("cc_carry_out")[:, :], ld)
        p.wait("tensor", ld, lv)
        pev = None
        for j in range(0, 2 * c.D, 512):
            w = min(512, 2 * c.D - j)
            psx = self.psB(1024 + j // 4, 1024 + j // 4 + w, rows=1) if False \
                else self.psA(j % 2048, j % 2048 + w, rows=1)
            pev = p.run("tensor", lambda e, psx=psx, j=j, w=w: e.matmul(
                psx, cpick[:, 0:1], gath[:, j:j + w], start=True, stop=True), pe)
        carrow = self.sb["rows"].ap()[0:1, 6144:][0:1, :]
        carrow = self.row(0, 0, 2 * c.D)
        p.wait("vector", pe, pev)
        vev = None
        for j in range(0, 2 * c.D, 512):
            w = min(512, 2 * c.D - j)
            psx = self.psA(j % 2048, j % 2048 + w, rows=1)
            vev = p.run("vector", lambda e, psx=psx, j=j, w=w, carrow=carrow:
                        e.tensor_copy(carrow[:, j:j + w], psx), ve)
        p.wait("gpsimd", ve, vev)
        cv = p.dma("gpsimd", carry, carrow, st)
        p.wait("vector", st, cv)
        vev = p.run("vector", lambda e: e.tensor_copy(runc, carry), ve)

        wbuf = [self.sb["WB0"].ap(), self.sb["WB1"].ap()]
        for tt in range(c.NTT):
            lo, hi = tt * c.TT, (tt + 1) * c.TT
            p.wait("sync", ve)
            p.wait("sync", pe)
            lw = lk = lg = None
            for i in range(c.DC):
                lw = p.dma("sync", C0[:, i * c.TT:(i + 1) * c.TT],
                           self.da("w_sp")[i * P:(i + 1) * P, lo:hi], ld)
            for i in range(c.DC):
                lk = p.dma("sync", C1[:, i * c.TT:(i + 1) * c.TT],
                           self.da("kv_sp")[i * P:(i + 1) * P, lo:hi], ld)
            for i in range(c.DC):
                lg = p.dma("sync", self.aslot(i),
                           self.da("gate_sp")[i * P:(i + 1) * P, lo:hi], ld)
            p.wait("vector", ld, lg)
            for i in range(c.DC):
                wt = C0[:, i * c.TT:(i + 1) * c.TT]
                kvt = C1[:, i * c.TT:(i + 1) * c.TT]
                gt = self.aslot(i)
                t4 = self.sb["T4"].ap()[:, :c.TT]
                t5 = self.sb["T5"].ap()[:, :c.TT]
                zcol = self.sb["sm"].ap()[:, 24:25]

                def scans(e, wt=wt, kvt=kvt, gt=gt, t4=t4, t5=t5, i=i,
                          runc=runc, zcol=zcol):
                    e.memset(zcol, 0.0)
                    zb = zcol.to_broadcast([P, c.TT])
                    e.tensor_tensor_scan(t4, wt, zb, runc[:, i:i + 1],
                                         ALU.add, ALU.bypass)
                    e.drain()
                    e.tensor_copy(runc[:, i:i + 1], t4[:, c.TT - 1:c.TT])
                    e.tensor_tensor_scan(t5, kvt, zb,
                                         runc[:, c.DC + i:c.DC + i + 1],
                                         ALU.add, ALU.bypass)
                    e.drain()
                    e.tensor_copy(runc[:, c.DC + i:c.DC + i + 1],
                                  t5[:, c.TT - 1:c.TT])
                    e.tensor_scalar(t4, t4, 1e-6, None, ALU.add)
                    e.reciprocal(t4, t4)
                    e.tensor_mul(t5, t5, t4)
                    return e.tensor_mul(wt, t5, gt)
                vev = p.run("vector", scans, ve)
            NOC = c.SW // P
            wld = [None, None]
            p.wait("sync", ve, vev)
            p.wait("tensor", ve, vev)
            wld[0] = self.colslice_load("wswT", 0, wbuf[0], c.DC, ld)
            if NOC > 1:
                wld[1] = self.colslice_load("wswT", 1, wbuf[1], c.DC, ld)
            for oc in range(NOC):
                b = oc % 2
                bank = oc % 2
                ps = self.psA(0, c.TT) if bank == 0 else self.psA(512, 512 + c.TT)
                p.wait("tensor", ld, wld[b])
                p.wait("tensor", se)
                pev = None
                for kc in range(c.DC):
                    lhs = wbuf[b][:, kc * P:(kc + 1) * P]
                    rhs = C0[:, kc * c.TT:(kc + 1) * c.TT]
                    pev = p.run("tensor", lambda e, lhs=lhs, rhs=rhs, ps=ps,
                                kc=kc: e.matmul(ps, lhs, rhs, start=(kc == 0),
                                                stop=(kc == c.DC - 1)), pe)
                if oc + 2 < NOC:
                    p.wait("sync", pe, pev)
                    wld[b] = self.colslice_load("wswT", oc + 2, wbuf[b], c.DC, ld)
                if oc < c.DC:
                    p.wait("scalar", pe, pev)
                    p.wait("scalar", ve)
                    sev = p.run("scalar", lambda e, ps=ps, oc=oc, C1=C1: e.copy(
                        C1[:, oc * c.TT:(oc + 1) * c.TT], ps), se)
                else:
                    t6 = self.sb["T6"].ap()[:, :c.TT]
                    p.wait("scalar", pe, pev)
                    p.wait("scalar", ve)
                    sev = p.run("scalar", lambda e, ps=ps, t6=t6: e.activation(
                        t6, ps, ACT.Silu), se)
                    uslot = C1[:, (oc - c.DC) * c.TT:(oc - c.DC + 1) * c.TT]
                    p.wait("vector", se, sev)
                    vev = p.run("vector", lambda e, uslot=uslot, t6=t6:
                                e.tensor_mul(uslot, uslot, t6), ve)
            p.wait("sync", pe)
            wld[0] = self.colslice_load("woutT", 0, wbuf[0], c.DC, ld)
            if c.DC > 1:
                wld[1] = self.colslice_load("woutT", 1, wbuf[1], c.DC, ld)
            for oc in range(c.DC):
                b = oc % 2
                bank = oc % 2
                ps = self.psB(0, c.TT) if bank == 0 else self.psB(512, 512 + c.TT)
                p.wait("tensor", ld, wld[b])
                p.wait("tensor", ve)
                pev = None
                for kc in range(c.DC):
                    lhs = wbuf[b][:, kc * P:(kc + 1) * P]
                    rhs = C1[:, kc * c.TT:(kc + 1) * c.TT]
                    pev = p.run("tensor", lambda e, lhs=lhs, rhs=rhs, ps=ps,
                                kc=kc: e.matmul(ps, lhs, rhs, start=(kc == 0),
                                                stop=(kc == c.DC - 1)), pe)
                if oc + 2 < c.DC:
                    p.wait("sync", pe, pev)
                    wld[b] = self.colslice_load("woutT", oc + 2, wbuf[b], c.DC, ld)
                at = self.sb[f"acc{oc}"].ap()[:, lo:hi]
                p.wait("vector", pe, pev)
                vev = p.run("vector", lambda e, at=at, ps=ps: e.tensor_add(
                    at, at, ps), ve)

    def _gate2_tile(self, gq, tmp, tmp2, s3bc, r3bc):
        c, p = self.cfg, self.p
        ve, se = p.sem("ve"), p.sem("se")
        vev = p.run("vector", lambda e, gq=gq, tmp=tmp, r3bc=r3bc, s3bc=s3bc:
                    self.qdq_chain(e, gq, gq, r3bc, s3bc, tmp), ve)
        p.wait("scalar", ve, vev)
        sev = p.run("scalar", lambda e, tmp=tmp, gq=gq: e.activation(
            tmp, gq, ACT.Sigmoid), se)
        p.wait("vector", se, sev)

        def fx(e, gq=gq, tmp=tmp, tmp2=tmp2):
            w = tmp.shape[-1]
            qi = self.sb["qint"].ap()[:, :w]
            e.tensor_scalar(tmp, tmp, 65536.0, None, ALU.mult)
            e.tensor_scalar(tmp, tmp, self.sb["chalf"].ap(), None, ALU.add)
            e.tensor_copy(qi, tmp)
            e.tensor_copy(tmp, qi)
            e.tensor_scalar(tmp, tmp, 65535.0, 1.0 / 65536.0, ALU.min, ALU.mult)
            return e.tensor_mul(gq, gq, tmp)
        return p.run("vector", fx, ve)

    def ph_l0_ffn_F23(self):
        c, p = self.cfg, self.p
        ld, pe, ve, se, st, cc = (p.sem(n) for n in
                                  ("ld", "pe", "ve", "se", "st", "cc"))
        s3bc, r3bc, vev = self.scale_from_cc("cc_max23_out", 1, 1)
        s2bc, r2bc, vev = self.scale_from_cc("cc_max23_out", 0, 2)
        C0 = self.sb["C0"].ap()
        C1 = self.sb["C1"].ap()
        mx4 = self.sb["sm"].ap()[:, 26:27]
        mx5 = self.sb["sm"].ap()[:, 27:28]
        p.run("vector", lambda e: e.memset(mx4, 0.0))
        tmp = self.sb["T4"].ap()[:, :c.TT]
        tmp2 = self.sb["T5"].ap()[:, :c.TT]
        NTL = c.FIC * c.NTT
        vhist = []
        for it in range(NTL):
            oc, ts = divmod(it, c.NTT)
            lo, hi = ts * c.TT, (ts + 1) * c.TT
            slot = it % 4
            gq = C0[:, slot * c.TT:(slot + 1) * c.TT]
            if it >= 4:
                p.wait("sync", ve, vhist[it - 4])
            lv = p.dma("sync", gq, self.da("gateraw_sp")[oc * P:(oc + 1) * P, lo:hi], ld)
            p.wait("vector", ld, lv)
            gvev = self._gate2_tile(gq, tmp, tmp2, s3bc, r3bc)

            def mxop(e, gq=gq, mx4=mx4):
                e.tensor_reduce(self.sb["sm"].ap()[:, 20:21], gq, AX.X,
                                ALU.max, apply_absolute_value=True)
                e.drain()
                return e.tensor_max(mx4, mx4, self.sb["sm"].ap()[:, 20:21])
            vev = p.run("vector", mxop, ve)
            vhist.append(vev)
        row = self.sb["rows"].ap()[0:1, 6144:][:, :P]
        b8 = self.sb["rows"].ap()[0:1, 6144:][:, 128:136]
        p.wait("gpsimd", ve, vev)
        v1 = p.dma("gpsimd", row, mx4, st)
        p.wait("vector", st, v1)
        g4 = self.sb["sm"].ap()[0:1, 21:22]

        def pk(e, g4=g4, row=row, b8=b8):
            e.tensor_reduce(g4, row, AX.X, ALU.max)
            e.memset(b8, 0.0)
            e.drain()
            return e.tensor_copy(b8[:, 0:1], g4)
        vev = p.run("vector", pk, ve)
        p.wait("gpsimd", ve, vev)
        v2 = p.dma("gpsimd", self.da("cc_max4_in")[:, :], b8, st)
        p.wait("gpsimd", st, v2)
        p.run("gpsimd", lambda e: e.collective_compute(
            "AllReduce", ALU.max, replica_groups=[list(range(8))],
            ins=[self.dram["cc_max4_in"].ap().opt()],
            outs=[self.dram["cc_max4_out"].ap().opt()]), cc)
        s4bc, r4bc, vev = self.scale_from_cc("cc_max4_out", 0, 3)
        p.run("vector", lambda e: e.memset(mx5, 0.0))
        vhist3 = []
        shist3 = []
        for it in range(NTL):
            oc, ts = divmod(it, c.NTT)
            lo, hi = ts * c.TT, (ts + 1) * c.TT
            slot = it % 4
            gq = C0[:, slot * c.TT:(slot + 1) * c.TT]
            uq = C1[:, slot * c.TT:(slot + 1) * c.TT]
            if it >= 4:
                p.wait("sync", ve, vhist3[it - 4])
                p.wait("sync", st, shist3[it - 4])
            lv = p.dma("sync", gq, self.da("gateraw_sp")[oc * P:(oc + 1) * P, lo:hi], ld)
            lv2 = p.dma("sync", uq, self.da("upraw_sp")[oc * P:(oc + 1) * P, lo:hi], ld)
            p.wait("vector", ld, lv)
            gvev = self._gate2_tile(gq, tmp, tmp2, s3bc, r3bc)

            def oraw(e, gq=gq, uq=uq, tmp=tmp, r4bc=r4bc, s4bc=s4bc,
                     r2bc=r2bc, s2bc=s2bc, mx5=mx5):
                self.qdq_chain(e, gq, gq, r4bc, s4bc, tmp)
                self.qdq_chain(e, uq, uq, r2bc, s2bc, tmp)
                e.tensor_mul(gq, gq, uq)
                e.tensor_reduce(self.sb["sm"].ap()[:, 20:21], gq, AX.X,
                                ALU.max, apply_absolute_value=True)
                e.drain()
                return e.tensor_max(mx5, mx5, self.sb["sm"].ap()[:, 20:21])
            p.wait("vector", ld, lv2)
            vev = p.run("vector", oraw, ve)
            vhist3.append(vev)
            p.wait("gpsimd", ve, vev)
            shist3.append(p.dma("gpsimd", self.da("oraw_sp")[oc * P:(oc + 1) * P,
                                                             lo:hi], gq, st))
        p.wait("gpsimd", ve, vev)
        v1 = p.dma("gpsimd", row, mx5, st)
        p.wait("vector", st, v1)
        g5 = self.sb["sm"].ap()[0:1, 21:22]

        def pk5(e, g5=g5, row=row, b8=b8):
            e.tensor_reduce(g5, row, AX.X, ALU.max)
            e.memset(b8, 0.0)
            e.drain()
            return e.tensor_copy(b8[:, 0:1], g5)
        vev = p.run("vector", pk5, ve)
        p.wait("gpsimd", ve, vev)
        v2 = p.dma("gpsimd", self.da("cc_max5_in")[:, :], b8, st)
        p.wait("gpsimd", st, v2)
        p.run("gpsimd", lambda e: e.collective_compute(
            "AllReduce", ALU.max, replica_groups=[list(range(8))],
            ins=[self.dram["cc_max5_in"].ap().opt()],
            outs=[self.dram["cc_max5_out"].ap().opt()]), cc)

    def ph_l0_ffn_F4(self):
        c, p = self.cfg, self.p
        ld, pe, ve, se, st, cc = (p.sem(n) for n in
                                  ("ld", "pe", "ve", "se", "st", "cc"))
        C0 = self.sb["C0"].ap()

        def natbuf(rt):
            if 2 * c.TOK >= c.FI:
                return C0[:, :c.FI]
            return self.sb[f"WB{rt % 2}"].ap()[:, :c.FI]
        self.dequant_spill("wdownT", "wdown_n", c.FIC, c.D, natbuf,
                           "wdowndq_sp", c.FI // c.QBLK)
        p.wait("sync", st)
        s5bc, r5bc, vev = self.scale_from_cc("cc_max5_out", 0, 4)
        wbuf = [self.sb["WB0"].ap(), self.sb["WB1"].ap()]
        tmp = self.sb["T0"].ap()[:, :c.TT]
        WKC = min(c.FIC, 2048 // P)
        for tt in range(c.NTT):
            lo, hi = tt * c.TT, (tt + 1) * c.TT
            p.wait("sync", ve)
            p.wait("sync", pe)
            p.wait("sync", se)
            lvs = []
            for i in range(c.FIC):
                lvs.append(p.dma("sync", self.oslot(i),
                                 self.da("oraw_sp")[i * P:(i + 1) * P, lo:hi], ld))
            for i in range(c.FIC):
                sl = self.oslot(i)
                p.wait("vector", ld, lvs[i])
                vev = p.run("vector", lambda e, sl=sl, tmp=tmp, r5bc=r5bc,
                            s5bc=s5bc: self.qdq_chain(e, sl, sl, r5bc, s5bc,
                                                      tmp), ve)
            p.wait("tensor", ve, vev)
            for oc in range(c.DC):
                b = oc % 2
                bank = oc % 2
                ps = self.psB(0, c.TT) if bank == 0 else self.psB(512, 512 + c.TT)
                nw = (c.FIC + WKC - 1) // WKC
                pev = None
                for w in range(nw):
                    k0, k1 = w * WKC, min((w + 1) * WKC, c.FIC)
                    p.wait("sync", pe)
                    wl = None
                    for kc in range(k0, k1):
                        wl = p.dma("sync", wbuf[b][:, (kc - k0) * P:(kc - k0 + 1) * P],
                                   self.da("wdowndq_sp")[kc * P:(kc + 1) * P,
                                                         oc * P:(oc + 1) * P],
                                   p.sems["ld"])
                    p.wait("tensor", ld, wl)
                    if w == 0:
                        p.wait("tensor", ve)
                    for kc in range(k0, k1):
                        lhs = wbuf[b][:, (kc - k0) * P:(kc - k0 + 1) * P]
                        pev = p.run("tensor", lambda e, lhs=lhs, kc=kc, ps=ps:
                                    e.matmul(ps, lhs, self.oslot(kc),
                                             start=(kc == 0),
                                             stop=(kc == c.FIC - 1)), pe)
                at = self.sb[f"acc{oc}"].ap()[:, lo:hi]
                p.wait("vector", pe, pev)
                vev = p.run("vector", lambda e, at=at, ps=ps: e.tensor_add(
                    at, at, ps), ve)

    def ph_l1_norm(self):
        c, p = self.cfg, self.p
        ld, pe, ve, se, st, cc = (p.sem(n) for n in
                                  ("ld", "pe", "ve", "se", "st", "cc"))
        ones = self.sb["cones"].ap()
        orow = self.sb["conesrow"].ap()
        stats = self.sb["rows"].ap()[0:1, :]
        C0 = self.sb["C0"].ap()
        for tt in range(c.NTT):
            lo, hi = tt * c.TT, (tt + 1) * c.TT
            pss0 = self.ps["A"].ap()[0:1, 0:c.TT]
            pss1 = self.ps["A"].ap()[0:1, 512:512 + c.TT]
            p.wait("scalar", pe)
            p.wait("scalar", ve)
            sev = None
            for i in range(c.DC):
                sq = C0[:, i * c.TT:(i + 1) * c.TT]
                xv = self.sb[f"acc{i}"].ap()[:, lo:hi]
                sev = p.run("scalar", lambda e, sq=sq, xv=xv: e.activation(
                    sq, xv, ACT.Square), se)
            p.wait("tensor", ve)
            for i in range(c.DC):
                xv = self.sb[f"acc{i}"].ap()[:, lo:hi]
                p.run("tensor", lambda e, i=i, pss0=pss0, xv=xv: e.matmul(
                    pss0, ones, xv, start=(i == 0), stop=(i == c.DC - 1)))
            p.wait("tensor", se, sev)
            pev = None
            for i in range(c.DC):
                sq = C0[:, i * c.TT:(i + 1) * c.TT]
                pev = p.run("tensor", lambda e, i=i, sq=sq, pss1=pss1: e.matmul(
                    pss1, ones, sq, start=(i == 0), stop=(i == c.DC - 1)), pe)
            p.wait("vector", pe, pev)
            s_m = stats[0:1, lo:hi]
            s_r = self.sb["rows"].ap()[0:1, 4096 + lo:4096 + hi]
            Df = float(c.D)

            def stat2(e, pss0=pss0, pss1=pss1, s_m=s_m, s_r=s_r, Df=Df):
                e.tensor_scalar(s_m, pss0, 1.0 / Df, None, ALU.mult)
                e.tensor_scalar(s_r, pss1, 1.0 / Df, None, ALU.mult)
                e.tensor_mul(pss0, s_m, s_m)
                e.tensor_sub(s_r, s_r, pss0)
                return e.tensor_scalar(s_r, s_r, 1e-5, None, ALU.add)
            vev = p.run("vector", stat2, ve)
            p.wait("scalar", ve, vev)
            sev = p.run("scalar", lambda e, s_r=s_r: e.activation(s_r, s_r, ACT.Sqrt), se)
            p.wait("vector", se, sev)
            vev = p.run("vector", lambda e, s_r=s_r: e.reciprocal(s_r, s_r), ve)
            bcm = self.psA(1024, 1024 + c.TT)
            bcr = self.psB(0, c.TT)
            p.wait("tensor", ve, vev)
            p.run("tensor", lambda e, s_m=s_m, bcm=bcm: e.matmul(
                bcm, orow, s_m, start=True, stop=True))
            pev = p.run("tensor", lambda e, s_r=s_r, bcr=bcr: e.matmul(
                bcr, orow, s_r, start=True, stop=True), pe)
            p.wait("vector", pe, pev)
            for i in range(c.DC):
                tn = self.sb[f"T{1 + (i % 2)}"].ap()[:, :c.TT]
                xt = self.sb[f"acc{i}"].ap()[:, lo:hi]
                p.wait("vector", st)

                def xn_ops(e, xt=xt, tn=tn, bcm=bcm, bcr=bcr):
                    e.tensor_sub(tn, xt, bcm)
                    e.tensor_mul(tn, tn, bcr)
                    return e.tensor_add(xt, xt, tn)
                vev = p.run("vector", xn_ops, ve)
                p.wait("gpsimd", ve, vev)
                p.dma("gpsimd", self.da("xn1_sp")[i * P:(i + 1) * P, lo:hi], tn, st)
                p.dma("gpsimd", self.da("cc_xn1_in")[i * P:(i + 1) * P, lo:hi], tn, st)
        p.wait("gpsimd", st)
        self._cc6 = p.run("gpsimd", lambda e: e.collective_compute(
            "AllGather", ALU.bypass,
            replica_groups=[[0, 1], [2, 3], [4, 5], [6, 7]],
            ins=[self.dram["cc_xn1_in"].ap().opt()],
            outs=[self.dram["cc_xn1_out"].ap().opt()]), cc)

    def ph_l1_attn(self):
        c, p = self.cfg, self.p
        ld, pe, ve, se, st, cc = (p.sem(n) for n in
                                  ("ld", "pe", "ve", "se", "st", "cc"))
        ones = self.sb["cones"].ap()
        orow = self.sb["conesrow"].ap()
        eye = self.sb["consts"].ap()
        C0 = self.sb["C0"].ap()
        C1 = self.sb["C1"].ap()
        GH = min(2, c.HPC)
        NG = c.HPC // GH
        NTB = c.T // 512
        p.wait("sync", p.sems["cc"], self._cc6)

        wbuf = [self.sb["WB0"].ap(), self.sb["WB1"].ap()]
        smax = self.sb["rows"].ap()[32:32 + min(2, self.cfg.HPC), :self.cfg.T]
        qsel = self.sb["qsel"].ap()
        expb = self.sb["expb"].ap()
        ctxb = self.sb["ctxb"].ap()
        for hg in range(NG):
            p.wait("sync", pe)
            p.wait("sync", ve)
            lq = lk = lv = None
            for ocl in range(GH * c.DK // P):
                oc = hg * (GH * c.DK // P) + ocl
                for kc in range(c.DC):
                    lq = p.dma("sync", wbuf[0][:, (ocl * c.DC + kc) * P:
                                                (ocl * c.DC + kc + 1) * P],
                               self.da("wqT_my")[kc * P:(kc + 1) * P,
                                                 oc * P:(oc + 1) * P], ld)
                    lk = p.dma("sync", wbuf[1][:, (ocl * c.DC + kc) * P:
                                                (ocl * c.DC + kc + 1) * P],
                               self.da("wkT_my")[kc * P:(kc + 1) * P,
                                                 oc * P:(oc + 1) * P], ld)
            VW = GH * c.DK
            NOCL = GH * c.DK // P
            C1a = self.sb["C1"].ap()

            def wvslot(kc):
                return C1a[:, kc * VW:(kc + 1) * VW]
            for kc in range(c.DC):
                lv = p.dma("sync", wvslot(kc),
                           self.da("wvT_my")[kc * P:(kc + 1) * P,
                                             hg * VW:(hg + 1) * VW], ld)
            bv = p.dma("sync", self.sb["rows"].ap()[0:1, 2048:][0:1, 0:VW],
                       self.da("bv_my_row")[0:1, hg * VW:(hg + 1) * VW], ld)
            bql = None
            for which, bias in enumerate(("bq_my_t", "bk_my_t")):
                for ocl in range(NOCL):
                    bcol = self.da(bias)[:, hg * NOCL + ocl:hg * NOCL + ocl + 1]
                    bs = self.sb["sm"].ap()[:, 40 + which * NOCL + ocl:
                                            41 + which * NOCL + ocl]
                    bql = p.dma("sync", bs, bcol, ld, slow=True)

            TBW = min(512, c.TOK)
            NTB = c.T // TBW
            TSUBS = TBW // P
            for tb in range(NTB):
                xl = []
                p.wait("sync", pe)
                half = (tb * TBW) // c.TOK
                lcol = (tb * TBW) % c.TOK
                for kc in range(c.DC):
                    xl.append(p.dma("sync", self.cslot(0, kc, TBW),
                                    self.da("cc_xn1_out")[
                                        half * c.D + kc * P:
                                        half * c.D + (kc + 1) * P,
                                        lcol:lcol + TBW],
                                    ld))
                p.wait("tensor", ld, xl[-1])
                p.wait("tensor", ld, lv)
                for which, (wb, bias, dst0) in enumerate((
                        (wbuf[0], "bq_my_t", 0), (wbuf[1], "bk_my_t", 1))):
                    for ocl in range(NOCL):
                        bank = (which * NOCL + ocl) % 2
                        ps = self.psA(0, TBW) if bank == 0 else self.psA(512, 512 + TBW)
                        p.wait("tensor", se)
                        pev = None
                        for kc in range(c.DC):
                            lhs = wb[:, (ocl * c.DC + kc) * P:(ocl * c.DC + kc + 1) * P]
                            rhs = self.cslot(0, kc, TBW)
                            pev = p.run("tensor", lambda e, lhs=lhs, rhs=rhs,
                                        ps=ps, kc=kc: e.matmul(
                                            ps, lhs, rhs, start=(kc == 0),
                                            stop=(kc == c.DC - 1)), pe)
                        dst = self.sb[f"bigA{dst0 + ocl}"].ap()[:, tb * TBW:(tb + 1) * TBW]
                        bs = self.sb["sm"].ap()[:, 40 + which * NOCL + ocl:
                                                41 + which * NOCL + ocl]
                        p.wait("scalar", ld, bql)
                        p.wait("scalar", pe, pev)
                        sev = p.run("scalar", lambda e, dst=dst, ps=ps, bs=bs:
                                    e.add(dst, ps, bs), se)
                for tsub in range(TSUBS):
                    psv = self.psB(0, VW)
                    p.wait("tensor", se)
                    pev = None
                    for kc in range(c.DC):
                        lhs = self.cslot(0, kc, TBW)[:, tsub * P:(tsub + 1) * P]
                        rhs = wvslot(kc)
                        pev = p.run("tensor", lambda e, lhs=lhs, rhs=rhs,
                                    psv=psv, kc=kc: e.matmul(
                                        psv, lhs, rhs, start=(kc == 0),
                                        stop=False), pe)
                    bvrow = self.sb["rows"].ap()[0:1, 2048:][0:1, 0:VW]
                    pev = p.run("tensor", lambda e, psv=psv, bvrow=bvrow:
                                e.matmul(psv, orow, bvrow, start=False,
                                         stop=True), pe)
                    vs = self.sb["T1"].ap()[:, :VW]
                    p.wait("scalar", pe, pev)
                    p.wait("scalar", st)
                    sev = p.run("scalar", lambda e, vs=vs, psv=psv:
                                e.copy(vs, psv), se)
                    rb = tb * TBW + tsub * P
                    p.wait("gpsimd", se, sev)
                    p.dma("gpsimd", self.da("v_sp")[rb:rb + P,
                                                    hg * VW:(hg + 1) * VW],
                          vs, st)
                wqmv = wbuf[0][:, :NOCL * c.DC * P].rearrange(
                    "p (o k n) -> p o k n", o=NOCL, k=c.DC)
                bqrow = self.sb["rows"].ap()[0:1, 2048:][0:1, 256:256 + VW]
                if tb == 0:
                    bqv = p.dma("sync", bqrow,
                                self.da("bq_my_row")[0:1, hg * VW:(hg + 1) * VW], ld)
                    p.wait("tensor", ld, bqv)
                for tsub in range(TSUBS):
                    psq = self.psB(1024, 1024 + VW)
                    p.wait("tensor", se)
                    pev = None
                    for kc in range(c.DC):
                        lhs = self.cslot(0, kc, TBW)[:, tsub * P:(tsub + 1) * P]
                        rhs = wqmv[:, :, kc, :]
                        pev = p.run("tensor", lambda e, lhs=lhs, rhs=rhs,
                                    psq=psq, kc=kc: e.matmul(
                                        psq, lhs, rhs, start=(kc == 0),
                                        stop=False), pe)
                    pev = p.run("tensor", lambda e, psq=psq, bqrow=bqrow:
                                e.matmul(psq, orow, bqrow, start=False,
                                         stop=True), pe)
                    stg = self.sb["T1"].ap()[:, :VW]
                    p.wait("scalar", pe, pev)
                    p.wait("scalar", st)
                    sev = p.run("scalar", lambda e, stg=stg, psq=psq:
                                e.copy(stg, psq), se)
                    rowbase = hg * c.T + tb * TBW + tsub * P
                    p.wait("gpsimd", se, sev)
                    p.dma("gpsimd",
                          self.da("qtm_sp")[rowbase:rowbase + P, :], stg, st)
            if self.debug_stop == "l1a1":
                return
            smx = self.sb["T0"].ap()[:, :c.LC]
            for hh in range(GH):
                qa = self.sb["bigA0"].ap()
                ka = self.sb["bigA1"].ap()
                base = (hh % 2) * c.DK
                for lt in range(c.LC):
                    nb = min(4, c.NKT)
                    pev = None
                    mxv = self.sb["sm"].ap()[:, 48:49]
                    for half in range((c.NKT + nb - 1) // nb):
                        psr = (self.ps["A"] if half == 0 else self.ps["B"]).ap()
                        p.wait("tensor", se)
                        p.wait("tensor", ve)
                        for g in range(nb):
                            kn = half * nb + g
                            if kn >= c.NKT:
                                continue
                            lhs = qa[base:base + c.DK, lt * P:(lt + 1) * P]
                            rhs = ka[base:base + c.DK, kn * c.KN:(kn + 1) * c.KN]
                            psx = psr[0:P, g * c.KN:(g + 1) * c.KN]
                            pev = p.run("tensor", lambda e, lhs=lhs, rhs=rhs,
                                        psx=psx: e.matmul(psx, lhs, rhs,
                                                          start=True, stop=True), pe)
                        wid = min(nb * c.KN, c.T - half * nb * c.KN)
                        p.wait("vector", pe, pev)

                        def red(e, psr=psr, wid=wid, half=half, mxv=mxv,
                                smx=smx, lt=lt):
                            e.tensor_reduce(self.sb["sm"].ap()[:, 49:50],
                                            psr[0:P, 0:wid], AX.X, ALU.max)
                            e.drain()
                            if half == 0:
                                return e.tensor_copy(mxv, self.sb["sm"].ap()[:, 49:50])
                            e.tensor_max(mxv, mxv, self.sb["sm"].ap()[:, 49:50])
                            e.drain()
                            return e.tensor_copy(smx[:, lt:lt + 1], mxv)
                        vev = p.run("vector", red, ve)
                        if c.NKT <= nb:
                            pass
                    if c.NKT <= nb:
                        def cpmx(e, smx=smx, lt=lt, mxv=mxv):
                            e.drain()
                            return e.tensor_copy(smx[:, lt:lt + 1], mxv)
                        vev = p.run("vector", cpmx, ve)
                pst = self.psB(1024, 1024 + P, rows=c.LC)
                p.wait("tensor", ve, vev)
                pev = p.run("tensor", lambda e, pst=pst, smx=smx, eye=eye:
                            e.transpose(pst, smx, eye), pe)
                t1 = self.sb["T1"].ap()[:c.LC, :P]
                p.wait("vector", pe, pev)
                p.wait("vector", st)
                vev = p.run("vector", lambda e, t1=t1, pst=pst:
                            e.tensor_copy(t1, pst), ve)
                p.wait("gpsimd", ve, vev)
                fv = p.dma("gpsimd", smax[hh:hh + 1, :], t1, st)
            if self.debug_stop == "l1a2":
                return
            g0 = hg * GH
            p.wait("vector", p.sems["st"])

            def sel8(e, g0=g0):
                e.max(self.sb["selv"].ap()[0:GH, :], smax[0:GH, :])
                e.drain()
                return e.max_index(self.sb["seli"].ap()[0:GH, :],
                                   self.sb["selv"].ap()[0:GH, :],
                                   smax[0:GH, :])
            vev = p.run("vector", sel8, ve)
            p.wait("gpsimd", ve, vev)
            iv = p.dma("gpsimd", self.sb["selrow"].ap()[0:1, g0 * 8:(g0 + GH) * 8],
                       self.sb["seli"].ap()[0:GH, :], st)
            fvv = p.dma("gpsimd", self.sb["rows"].ap()[0:1, 2048:][0:1, 512 + g0 * 8:512 + (g0 + GH) * 8],
                        self.sb["selv"].ap()[0:GH, :], st)
            s32g = self.sb["s32g0"].ap()
            def cps32(e, s32g=s32g):
                e.drain()
                return e.tensor_copy(s32g[:GH, :], self.sb["seli"].ap()[0:GH, :])
            vev = p.run("vector", cps32, ve)
            fparr = self.sb["rows"].ap()[0:1, 2048:][0:1, 768:768 + 128]
            vev = p.run("vector", lambda e, fparr=fparr: e.memset(fparr, 0.0), ve)
            p.wait("gpsimd", ve, vev)
            fav = None
            for hh in range(GH):
                fav = p.dma("gpsimd", fparr[0:1, hh * 32:hh * 32 + 8],
                            s32g[hh:hh + 1, :], st)
            psqi = self.psB(1544, 1545, rows=P)
            p.wait("tensor", st, fav)
            pev = p.run("tensor", lambda e, psqi=psqi, fparr=fparr:
                        e.transpose(psqi, fparr, self.sb["consts"].ap()[0:1, 0:1]), pe)
            gidxc = self.sb["gidxc"].ap()
            p.wait("vector", pe, pev)
            vev = p.run("vector", lambda e, gidxc=gidxc, psqi=psqi, hg=hg:
                        e.tensor_scalar(gidxc, psqi, float(hg * c.T), None,
                                        ALU.add), ve)
            p.wait("gpsimd", ve, vev)
            p.wait("gpsimd", st)
            from concourse.bass_core_rust import IndirectOffsetOnAxis as IOA
            qg = self.sb["qg"].ap()
            gqs = p.run("gpsimd", lambda e, qg=qg, gidxc=gidxc: e.indirect_dma_start(
                qg[:, :VW], None, self.da("qtm_sp")[:, :],
                IOA(gidxc, 0)), p.sems["st"], k=16)
            gqv = p.run("gpsimd", lambda e: e.drain(), p.sem("gp"))
            self._gqs = gqs
            if self.debug_stop == "l1a3":
                return
            for hh in range(GH):
                hr = hg * GH + hh
                qa = self.sb["bigA0"].ap()
                ka = self.sb["bigA1"].ap()
                base = (hh % 2) * c.DK
                p.wait("vector", p.sem("gp"), gqv)
                p.wait("vector", p.sems["st"], self._gqs)
                tq = self.sb["T1"].ap()[0:32, 64:64 + c.DK]

                def gq(e, hh=hh, base=base, qsel=qsel, tq=tq):
                    src = self.sb["qg"].ap()[hh * 32:hh * 32 + 32,
                                             hh * c.DK:(hh + 1) * c.DK]
                    e.transpose(tq, src)
                    e.drain()
                    e.tensor_copy(qsel[base:base + 32, :], tq[0:32, 0:8])
                    return e.tensor_copy(qsel[base + 32:base + 64, :],
                                         tq[0:32, 32:40])
                vev = p.run("vector", gq, ve)
                nm = self.sb["rows"].ap()[0:1, 2048:][0:1, 640:648]
                p.wait("vector", st, fvv)
                vev = p.run("vector", lambda e, nm=nm, hr=hr: e.tensor_scalar(
                    nm, self.sb["rows"].ap()[0:1, 2048:][0:1, 512 + hr * 8:512 + hr * 8 + 8],
                    -1.0, None, ALU.mult), ve)
                p.wait("tensor", ve, vev)
                for kt in range(c.LC):
                    bank = kt % 2
                    psx = self.psA(0, 8, rows=P) if bank == 0 \
                        else self.psA(512, 520, rows=P)
                    p.wait("tensor", se)
                    lhs = ka[base:base + c.DK, kt * P:(kt + 1) * P]
                    p.run("tensor", lambda e, lhs=lhs, psx=psx, qsel=qsel,
                          base=base: e.matmul(psx, lhs,
                                              qsel[base:base + c.DK, :],
                                              start=True, stop=False), pe)
                    pev = p.run("tensor", lambda e, psx=psx, nm=nm: e.matmul(
                        psx, orow, nm, start=False, stop=True), pe)
                    dst = expb[:, kt * 8:(kt + 1) * 8]
                    p.wait("scalar", pe, pev)
                    sev = p.run("scalar", lambda e, dst=dst, psx=psx:
                                e.activation(dst, psx, ACT.Exp, scale=0.125), se)
                pss = self.psB(1536, 1537, rows=8)
                p.wait("tensor", se, sev)
                pev = None
                for kt in range(c.LC):
                    src = expb[:, kt * 8:(kt + 1) * 8]
                    pev = p.run("tensor", lambda e, src=src, pss=pss, kt=kt:
                                e.matmul(pss, src, ones, start=(kt == 0),
                                         stop=(kt == c.LC - 1)), pe)
                s81 = self.sb["T1"].ap()[0:8, 0:1]
                p.wait("vector", pe, pev)
                vev = p.run("vector", lambda e, s81=s81, pss=pss:
                            e.tensor_copy(s81, pss), ve)
                pst8 = self.psB(1544, 1552, rows=1)
                p.wait("tensor", ve, vev)
                pev = p.run("tensor", lambda e, pst8=pst8, s81=s81, eye=eye:
                            e.transpose(pst8, s81, eye[0:8, 0:8]), pe)
                r8 = self.sb["rows"].ap()[0:1, 2048:][0:1, 656:664]
                p.wait("vector", pe, pev)

                def rc(e, r8=r8, pst8=pst8):
                    e.tensor_copy(r8, pst8)
                    e.drain()
                    return e.reciprocal(r8, r8)
                vev = p.run("vector", rc, ve)
                psb8 = self.psB(1552, 1560)
                p.wait("tensor", ve, vev)
                pev = p.run("tensor", lambda e, psb8=psb8, r8=r8: e.matmul(
                    psb8, orow, r8, start=True, stop=True), pe)
                rs8 = self.sb["T1"].ap()[:, 8:16]
                p.wait("vector", pe, pev)
                vev = p.run("vector", lambda e, rs8=rs8, psb8=psb8:
                            e.tensor_copy(rs8, psb8), ve)
                vh = self.sb["C1"].ap()[:, c.TOK:c.TOK + c.LC * c.DK]
                vsrc = self.da("v_sp").rearrange(
                    "(kt p) m -> p kt m", p=P)[:, :, hg * VW + hh * c.DK:
                                               hg * VW + (hh + 1) * c.DK]
                p.wait("sync", pe)
                p.wait("sync", st)
                vhl = p.dma("sync", vh.rearrange("p (kt d) -> p kt d", d=c.DK),
                            vsrc, ld)
                psc = self.psB(1560, 1568, rows=c.DK)
                pev = None
                p.wait("tensor", se)
                p.wait("tensor", ld, vhl)
                for kt in range(c.LC):
                    vsl = vh[:, kt * c.DK:(kt + 1) * c.DK]
                    src = expb[:, kt * 8:(kt + 1) * 8]
                    pev = p.run("tensor", lambda e, vsl=vsl, src=src, psc=psc,
                                kt=kt: e.matmul(psc, vsl, src, start=(kt == 0),
                                                stop=(kt == c.LC - 1)), pe)
                dstc = ctxb[:, hr * 8:(hr + 1) * 8]
                p.wait("vector", pe, pev)
                vev = p.run("vector", lambda e, dstc=dstc, psc=psc, rs8=rs8:
                            e.tensor_tensor(dstc, psc, rs8[0:c.DK, :], ALU.mult), ve)
        if self.debug_stop == "l1a4":
            return
        lvs = []
        p.wait("sync", ve)
        for kc in range(c.DC):
            lvs.append(p.dma("sync", self.sb[f"T{2 + kc % 8}"].ap()[:, 0:1],
                             self.da("xn1_sp")[kc * P:(kc + 1) * P,
                                               c.TOK - 1:c.TOK], ld, slow=True))
        lw = None
        for kc in range(c.DC):
            lw = p.dma("sync", wbuf[0][:, kc * c.NAV:(kc + 1) * c.NAV],
                       self.da("Wp1")[kc * P:(kc + 1) * P, :], ld)
        lb1 = p.dma("sync", self.sb["T1"].ap()[:, 16:17], self.da("bp1")[:, :], ld)
        psh = self.psB(0, 1, rows=P)
        p.wait("tensor", ld, lw)
        p.wait("tensor", ld, lvs[-1])
        p.wait("tensor", se)
        p.wait("tensor", ve)
        pev = None
        for kc in range(c.DC):
            lhs = wbuf[0][:, kc * c.NAV:(kc + 1) * c.NAV]
            rhs = self.sb[f"T{2 + kc % 8}"].ap()[:, 0:1]
            pev = p.run("tensor", lambda e, lhs=lhs, rhs=rhs, psh=psh, kc=kc:
                        e.matmul(psh, lhs, rhs, start=(kc == 0),
                                 stop=(kc == c.DC - 1)), pe)
        h1 = self.sb["T1"].ap()[:, 17:18]
        p.wait("scalar", pe, pev)
        p.wait("scalar", ld, lb1)
        sev = p.run("scalar", lambda e, h1=h1, psh=psh: e.activation(
            h1, psh, ACT.Gelu, bias=self.sb["T1"].ap()[:, 16:17]), se)
        p.wait("sync", pe)
        lw2 = p.dma("sync", self.sb["T3"].ap()[:c.NAV, 0:c.E], self.da("Wp2")[:, :], ld)
        lb2 = p.dma("sync", self.sb["T3"].ap()[:c.E, 8:9], self.da("bp2")[:, :], ld)
        psl = self.psB(512, 513, rows=c.E)
        p.wait("tensor", se, sev)
        p.wait("tensor", ld, lw2)
        pev = p.run("tensor", lambda e, psl=psl, h1=h1: e.matmul(
            psl, self.sb["T3"].ap()[:c.NAV, 0:c.E], h1, start=True,
            stop=True), pe)
        lg8 = self.sb["T3"].ap()[:c.E, 9:10]
        p.wait("scalar", pe, pev)
        p.wait("scalar", ld, lb2)
        sev = p.run("scalar", lambda e, lg8=lg8, psl=psl: e.add(
            lg8, psl, self.sb["T3"].ap()[:c.E, 8:9]), se)
        pstl = self.psB(1024, 1024 + c.E, rows=1)
        p.wait("tensor", se, sev)
        pev = p.run("tensor", lambda e, pstl=pstl, lg8=lg8, eye=eye:
                    e.transpose(pstl, lg8, eye[0:c.E, 0:c.E]), pe)
        lrow = self.sb["rows"].ap()[0:1, 2048:][0:1, 672:672 + c.E]
        vrow = self.sb["rows"].ap()[0:1, 2048:][0:1, 688:696]
        irow = self.sb["selrow"].ap()[0:1, 4 * c.HPC * 8 - 8:4 * c.HPC * 8]
        p.wait("vector", pe, pev)

        def amax(e, lrow=lrow, vrow=vrow, irow=irow, pstl=pstl):
            e.tensor_copy(lrow, pstl)
            e.drain()
            e.max(vrow, lrow)
            e.drain()
            return e.max_index(irow.bitcast(U32), vrow, lrow)
        vev = p.run("vector", amax, ve)
        selslot = self.sb["rows"].ap()[0:1, 2048:][0:1, 704:705]
        def cpsel(e, selslot=selslot, irow=irow):
            e.drain()
            return e.tensor_copy(selslot, irow[0:1, 0:1])
        vev = p.run("vector", cpsel, ve)
        if self.debug_stop == "l1a5":
            return
        nctx_ctx = c.DK * c.HPC * 8
        nctx = self._ctx_pack_len()
        zrow = self.sb["rows"].ap()[64:65, 0:512]
        vev = p.run("vector", lambda e, zrow=zrow: e.memset(zrow, 0.0), ve)
        idxf = self.sb["rows"].ap()[0:1, 2048:][0:1, 896:896 + c.HPC * 8]
        p.wait("vector", st)
        vev = p.run("vector", lambda e, idxf=idxf: e.tensor_copy(
            idxf, self.sb["selrow"].ap()[0:1, :c.HPC * 8]), ve)
        p.wait("gpsimd", ve, vev)
        p.wait("gpsimd", se)
        cin = self.da("cc_ctx_in")
        pv1 = p.dma("gpsimd", cin[0:1, 0:nctx_ctx], ctxb, st)
        pv2 = p.dma("gpsimd", cin[0:1, nctx_ctx:nctx_ctx + c.HPC * 8], idxf, st)
        pv3 = p.dma("gpsimd", cin[0:1, nctx_ctx + c.HPC * 8:
                                 nctx_ctx + c.HPC * 8 + 1], selslot, st)
        padlen = nctx - (nctx_ctx + c.HPC * 8 + 1)
        v = pv3
        off = nctx_ctx + c.HPC * 8 + 1
        while padlen > 0:
            wdt = min(512, padlen)
            v = p.dma("gpsimd", cin[0:1, off:off + wdt], zrow[0:1, :wdt], st)
            off += wdt
            padlen -= wdt
        p.wait("gpsimd", st, v)
        self._cc7 = p.run("gpsimd", lambda e: e.collective_compute(
            "AllGather", ALU.bypass, replica_groups=[list(range(8))],
            ins=[self.dram["cc_ctx_in"].ap().opt()],
            outs=[self.dram["cc_ctx_out"].ap().opt()]), cc)
        if self.debug_stop in ("l1a6", "dumppack"):
            return
        nctx = self._ctx_pack_len()
        gath = self.sb["rows"].ap()[32:40, :]
        cpick = self.sb["rows"].ap()[32:40, self.RW - 8:self.RW - 4]
        p.wait("sync", cc, self._cc7)
        lv = p.dma("sync", gath[:, :nctx], self.da("cc_ctx_out")[:, :], ld)
        p.wait("tensor", ld, lv)
        p.wait("tensor", ve)
        p.wait("tensor", se)
        packs = []
        stgrow = self.sb["rows"].ap()[64:65, 512:1024]
        for which, pickcol in ((0, 2), (1, 3)):
            spn = f"pp{which}_sp"
            pev = None
            for j in range(0, nctx, 512):
                w = min(512, nctx - j)
                psx = self.psA(j % 2048, j % 2048 + w, rows=1)
                pev = p.run("tensor", lambda e, psx=psx, j=j, w=w,
                            pickcol=pickcol: e.matmul(
                                psx, cpick[:, pickcol:pickcol + 1],
                                gath[:, j:j + w], start=True, stop=True), pe)
                p.wait("vector", pe, pev)
                p.wait("vector", st)
                vev = p.run("vector", lambda e, psx=psx, w=w, stgrow=stgrow:
                            e.tensor_copy(stgrow[0:1, :w], psx), ve)
                p.wait("gpsimd", ve, vev)
                p.dma("gpsimd", self.da(spn)[0:1, j:j + w], stgrow[0:1, :w], st)
                p.wait("tensor", ve, vev)
            packs.append(spn)
        ctxs = self.sb["ctxs"].ap()
        NIT = 2 * c.HPC * 8
        fr = self.sb["rows"].ap()[0:1, 2048:]
        ltrow_f = fr[0:1, 1024:1024 + NIT]
        nctx_ctx2 = c.DK * c.HPC * 8
        for sidx in range(2):
            spn = packs[sidx]
            p.wait("gpsimd", ve)
            p.wait("gpsimd", st)
            cvv = p.dma("gpsimd", ctxs[:, sidx * c.HPC * 8:(sidx + 1) * c.HPC * 8],
                        self.da(spn)[0:1, 0:nctx_ctx2], st)
            idxrow = self.sb[f"idxr{sidx}"].ap()[0:1, :c.HPC * 8]
            p.wait("sync", st)
            iv2 = p.dma("sync", idxrow,
                        self.da(spn)[0:1, nctx_ctx2:nctx_ctx2 + c.HPC * 8], ld)
            p.wait("vector", ld, iv2)
            ltf = ltrow_f[0:1, sidx * c.HPC * 8:(sidx + 1) * c.HPC * 8]

            def mklt(e, idxrow=idxrow, ltf=ltf):
                toff = self.sb["crole"].ap()[0:1, 1:2]
                return e.scalar_tensor_tensor(
                    ltf, idxrow, 0.0, toff.to_broadcast([1, c.HPC * 8]),
                    ALU.add, ALU.subtract)
            vev = p.run("vector", mklt, ve)
            if c.U < 8:
                def drop(e, ltf=ltf):
                    r = None
                    for h in range(c.HPC):
                        r = e.memset(ltf[0:1, h * 8 + c.U:h * 8 + 8], -1.0)
                    return r
                vev = p.run("vector", drop, ve)
        psltc = self.psB(1024, 1025, rows=NIT)
        eye = self.sb["consts"].ap()
        p.wait("tensor", ve, vev)
        pev = p.run("tensor", lambda e, psltc=psltc, ltrow_f=ltrow_f, eye=eye:
                    e.transpose(psltc, ltrow_f, eye[0:1, 0:1]), pe)
        ltcol = self.sb["sm"].ap()[:NIT, 54:55]
        p.wait("vector", pe, pev)
        p.wait("vector", st)
        vev = p.run("vector", lambda e, ltcol=ltcol, psltc=psltc:
                    e.tensor_copy(ltcol, psltc), ve)

        def pcols(ci):
            return self.sb[f"T{2 + ci}"].ap()[:, :NIT]

        def pcolsT(ci):
            return self.sb["rows"].ap()[0:NIT, 5120 + ci * P:5120 + (ci + 1) * P]
        def memz(e):
            r = None
            for ci in range(c.DC):
                r = e.memset(pcols(ci), 0.0)
            return r
        vev2 = p.run("vector", memz, ve)

        def asm(e):
            r = None
            for sidx in range(2):
                for h in range(c.HPC):
                    ghead = sidx * c.HPC + h
                    drow = ghead * c.DK
                    ci, bs = drow // P, drow % P
                    cols = slice((sidx * c.HPC + h) * 8, (sidx * c.HPC + h) * 8 + 8)
                    r = e.tensor_copy(pcols(ci)[bs:bs + c.DK, cols],
                                      ctxs[:, cols])
            return r
        vev = p.run("vector", asm, ve)
        p.wait("tensor", ve, vev)
        for ci in range(c.DC):
            pst = self.psB(0, P, rows=NIT)
            pev = p.run("tensor", lambda e, pst=pst, ci=ci, eye=eye:
                        e.transpose(pst, pcols(ci), eye), pe)
            p.wait("vector", pe, pev)
            vev = p.run("vector", lambda e, pst=pst, ci=ci:
                        e.tensor_copy(pcolsT(ci), pst), ve)
            p.wait("tensor", ve, vev)
        iota = self.da("iotarow")
        SLW = min(512, c.TOK)
        iosb = self.sb["T0"].ap()[0:1, :SLW]
        for sl in range(c.TOK // SLW):
            ls, lw = sl * SLW, SLW
            lio = p.dma("sync", iosb, iota[0:1, ls:ls + lw], ld)
            psio = self.psB(512, 512 + lw)
            p.wait("tensor", ld, lio)
            p.wait("tensor", ve)
            pev = p.run("tensor", lambda e, psio=psio, iosb=iosb: e.matmul(
                psio, self.sb["conesrow"].ap(), iosb, start=True, stop=True), pe)
            oh = self.sb["qg"].ap()[:, :lw]
            p.wait("vector", pe, pev)
            vev = p.run("vector", lambda e, oh=oh, psio=psio, ltcol=ltcol:
                        e.tensor_scalar(oh[:NIT, :], psio[:NIT, :], ltcol, None,
                                        ALU.is_equal), ve)
            p.wait("tensor", ve, vev)
            for ci in range(c.DC):
                psp = self.psA(0, lw) if ci % 2 == 0 else self.psA(512, 512 + lw)
                p.wait("tensor", se)
                pev = p.run("tensor", lambda e, psp=psp, ci=ci, oh=oh: e.matmul(
                    psp, pcolsT(ci)[:NIT, :], oh[:NIT, :], start=True,
                    stop=True), pe)
                dst = self.pslot(ci)[:, ls:ls + lw]
                p.wait("scalar", pe, pev)
                sev = p.run("scalar", lambda e, dst=dst, psp=psp:
                            e.copy(dst, psp), se)
        p.wait("tensor", se, sev)
        wbufs = [self.sb["WB0"].ap(), self.sb["WB1"].ap()]
        p.wait("sync", ve, vev)
        for tt in range(c.NTT):
            lo, hi = tt * c.TT, (tt + 1) * c.TT
            wld = [None, None]
            p.wait("sync", pe)
            wld[0] = self.colslice_load("woT", 0, wbufs[0], c.DC, ld)
            if c.DC > 1:
                wld[1] = self.colslice_load("woT", 1, wbufs[1], c.DC, ld)
            for oc in range(c.DC):
                b = oc % 2
                ps = self.psA(0, c.TT) if b == 0 else self.psA(512, 512 + c.TT)
                p.wait("tensor", ld, wld[b])
                p.wait("tensor", ve)
                pev = None
                for kc in range(c.DC):
                    lhs = wbufs[b][:, kc * P:(kc + 1) * P]
                    rhs = self.pslot(kc)[:, lo:hi]
                    pev = p.run("tensor", lambda e, lhs=lhs, rhs=rhs, ps=ps,
                                kc=kc: e.matmul(ps, lhs, rhs, start=(kc == 0),
                                                stop=(kc == c.DC - 1)), pe)
                if oc + 2 < c.DC:
                    p.wait("sync", pe, pev)
                    wld[b] = self.colslice_load("woT", oc + 2, wbufs[b], c.DC, ld)
                at = self.sb[f"acc{oc}"].ap()[:, lo:hi]
                bo = self.da("bo_t")[:, oc:oc + 1]
                bos = self.sb["sm"].ap()[:, 50:51]
                p.wait("sync", ve)
                lb = p.dma("sync", bos, bo, ld, slow=True)
                p.wait("vector", pe, pev)
                p.wait("vector", ld, lb)
                vev = p.run("vector", lambda e, at=at, ps=ps, bos=bos:
                            e.scalar_tensor_tensor(at, ps, bos, at, ALU.add,
                                                   ALU.add), ve)

    def ph_l1_ffn(self):
        c, p = self.cfg, self.p
        ld, pe, ve, se, st, cc = (p.sem(n) for n in
                                  ("ld", "pe", "ve", "se", "st", "cc"))
        C0 = self.sb["C0"].ap()
        C1 = self.sb["C1"].ap()
        nctx_ctx = c.DK * c.HPC * 8
        selr = self.sb["selr1"].ap()[0:1, 0:1]
        p.wait("sync", st)
        lv = p.dma("sync", selr,
                   self.da("pp1_sp")[0:1, nctx_ctx + c.HPC * 8:
                                     nctx_ctx + c.HPC * 8 + 1], ld)
        p.wait("tensor", ld, lv)
        if DEBUG_FORCE_SEL0:
            p.wait("vector", ld, lv)
            p.run("vector", lambda e: e.memset(selr, 0.0), p.sems["ve"])
            p.wait("tensor", p.sems["ve"])
        from concourse.bass_core_rust import IndirectOffsetOnAxis as IOA
        orow = self.sb["conesrow"].ap()
        selps = self.psB(1544, 1545)
        pev = p.run("tensor", lambda e: e.matmul(selps, orow, selr,
                                                 start=True, stop=True), pe)
        selc = self.sb["sm"].ap()[:, 60:61]
        idxf = self.sb["sm"].ap()[:, 61:62]
        gidx = self.sb["gidxc"].ap()
        iop = self.sb["ciotaT"].ap()[:, 0:1]
        p.wait("vector", pe, pev)
        vev = p.run("vector", lambda e: e.tensor_copy(selc, selps), ve)

        W1f = self.da("W1").rearrange("e d f -> (e d) f")
        W2f = self.da("W2").rearrange("e f d -> (e f) d")
        b1f = self.da("b1_t").rearrange("e p f -> (e p) f")
        b2f = self.da("b2_t").rearrange("e p d -> (e p) d")
        wb0 = self.sb["WB0"].ap()
        b1s = self.sb["bffn"].ap()[:, :c.FC]
        b2s = self.sb["bffn2"].ap()[:, :]
        p.wait("gpsimd", pe)
        p.wait("gpsimd", se)
        p.wait("gpsimd", ld)

        def gather_rows(srcf, nr, kc, dst_sb):
            def mkidx(e, kc=kc, nr=nr):
                e.drain()
                e.tensor_scalar(idxf, selc, float(nr), float(kc * P),
                                ALU.mult, ALU.add)
                e.drain()
                e.tensor_tensor(idxf, idxf, iop, ALU.add)
                e.drain()
                return e.tensor_copy(gidx, idxf)
            mv = p.run("vector", mkidx, ve)
            p.wait("gpsimd", ve, mv)
            gsv = p.run("gpsimd", lambda e, dst_sb=dst_sb, srcf=srcf:
                        e.indirect_dma_start(dst_sb, None, srcf,
                                             IOA(gidx, 0)), st, k=16)
            gv = p.run("gpsimd", lambda e: e.drain(), p.sem("gp"))
            p.wait("vector", p.sem("gp"), gv)
            p.wait("vector", st, gsv)
            return (gv, gsv)

        vw1 = vw2 = None
        for (srcf, nrch, width, spname) in ((W1f, c.DC, c.F, "w1_sp"),
                                            (W2f, c.FC, c.D, "w2_sp")):
            for kc in range(nrch):
                gv, gsv = gather_rows(srcf, nrch * P, kc, wb0[:, :width])
                p.wait("sync", p.sem("gp"), gv)
                p.wait("sync", st, gsv)
                lv = p.dma("sync", self.da(spname)[kc * P:(kc + 1) * P, :],
                           wb0[:, :width], ld)
                p.wait("gpsimd", ld, lv)
            if spname == "w1_sp":
                vw1 = lv
            else:
                vw2 = lv
        v1, v1s = gather_rows(b1f, P, 0, b1s)
        v2, v2s = gather_rows(b2f, P, 0, b2s)
        if self.debug_stop == "l1f1":
            return
        wbufs = [self.sb["WB0"].ap(), self.sb["WB1"].ap()]
        arenas = [self.sb["C1"].ap(), self.sb["bigA0"].ap(),
                  self.sb["bigA1"].ap()]
        per = (2 * c.TOK) // c.TT

        def hslot(i):
            a, sidx = divmod(i, per)
            return arenas[a][:, sidx * c.TT:(sidx + 1) * c.TT]

        self._ffn_pehist = [None, None]
        for tt in range(c.NTT):
            if self.debug_stop == f"l1f2_{tt}":
                return
            lo, hi = tt * c.TT, (tt + 1) * c.TT
            p.wait("sync", pe)
            p.wait("sync", ve)
            xl = None
            for i in range(c.DC):
                xl = p.dma("sync", C0[:, i * c.TT:(i + 1) * c.TT],
                           self.da("xn1_sp")[i * P:(i + 1) * P, lo:hi], ld)
            for oc in range(c.FC):
                b = oc % 2
                if tt == 0 and oc < 2:
                    p.wait("sync", pe)
                    if oc == 0:
                        p.wait("sync", ld, vw1)
                elif self._ffn_pehist[b] is not None:
                    p.wait("sync", pe, self._ffn_pehist[b])
                wl = self.colslice_load("w1_sp", oc, wbufs[b], c.DC, ld)
                ps = self.psA(0, c.TT) if b == 0 else self.psA(512, 512 + c.TT)
                p.wait("tensor", ld, wl)
                p.wait("tensor", ld, xl)
                p.wait("tensor", se)
                p.wait("tensor", ve)
                pev = None
                for kc in range(c.DC):
                    lhs = wbufs[b][:, kc * P:(kc + 1) * P]
                    rhs = C0[:, kc * c.TT:(kc + 1) * c.TT]
                    pev = p.run("tensor", lambda e, lhs=lhs, rhs=rhs, ps=ps,
                                kc=kc: e.matmul(ps, lhs, rhs, start=(kc == 0),
                                                stop=(kc == c.DC - 1)), pe)
                dst = hslot(oc)
                self._ffn_pehist[b] = pev
                p.wait("scalar", pe, pev)
                p.wait("scalar", p.sem("gp"), v1)
                if tt == 0 and oc == 0:
                    p.wait("scalar", st, v1s)
                sev = p.run("scalar", lambda e, dst=dst, ps=ps, oc=oc, b1s=b1s:
                            e.activation(dst, ps, ACT.Gelu,
                                         bias=b1s[:, oc:oc + 1]), se)
            rbase = c.FC

            def rslot(i):
                return hslot(rbase + i)
            for oc in range(c.DC):
                b = oc % 2
                if tt == 0 and oc < 2:
                    p.wait("sync", pe)
                    if oc == 0:
                        p.wait("sync", ld, vw2)
                elif self._ffn_pehist[b] is not None:
                    p.wait("sync", pe, self._ffn_pehist[b])
                wl = self.colslice_load("w2_sp", oc, wbufs[b], c.FC, ld)
                ps = self.psB(0, c.TT) if b == 0 else self.psB(512, 512 + c.TT)
                p.wait("tensor", ld, wl)
                p.wait("tensor", se)
                pev = None
                for kc in range(c.FC):
                    lhs = wbufs[b][:, kc * P:(kc + 1) * P]
                    pev = p.run("tensor", lambda e, lhs=lhs, kc=kc, ps=ps:
                                e.matmul(ps, lhs, hslot(kc), start=(kc == 0),
                                         stop=(kc == c.FC - 1)), pe)
                    dst = rslot(oc)
                self._ffn_pehist[b] = pev
                p.wait("scalar", pe, pev)
                p.wait("scalar", p.sem("gp"), v2)
                if tt == 0 and oc == 0:
                    p.wait("scalar", st, v2s)
                sev = p.run("scalar", lambda e, dst=dst, ps=ps, oc=oc, b2s=b2s:
                            e.add(dst, ps, b2s[:, oc:oc + 1]), se)
            wld = [None, None]
            p.wait("sync", pe)
            p.wait("sync", se)
            wld[0] = self.colslice_load("Wo1", 0, wbufs[0], c.DC, ld)
            if c.DC > 1:
                wld[1] = self.colslice_load("Wo1", 1, wbufs[1], c.DC, ld)
            for oc in range(c.DC):
                b = oc % 2
                ps = self.psA(0, c.TT) if b == 0 else self.psA(512, 512 + c.TT)
                p.wait("tensor", ld, wld[b])
                p.wait("tensor", ve)
                p.wait("tensor", se)
                pev = None
                for kc in range(c.DC):
                    lhs = wbufs[b][:, kc * P:(kc + 1) * P]
                    pev = p.run("tensor", lambda e, lhs=lhs, kc=kc, ps=ps:
                                e.matmul(ps, lhs, rslot(kc), start=(kc == 0),
                                         stop=(kc == c.DC - 1)), pe)
                if oc + 2 < c.DC:
                    p.wait("sync", pe, pev)
                    wld[b] = self.colslice_load("Wo1", oc + 2, wbufs[b], c.DC, ld)
                at = self.sb[f"acc{oc}"].ap()[:, lo:hi]
                self._ffn_pehist[b] = pev
                p.wait("vector", pe, pev)
                vev = p.run("vector", lambda e, at=at, ps=ps:
                            e.scalar_tensor_tensor(at, ps, 0.3, at, ALU.mult,
                                                   ALU.add), ve)
        for i in range(c.DC):
            bos = self.sb["sm"].ap()[:, 51:52]
            p.wait("sync", ve)
            lb = p.dma("sync", bos, self.da("bo1_t")[:, i:i + 1], ld, slow=True)
            at = self.sb[f"acc{i}"].ap()
            p.wait("vector", ld, lb)
            p.wait("vector", pe)

            def bo1add(e, at=at, bos=bos):
                e.tensor_scalar(self.sb["sm"].ap()[:, 52:53], bos, 0.3, None,
                                ALU.mult)
                e.drain()
                return e.tensor_scalar(at[:, :], at[:, :],
                                       self.sb["sm"].ap()[:, 52:53], None,
                                       ALU.add)
            vev = p.run("vector", bo1add, ve)

    def fin_or_dump(self):
        if getattr(self, "_dump", None):
            return self.finish_dump(self._dump)
        return self.finish()

    def finish_dump(self, spec):
        c, p = self.cfg, self.p
        st = p.sem("st")
        p.wait("gpsimd", p.sem("cc"))
        for eng, sname in (("gpsimd", "ve"), ("gpsimd", "se"), ("gpsimd", "pe")):
            p.wait(eng, p.sem(sname))
        outrow = 0
        v = None
        for part in spec.split(","):
            if "@" in part:
                name, rr = part.split("@")
                lo, hi = (int(t) for t in rr.split(":"))
            else:
                name, lo, hi = part, 0, None
            src = self.da(name)
            nrow, ncol = src.shape[0], src.shape[1]
            if hi is None:
                hi = nrow
            per = (ncol + c.TOK - 1) // c.TOK
            for r in range(lo, hi):
                for j in range(per):
                    w = min(c.TOK, ncol - j * c.TOK)
                    v = p.dma("gpsimd",
                              self.da("out_fm")[outrow:outrow + 1, :w],
                              src[r:r + 1, j * c.TOK:j * c.TOK + w], st)
                    outrow += 1
        assert outrow <= c.D, f"dump too big: {outrow}"
        p.wait("gpsimd", st, v)
        self.p.emit()
        return self.nc

    def finish(self):
        c, p = self.cfg, self.p
        st = p.sem("st")
        for eng, s in (("gpsimd", "ve"), ("gpsimd", "se"), ("gpsimd", "pe")):
            p.wait(eng, p.sem(s))
        v = None
        for i in range(c.DC):
            v = p.dma("gpsimd", self.da("out_fm")[i * P:(i + 1) * P, :],
                      self.sb[f"acc{i}"].ap()[:, :], st)
        p.wait("gpsimd", st, v)
        self.p.emit()
        return self.nc



def host_inmaps(cfg: Cfg, inputs: dict) -> list:
    c = cfg
    f32 = lambda a: np.ascontiguousarray(a, dtype=np.float32)
    tr = lambda a: f32(np.asarray(a).T)
    nat = lambda a: f32(np.asarray(a))

    shared = {
        "wqkvT": tr(inputs["L0_attn_w_qkv"]),
        "wswT": tr(inputs["L0_attn_w_swiglu"]),
        "woutT": tr(inputs["L0_attn_w_out"]),
        "wupT": tr(inputs["L0_ffn_w_up"]),
        "wgateT": tr(inputs["L0_ffn_w_gate"]),
        "wdownT": tr(inputs["L0_ffn_w_down"]),
        "wup_n": nat(inputs["L0_ffn_w_up"]),
        "wgate_n": nat(inputs["L0_ffn_w_gate"]),
        "wdown_n": nat(inputs["L0_ffn_w_down"]),
        "woT": tr(inputs["L1_attn_wo"]),
        "bo_t": f32(np.asarray(inputs["L1_attn_bo"]).reshape(c.DC, P).T),
        "Wp1": nat(inputs["L1_ffn_Wp1"]),
        "bp1": f32(np.asarray(inputs["L1_ffn_bp1"]).reshape(c.NAV, 1)),
        "Wp2": nat(inputs["L1_ffn_Wp2"]),
        "bp2": f32(np.asarray(inputs["L1_ffn_bp2"]).reshape(c.E, 1)),
        "W1": nat(inputs["L1_ffn_W1"]),
        "b1_t": f32(np.asarray(inputs["L1_ffn_b1"]).reshape(c.E, c.FC, P)
                    .transpose(0, 2, 1)),
        "W2": nat(inputs["L1_ffn_W2"]),
        "b2_t": f32(np.asarray(inputs["L1_ffn_b2"]).reshape(c.E, c.DC, P)
                    .transpose(0, 2, 1)),
        "Wo1": nat(inputs["L1_ffn_Wo"]),
        "bo1_t": f32(np.asarray(inputs["L1_ffn_bo"]).reshape(c.DC, P).T),
        "ones": np.ones((P, 1), np.float32),
        "onesrow": np.ones((1, P), np.float32),
        "eye": np.eye(P, dtype=np.float32),
        "iotaT": f32(np.arange(c.T, dtype=np.float32).reshape(c.LC, P).T),
        "iotarow": f32(np.arange(c.TOK, dtype=np.float32).reshape(1, c.TOK)),
    }

    x = np.asarray(inputs["L0_attn_x"])
    wqT = tr(inputs["L1_attn_wq"])
    wkT = tr(inputs["L1_attn_wk"])
    wvT = tr(inputs["L1_attn_wv"])
    bq = np.asarray(inputs["L1_attn_bq"], dtype=np.float32)
    bk = np.asarray(inputs["L1_attn_bk"], dtype=np.float32)
    bv = np.asarray(inputs["L1_attn_bv"], dtype=np.float32)
    MH = c.HPC * c.DK
    maps = []
    for core in range(8):
        b, h = core // 2, core % 2
        m = dict(shared)
        m["x_fm"] = f32(x[b, h * c.TOK:(h + 1) * c.TOK, :].T)
        hsl = slice(h * MH, (h + 1) * MH)
        m["wqT_my"] = f32(wqT[:, hsl])
        m["wkT_my"] = f32(wkT[:, hsl])
        m["wvT_my"] = f32(wvT[:, hsl])
        m["bq_my_t"] = f32(bq[hsl].reshape(MH // P, P).T)
        m["bk_my_t"] = f32(bk[hsl].reshape(MH // P, P).T)
        m["bv_my_row"] = f32(bv[hsl].reshape(1, MH))
        m["bq_my_row"] = f32(bq[hsl].reshape(1, MH))
        pc = np.zeros((8, 1), np.float32)
        if h == 1:
            pc[2 * b, 0] = 1.0
        psel = np.zeros((8, 1), np.float32)
        psel[2 * b + 1, 0] = 1.0
        pe_ = np.zeros((8, 1), np.float32)
        pe_[2 * b, 0] = 1.0
        po_ = np.zeros((8, 1), np.float32)
        po_[2 * b + 1, 0] = 1.0
        m["pick_carry"] = pc
        m["pick_sel"] = psel
        m["pick_even"] = pe_
        m["pick_odd"] = po_
        m["role"] = np.array([[float(h), float(h * c.TOK), float(core),
                               float(core ^ 1), 0, 0, 0, 0]], np.float32)
        maps.append(m)
    return maps


def host_gather(cfg: Cfg, results) -> np.ndarray:
    c = cfg
    out = np.empty((4, c.T, c.D), np.float32)
    for core in range(8):
        b, h = core // 2, core % 2
        out[b, h * c.TOK:(h + 1) * c.TOK, :] = results[core]["out_fm"].T
    return out


def build_nc(cfg: Cfg = CFG, debug_stop=None):
    m1 = MK(cfg, debug_stop=debug_stop)
    m1.build()
    m2 = MK(cfg, debug_stop=debug_stop, waited=m1.p.rec_waited)
    return m2.build()


def kernel(**inputs):
    cfg = CFG
    nc = build_nc(cfg)
    in_maps = host_inmaps(cfg, inputs)
    res = run_bass_kernel_spmd(nc, in_maps, core_ids=list(range(8)))
    return host_gather(cfg, res.results)
